# version 13
# baseline (speedup 1.0000x reference)
import math
import numpy as np
import ml_dtypes
from contextlib import ExitStack

import concourse.bass as bass
import concourse.mybir as mybir
from concourse.bass_utils import run_bass_kernel_spmd

F32 = mybir.dt.float32
BF16 = mybir.dt.bfloat16
ALU = mybir.AluOpType
AF = mybir.ActivationFunctionType
AX = mybir.AxisListType

NCORES = 8
D = 1024
T = 2048
KC = 8
I3 = 3072
IT = 24
NB = 4
BL = 512
HEADS = 16
DH = 64
M = 256
NTT = 16
MAGIC = 12582912.0
QMAX = 32767.0
EPS_LN = 1e-5
KEPS = 1e-4
NORM = float(DH) ** -0.25

DEBUG_STAGE = 0
DEBUG_DUMPS = False
F4_BARRIER = False
F4_SERIAL_W = False


class P:
    ENGS = ("t", "v", "a", "g", "s")

    def __init__(self, nc):
        self.nc = nc
        self.steps = {e: [] for e in self.ENGS}
        self.has_inc = {e: -1 for e in self.ENGS}
        self.sems = {}
        self.cnt = {}
        self.bar_n = 0

    def sem(self, name):
        h = self.nc.alloc_semaphore(name)
        self.sems[name] = h
        self.cnt[name] = 0
        return h

    def _emit(self, e, fn):
        self.steps[e].append(fn)

    def t(self, fn): self._emit("t", fn)
    def v(self, fn): self._emit("v", fn)
    def a(self, fn): self._emit("a", fn)
    def g(self, fn): self._emit("g", fn)
    def s(self, fn): self._emit("s", fn)

    def inc(self, e, fn, sem, n=1):
        h = self.sems[sem]
        self._emit(e, lambda eng: fn(eng).then_inc(h, n))
        self.has_inc[e] = len(self.steps[e])
        self.cnt[sem] += n
        return self.cnt[sem]

    def mark(self, e, sem):
        if e == "v":
            jk = self.junk_ap
            self._emit("v", lambda eng: eng.tensor_scalar_add(jk, jk, 1.0))
        h = self.sems[sem]
        assert self.steps[e], "mark with no prior instruction"
        if self.has_inc[e] == len(self.steps[e]):
            self._emit(e, lambda eng: eng.nop())
        last = self.steps[e][-1]
        self.steps[e][-1] = (lambda eng, f=last, h=h: f(eng).then_inc(h, 1))
        self.has_inc[e] = len(self.steps[e])
        self.cnt[sem] += 1
        return self.cnt[sem]

    def wait(self, e, sem, val=None):
        h = self.sems[sem]
        v = self.cnt[sem] if val is None else val
        self._emit(e, lambda eng: eng.wait_ge(h, v))

    def dma(self, e, sem, dst, src):
        h = self.sems[sem]
        self._emit(e, lambda eng: eng.dma_start(dst, src).then_inc(h, 16))
        self.cnt[sem] += 16
        return self.cnt[sem]

    def barrier(self, engines="tvags"):
        self.bar_n += 1
        n = self.bar_n
        for e in engines:
            name = "bar_" + e
            if name not in self.sems:
                self.sem(name)
        for e in engines:
            h = self.sems["bar_" + e]
            self._emit(e, lambda eng, h=h: eng.drain().then_inc(h, 1))
            self.cnt["bar_" + e] += 1
        for e in engines:
            for o in engines:
                if o != e:
                    self._emit(e, lambda eng, h=self.sems["bar_" + o], n=n: eng.wait_ge(h, n))


def build_nc():
    nc = bass.Bass("TRN2", target_bir_lowering=False, debug=False)
    L = {}

    def D_(name, shape, dt=F32, kind="Internal", addr_space="Local"):
        L[name] = nc.dram_tensor(name, shape, dt, kind=kind, addr_space=addr_space).ap()
        return L[name]

    D_("x_in", [D, T], kind="ExternalInput")
    D_("att4_in", [4 * 128, D], kind="ExternalInput")
    D_("bo_in", [D, 1], kind="ExternalInput")
    D_("pjT_in", [DH, M], kind="ExternalInput")
    D_("wgc_in", [I3, 128], kind="ExternalInput")
    D_("wuc_in", [I3, 128], kind="ExternalInput")
    D_("wdc_in", [D, 384], kind="ExternalInput")
    D_("out", [D, T], kind="ExternalOutput")
    if DEBUG_DUMPS:
        D_("dbg_m", [128, T], kind="ExternalOutput")
        D_("dbg_r", [128, T], kind="ExternalOutput")
        D_("dbg_xp", [128, T], BF16, kind="ExternalOutput")
        D_("dbg_gate", [128, T], kind="ExternalOutput")
        D_("dbg_p", [128, T], kind="ExternalOutput")
        D_("dbg_q", [128, T], kind="ExternalOutput")
        D_("dbg_cols", [128, 64], kind="ExternalOutput")
        D_("dbg_wg", [128, I3], BF16, kind="ExternalOutput")
        D_("dbg_row", [1, 512], kind="ExternalOutput")
        D_("dbg_nd", [1, 2 * T], kind="ExternalOutput")
        D_("dbg_stab", [16, 128], BF16, kind="ExternalOutput")
        D_("dbg_eku", [128, M], BF16, kind="ExternalOutput")
        D_("dbg_ctx", [65, M], kind="ExternalOutput")
        D_("dbg_sv", [65, HEADS], kind="ExternalOutput")
        D_("dbg_eq", [128, T], BF16, kind="ExternalOutput")
        D_("dbg_inv", [1, T], kind="ExternalOutput")
        D_("dbg_att", [64, T], BF16, kind="ExternalOutput")
        D_("dbg_qp", [128, T], BF16, kind="ExternalOutput")
        D_("dbg_lnb", [128, T], BF16, kind="ExternalOutput")
        D_("dbg_wq", [128, D], BF16, kind="ExternalOutput")
        D_("dbg_gall", [IT * 128, T], kind="ExternalOutput")
        D_("dbg_uall", [IT * 128, T], kind="ExternalOutput")
        D_("dbg_acc", [128, 8], kind="ExternalOutput")
        D_("dbg_s2", [128, T], kind="ExternalOutput")
        D_("dbg_wgall", [NCORES * 128, I3], BF16, kind="ExternalOutput")

    D_("cc_in_att", [512, D], BF16)
    D_("cc_out_att", [NCORES * 512, D], BF16, addr_space="Shared")
    D_("cc_in_g", [128, I3], BF16)
    D_("cc_out_g", [NCORES * 128, I3], BF16, addr_space="Shared")
    D_("cc_in_u", [128, I3], BF16)
    D_("cc_out_u", [NCORES * 128, I3], BF16, addr_space="Shared")
    D_("cc_in_d", [3072, 128], BF16)
    D_("cc_out_d", [NCORES * 3072, 128], BF16, addr_space="Shared")
    D_("gate_d", [IT, 128, T])
    D_("up_d", [IT, 128, T])
    D_("p_d", [IT, 128, T])
    D_("q_d", [IT, 128, T])
    D_("eq_d", [HEADS, 2, 128, T], BF16)
    D_("eps_d", [HEADS, 2048])
    D_("cc_in_s", [1, 64])
    D_("cc_out_s", [1, 64])
    D_("cc_in_ctx", [HEADS * 65, M])
    D_("cc_out_ctx", [HEADS * 65, M])
    D_("cc_in_sv", [65, HEADS])
    D_("cc_out_sv", [65, HEADS])

    L["ident_f"] = nc.inline_tensor(np.eye(128, dtype=np.float32), name="ident_f").ap()
    L["ones_col_f"] = nc.inline_tensor(np.ones((128, 1), np.float32), name="ones_col_f").ap()
    L["ones_row_f"] = nc.inline_tensor(np.ones((1, 128), np.float32), name="ones_row_f").ap()
    negsel_np = np.zeros((16, 16 * 128), np.float32)
    for t_ in range(16):
        negsel_np[t_, t_ * 128:(t_ + 1) * 128] = -1.0
    L["negsel"] = nc.inline_tensor(negsel_np.astype(ml_dtypes.bfloat16), name="negsel").ap()
    CONSTS = [0.0, 1.0, 0.5, MAGIC, 65536.0, EPS_LN, -0.5, 1.0 / 1024.0, math.log(KEPS), -1e30]
    L["consts_in"] = nc.inline_tensor(np.tile(np.array([CONSTS], np.float32), (128, 1)), name="consts").ap()
    L["CONSTS"] = CONSTS

    st = ExitStack()
    with st:
        def sbt(name, shape, dt=F32):
            h = st.enter_context(nc.sbuf_tensor(name, shape, dt))
            L[name] = h
            return h

        wdt1 = sbt("wdt1", [128, 128], BF16)
        wdt2 = sbt("wdt2", [128, 128], BF16)
        x_t = [sbt(f"x{i}", [128, T]) for i in range(KC)]
        L["x_t"] = x_t
        stage = [sbt(f"stg{i}", [128, T]) for i in range(3)]
        L["stage"] = stage
        arena = sbt("arena", [128, 49152], BF16)
        rowh = [sbt("rowh0", [128, T]), sbt("rowh1", [128, 512])]
        L["rowh"] = rowh
        sbt("cols", [128, 64])
        sbt("colw", [128, 64])
        sbt("junk", [128, 8])
        L["redt"] = [sbt(f"red{i}", [128, 1]) for i in range(2)]
        sbt("acc", [128, 1])
        sbt("dbgt", [128, 8])
        sbt("constt", [128, len(CONSTS)])
        sbt("identt", [128, 128])
        sbt("identb", [128, 128], BF16)
        sbt("onescf", [128, 1])
        sbt("onescb", [128, 1], BF16)
        sbt("onesrf", [1, 128])
        sbt("onesr3", [128, 256])
        sbt("negselt", [16, 16 * 128], BF16)
        sbt("pjT", [128, M], BF16)
        sbt("stabT", [16, 128], BF16)
        sbt("stab_tm", [128, 16])
        sbt("stabb", [128, 16], BF16)
        sbt("kacc", [128, 1])
        sbt("ctxbf", [65, M], BF16)
        sbt("ctxe", [65, M])
        sbt("ctxb", [128, 2 * 65], BF16)
        sbt("svc", [65, HEADS + 1])
        L["wdtile"] = [sbt("wdt0", [128, 128], BF16), wdt1, wdt2]
        L["sblk"] = [sbt(f"sblk{i}", [128, 96]) for i in range(3)]
        sbt("bocol", [128, KC])

        for j, val in enumerate(CONSTS):
            nc.const_aps.register(F32, float(val), L["constt"][:, j:j + 1])

        L["Xp"] = lambda i: arena[:, 2048 * i: 2048 * (i + 1)]
        L["Wb"] = lambda k: arena[:, 16384 + 3072 * k: 16384 + 3072 * (k + 1)]
        L["Ob"] = lambda i: arena[:, 2048 * i: 2048 * (i + 1)]
        L["Ab"] = lambda i: arena[:, 16384 + 2048 * i: 16384 + 2048 * (i + 1)]
        L["row"] = lambda i, slot: rowh[i][slot:slot + 1, :]

        L["psf"] = [st.enter_context(nc.psum_tensor(f"ps{i}", [128, 512], F32)) for i in range(7)]
        L["psb"] = st.enter_context(nc.psum_tensor("psb", [128, 1024], BF16))

        L["dump"] = DEBUG_DUMPS and DEBUG_STAGE == 2
        p = P(nc)
        p.junk_ap = L["junk"][:, 7:8]
        for name in ("di", "dg", "T", "V", "A", "G", "CC"):
            p.sem(name)

        nc.all_engine_barrier()
        for h in list(p.sems.values()):
            nc.gpsimd.sem_clear(h)
        nc.all_engine_barrier()

        author(p, L)

        with nc.Block() as block:
            @block.tensor
            def _(e):
                for f in p.steps["t"]:
                    f(e)

            @block.vector
            def _(e):
                for f in p.steps["v"]:
                    f(e)

            @block.scalar
            def _(e):
                for f in p.steps["a"]:
                    f(e)

            @block.gpsimd
            def _(e):
                for f in p.steps["g"]:
                    f(e)

            @block.sync
            def _(e):
                for f in p.steps["s"]:
                    f(e)

        nc.all_engine_barrier()
        for h in list(p.sems.values()):
            nc.gpsimd.sem_clear(h)
        nc.all_engine_barrier()
    return nc


def author(p, L):
    x_t, stage = L["x_t"], L["stage"]
    row = L["row"]

    p.dma("s", "di", L["identt"][:], L["ident_f"])
    p.dma("s", "di", L["constt"][:], L["consts_in"])
    p.dma("s", "di", L["onesrf"][:], L["ones_row_f"])
    for b in (0, 32, 64):
        p.dma("s", "di", L["onesr3"][b:b + 1, 0:128], L["ones_row_f"])
        p.dma("s", "di", L["onesr3"][b:b + 1, 128:256], L["ones_row_f"])
    p.dma("s", "di", L["onescf"][:], L["ones_col_f"])
    p.dma("s", "di", L["negselt"][:], L["negsel"])
    for i in range(KC):
        p.dma("s", "di", x_t[i][:], L["x_in"][128 * i:128 * (i + 1), :])
        p.dma("s", "di", L["bocol"][:, i:i + 1], L["bo_in"][128 * i:128 * (i + 1), :])
    for e in "tvag":
        p.wait(e, "di")
    p.v(lambda e: e.tensor_copy(L["onescb"][:], L["onescf"][:]))
    p.v(lambda e: e.tensor_copy(L["identb"][:], L["identt"][:]))
    p.barrier()

    emit_weight_prep(p, L)

    emit_ffn(p, L)
    if DEBUG_STAGE != 1:
        emit_attn(p, L)
        if DEBUG_STAGE != 2:
            emit_ffn(p, L)

    p.barrier()
    if DEBUG_DUMPS:
        p.dma("s", "di", L["dbg_m"], stage[0][:])
        p.dma("s", "di", L["dbg_r"], stage[1][:])
        p.dma("s", "di", L["dbg_xp"], L["Xp"](0))
        p.dma("s", "di", L["dbg_gate"], L["gate_d"][0])
        p.dma("s", "di", L["dbg_p"], L["p_d"][0])
        p.dma("s", "di", L["dbg_q"], L["q_d"][0])
        p.dma("s", "di", L["dbg_cols"], L["cols"][:])
        p.dma("s", "di", L["dbg_wg"], L["cc_out_g"][0:128, :])
        p.dma("s", "di", L["dbg_row"], L["rowh1"][0:1, :])
        for ot_ in range(IT):
            p.dma("s", "di", L["dbg_gall"][128 * ot_:128 * (ot_ + 1), :], L["gate_d"][ot_])
            p.dma("s", "di", L["dbg_uall"][128 * ot_:128 * (ot_ + 1), :], L["up_d"][ot_])
        p.dma("s", "di", L["dbg_acc"], L["dbgt"][:])
        p.dma("s", "di", L["dbg_s2"], stage[2][:])
        for k_ in range(NCORES):
            p.dma("s", "di", L["dbg_wgall"][128 * k_:128 * (k_ + 1), :], L["cc_out_g"][128 * k_:128 * (k_ + 1), :])
        if DEBUG_STAGE == 2:
            p.dma("s", "di", L["dbg_eq"], L["eq_d"][0, 0])
            p.dma("s", "di", L["dbg_wq"], L["cc_out_att"][0:128, :])
    for i in range(KC):
        p.dma("s", "di", L["out"][128 * i:128 * (i + 1), :], x_t[i][:])
    p.wait("s", "di")


def emit_weight_prep(p, L):
    stage, sblk = L["stage"], L["sblk"]
    identb, psb = L["identb"], L["psb"]
    Ab, Wb = L["Ab"], L["Wb"]
    junk = L["junk"]

    units = [("att", r) for r in range(4)]
    units += [("g", ot) for ot in range(IT)]
    units += [("u", ot) for ot in range(IT)]
    units += [("d", j, h) for j in range(3) for h in range(KC)]

    def unit_src(u):
        if u[0] == "att":
            return L["att4_in"][128 * u[1]:128 * (u[1] + 1), :], D
        if u[0] == "g":
            return L["wgc_in"][128 * u[1]:128 * (u[1] + 1), :], 128
        if u[0] == "u":
            return L["wuc_in"][128 * u[1]:128 * (u[1] + 1), :], 128
        _, j, h = u
        return L["wdc_in"][128 * h:128 * (h + 1), 128 * j:128 * (j + 1)], 128

    def unit_dst(u):
        if u[0] == "att":
            return L["cc_in_att"][128 * u[1]:128 * (u[1] + 1), :]
        if u[0] == "g":
            return L["cc_in_g"][:, 128 * u[1]:128 * (u[1] + 1)]
        if u[0] == "u":
            return L["cc_in_u"][:, 128 * u[1]:128 * (u[1] + 1)]
        _, j, h = u
        return L["cc_in_d"][1024 * j + 128 * h:1024 * j + 128 * (h + 1), :]

    di_at = {}
    gdone = {}
    arena = L["arena"]
    dqs = []

    def slot_ap(d):
        return arena[:, 16384 + 128 * d:16384 + 128 * (d + 1)]

    batches = [("att", r) for r in range(4)]
    batches += [("wg", b) for b in range(6)] + [("wu", b) for b in range(6)]
    batches += [("wd", j, hb) for j in range(3) for hb in range(2)]

    wg3 = L["wgc_in"].rearrange("(t p) c -> p t c", p=128)
    wu3 = L["wuc_in"].rearrange("(t p) c -> p t c", p=128)
    wd3 = L["wdc_in"].rearrange("(t p) c -> p t c", p=128)

    def batch_src(u):
        if u[0] == "att":
            return L["att4_in"][128 * u[1]:128 * (u[1] + 1), :], D
        if u[0] == "wg":
            return wg3[:, 4 * u[1]:4 * (u[1] + 1), :], 512
        if u[0] == "wu":
            return wu3[:, 4 * u[1]:4 * (u[1] + 1), :], 512
        _, j, hb = u
        return wd3[:, 4 * hb:4 * (hb + 1), 128 * j:128 * (j + 1)], 512

    def batch_tiles(u):
        if u[0] == "wg":
            return [("g", 4 * u[1] + t) for t in range(4)]
        if u[0] == "wu":
            return [("u", 4 * u[1] + t) for t in range(4)]
        _, j, hb = u
        return [("d", j, 4 * hb + t) for t in range(4)]

    def in_slot(n, w):
        return stage[(n % 4) // 2][:, 1024 * (n % 2):1024 * (n % 2) + w]

    def do_load(n):
        u = batches[n]
        src_, w = batch_src(u)
        dst = in_slot(n, w)
        if u[0] != "att":
            dst = dst.rearrange("p (t c) -> p t c", c=128)
        di_at[n] = p.dma("s", "di", dst, src_)

    for n in range(4):
        do_load(n)
    cast_hist = []
    for n, u in enumerate(batches):
        _, w = batch_src(u)
        wbuf = in_slot(n, w)
        p.wait("v", "di", di_at[n])
        if u[0] == "att":
            btile = arena[:, 25600 + 1024 * (n % 2):25600 + 1024 * (n % 2) + 1024]
            if n >= 2 and batches[n - 2][0] == "att":
                p.wait("v", "dg", gdone[batches[n - 2]])
            scale = NORM if u[1] < 2 else 1.0
            if scale != 1.0:
                p.v(lambda e, b=btile, wb=wbuf, s_=scale: e.tensor_scalar_mul(b, wb, s_))
            else:
                p.v(lambda e, b=btile, wb=wbuf: e.tensor_copy(b, wb))
            nv = p.mark("v", "V")
            p.wait("g", "V", nv)
            gdone[u] = p.dma("g", "dg", unit_dst(u), btile)
        else:
            d0 = len(dqs)
            btile = arena[:, 16384 + 128 * d0:16384 + 128 * d0 + 512]
            wrow = wbuf
            wv3 = wrow.rearrange("p (b c) -> p b c", c=32)
            nblk = w // 32
            s1 = sblk[0][:, 0:nblk]
            s2 = sblk[1][:, 0:nblk]
            s3 = sblk[2][:, 0:nblk]
            p.v(lambda e, wv3=wv3, s1=s1: e.tensor_reduce(s1, wv3, AX.X, ALU.max, apply_absolute_value=True))
            p.v(lambda e: e.tensor_scalar_add(junk[:, 0:1], junk[:, 0:1], 1.0))
            p.v(lambda e, s1=s1, s2=s2: e.tensor_scalar(s2, s1, 1.0 / 7.0, 1e-12, ALU.mult, ALU.max))
            p.v(lambda e: e.tensor_scalar_add(junk[:, 0:1], junk[:, 0:1], 1.0))
            p.v(lambda e, s2=s2, s3=s3: e.reciprocal(s3, s2))
            p.v(lambda e: e.tensor_scalar_add(junk[:, 0:1], junk[:, 0:1], 1.0))
            s3b = s3.unsqueeze(2).broadcast_to([128, nblk, 32])
            s2b = s2.unsqueeze(2).broadcast_to([128, nblk, 32])
            p.v(lambda e, wv3=wv3, s3b=s3b: e.tensor_tensor(wv3, wv3, s3b, ALU.mult))
            p.v(lambda e, w2=wrow: e.tensor_single_scalar(w2, w2, MAGIC, ALU.add))
            p.v(lambda e, w2=wrow: e.tensor_single_scalar(w2, w2, MAGIC, ALU.subtract))
            f32q = stage[2][:, 512 * (n % 4):512 * (n % 4) + 512]
            if len(cast_hist) >= 4:
                p.wait("v", "A", cast_hist[-4])
            p.v(lambda e, wv3=wv3, s2b=s2b, f3=f32q.rearrange("p (b c) -> p b c", c=32):
                e.tensor_tensor(f3, wv3, s2b, ALU.mult))
            nv = p.mark("v", "V")
            p.wait("a", "V", nv)
            p.inc("a", lambda e, b=btile, f=f32q: e.activation(b, f, AF.Copy), "A")
            cast_hist.append(p.cnt["A"])
            dqs.extend(batch_tiles(u))
        if n + 4 < len(batches):
            p.wait("s", "V")
            do_load(n + 4)
    p.barrier()

    evA = 0
    abs_ = [arena[:, 27648:27776], arena[:, 27776:27904]]
    gd2 = {}
    for d, m in enumerate(dqs):
        if evA:
            p.wait("t", "A", evA)
        p.t(lambda e, b=slot_ap(d): e.transpose(psb[:, 0:128], b, identb[:]))
        nt = p.mark("t", "T")
        p.wait("a", "T", nt)
        if d >= 2:
            p.wait("a", "dg", gd2[d - 2])
        p.inc("a", lambda e, ab=abs_[d % 2]: e.activation(ab, psb[:, 0:128], AF.Copy), "A")
        evA = p.cnt["A"]
        p.wait("g", "A", evA)
        gd2[d] = p.dma("g", "dg", unit_dst(m), abs_[d % 2])
        p.wait("g", "dg")

    p.barrier()
    p.dma("s", "di", stage[0][0:DH, 0:M], L["pjT_in"])
    p.wait("v", "di")
    p.v(lambda e: e.tensor_copy(L["pjT"][0:DH, :], stage[0][0:DH, 0:M]))
    p.v(lambda e: e.tensor_copy(L["pjT"][64:64 + DH, :], stage[0][0:DH, 0:M]))
    nv = p.mark("v", "V")
    p.wait("s", "V", nv)
    p.barrier()

    for (ci, co, key) in [("cc_in_g", "cc_out_g", "cc_g"), ("cc_in_u", "cc_out_u", "cc_u")]:
        p.inc("g", lambda e, ci=ci, co=co: e.collective_compute(
            "AllGather", ALU.bypass, [[0, 1, 2, 3, 4, 5, 6, 7]],
            [L[ci][:].opt()], [L[co][:].opt()]), "CC")
        L[key] = p.cnt["CC"]


def ln_stats(p, L):
    x_t, stage, psf = L["x_t"], L["stage"], L["psf"]
    row = L["row"]
    onescf = L["onescf"]
    for bl in range(NB):
        for k in range(KC):
            p.t(lambda e, k=k, bl=bl: e.matmul(psf[4][0:1, :], onescf[:],
                                               x_t[k][:, BL * bl:BL * (bl + 1)],
                                               start=(k == 0), stop=(k == KC - 1)))
        for k in range(KC):
            na = p.inc("a", lambda e, k=k, bl=bl: e.activation(
                stage[2][:, 0:BL], x_t[k][:, BL * bl:BL * (bl + 1)], AF.Square), "A")
            p.wait("t", "A", na)
            p.t(lambda e, k=k: e.matmul(psf[5][0:1, :], onescf[:], stage[2][:, 0:BL],
                                        start=(k == 0), stop=(k == KC - 1)))
            if k < KC - 1:
                nt = p.mark("t", "T")
                p.wait("a", "T", nt)
        nt = p.mark("t", "T")
        p.wait("v", "T", nt)
        p.wait("a", "T", nt)
        mrow = row(0, 0)[:, 0:BL]
        erow = row(0, 0)[:, BL:2 * BL]
        tmp = row(0, 0)[:, 2 * BL:3 * BL]
        p.v(lambda e, mr=mrow: e.tensor_scalar_mul(mr, psf[4][0:1, :], 1.0 / 1024.0))
        p.v(lambda e, rr=erow: e.tensor_scalar_mul(rr, psf[5][0:1, :], 1.0 / 1024.0))
        p.v(lambda e, mr=mrow, tm=tmp: e.tensor_mul(tm, mr, mr))
        p.v(lambda e, rr=erow, tm=tmp: e.tensor_sub(rr, rr, tm))
        nv = p.mark("v", "V")
        p.wait("a", "V", nv)
        p.a(lambda e, rr=erow: e.activation(rr, rr, AF.Sqrt, bias=EPS_LN, scale=1.0))
        na = p.mark("a", "A")
        p.wait("v", "A", na)
        p.v(lambda e, rr=erow: e.reciprocal(rr, rr))
        nv = p.mark("v", "V")
        p.wait("t", "V", nv)
        for (r, d, ps) in [(mrow, stage[0], psf[4]), (erow, stage[1], psf[5])]:
            p.t(lambda e, r=r, ps=ps: e.matmul(ps[:, 0:BL], L["onesr3"][0:1, 0:128], r, start=True, stop=True))
            nt = p.mark("t", "T")
            p.wait("a", "T", nt)
            na = p.inc("a", lambda e, ps=ps, d=d, bl=bl: e.activation(
                d[:, BL * bl:BL * (bl + 1)], ps[:, 0:BL], AF.Copy), "A")
            p.wait("t", "A", na)
            p.wait("v", "A", na)
    p.mark("a", "A")


def scalars_to_cols(p, L, src_row, ncol, dst_col0):
    psf, onesrf, cols = L["psf"], L["onesrf"], L["cols"]
    nv = p.mark("v", "V")
    ng = p.mark("g", "G")
    p.wait("t", "V", nv)
    p.wait("t", "G", ng)
    b0 = src_row.base_partition()
    p.t(lambda e, b0=b0: e.matmul(psf[6][:, 0:ncol], L["onesr3"][b0:b0 + 1, 0:128], src_row, start=True, stop=True))
    nt = p.mark("t", "T")
    p.wait("a", "T", nt)
    na = p.inc("a", lambda e: e.activation(cols[:, dst_col0:dst_col0 + ncol], psf[6][:, 0:ncol], AF.Copy), "A")
    p.wait("v", "A", na)
    p.wait("g", "A", na)
    p.wait("t", "A", na)


class MaxChain:

    def __init__(self, p, L):
        self.p, self.L = p, L
        self.n = 0

    def add(self, tile_ap):
        p, L, n = self.p, self.L, self.n
        redt = L["redt"]
        p.v(lambda e, t=tile_ap, r=redt[n % 2]: e.tensor_reduce(
            r[:], t, AX.X, ALU.max, apply_absolute_value=True))
        if n == 1:
            p.v(lambda e: e.tensor_copy(L["acc"][:], redt[0][:]))
        elif n > 1:
            p.v(lambda e, r=redt[(n - 1) % 2]: e.tensor_max(L["acc"][:], L["acc"][:], r[:]))
        self.n += 1

    def finish(self):
        p, L, n = self.p, self.L, self.n
        redt, junk = L["redt"], L["junk"]
        p.v(lambda e: e.tensor_scalar_add(junk[:, 0:1], junk[:, 0:1], 1.0))
        if n == 1:
            p.v(lambda e: e.tensor_copy(L["acc"][:], redt[0][:]))
        else:
            p.v(lambda e, r=redt[(n - 1) % 2]: e.tensor_max(L["acc"][:], L["acc"][:], r[:]))
        p.v(lambda e: e.tensor_scalar_add(junk[:, 1:2], junk[:, 1:2], 1.0))


def export_colmax(p, L, dst_rowcol):
    colw, acc = L["colw"], L["acc"]
    p.v(lambda e: e.tensor_copy(colw[:], acc[:, 0:1].broadcast_to([128, 64])))
    n = p.mark("v", "V")
    p.wait("g", "V", n)
    return p.inc("g", lambda e: e.tensor_reduce(dst_rowcol, colw[:, 0:1], AX.C, ALU.max), "G")


def allreduce_row(p, L, row_ap, ncols, op=ALU.max):
    cc_in, cc_out = L["cc_in_s"], L["cc_out_s"]
    p.dma("g", "dg", cc_in[0:1, 0:ncols], row_ap)
    p.wait("g", "dg")
    n = p.inc("g", lambda e: e.collective_compute(
        "AllReduce", op, [[0, 1, 2, 3, 4, 5, 6, 7]],
        [cc_in[0:1, 0:ncols].opt()], [cc_out[0:1, 0:ncols].opt()]), "CC")
    p.wait("g", "CC", n)
    p.dma("g", "dg", row_ap, cc_out[0:1, 0:ncols])
    p.wait("v", "dg")


def emit_ffn(p, L):
    x_t, stage, psf = L["x_t"], L["stage"], L["psf"]
    row, cols, junk = L["row"], L["cols"], L["junk"]
    redt = L["redt"]
    Xp, Wb, Ob = L["Xp"], L["Wb"], L["Ob"]
    wdtile = L["wdtile"]

    ln_stats(p, L)
    p.barrier()
    mc = MaxChain(p, L)
    for i in range(KC):
        p.v(lambda e, i=i: e.tensor_sub(stage[2][:], x_t[i][:], stage[0][:]))
        p.v(lambda e: e.tensor_mul(stage[2][:], stage[2][:], stage[1][:]))
        mc.add(stage[2][:])
    mc.finish()
    export_colmax(p, L, row(1, 0)[:, 0:1])
    p.wait("g", "G")
    allreduce_row(p, L, row(1, 0)[:, 0:1], 1, ALU.max)
    if "cc_d" not in L:
        for (ci, co, key) in [("cc_in_d", "cc_out_d", "cc_d"), ("cc_in_att", "cc_out_att", "cc_att")]:
            p.inc("g", lambda e, ci=ci, co=co: e.collective_compute(
                "AllGather", ALU.bypass, [[0, 1, 2, 3, 4, 5, 6, 7]],
                [L[ci][:].opt()], [L[co][:].opt()]), "CC")
            L[key] = p.cnt["CC"]
    sx = row(1, 0)[:, 1:2]
    wideA = row(1, 0)[:, 64:128]
    p.wait("g", "dg")
    p.g(lambda e: e.tensor_scalar_mul(sx, row(1, 0)[:, 0:1], 1.0 / QMAX))
    p.g(lambda e: e.tensor_scalar_max(sx, sx, 1e-12))
    ng = p.mark("g", "G")
    p.wait("v", "G", ng)
    p.v(lambda e: e.reciprocal(wideA, sx.broadcast_to([1, 64])))
    nv = p.mark("v", "V")
    p.wait("g", "V", nv)
    p.g(lambda e: e.tensor_copy(row(1, 0)[:, 2:3], wideA[:, 0:1]))
    scalars_to_cols(p, L, row(1, 0)[:, 1:3], 2, 0)
    for i in range(KC):
        p.v(lambda e, i=i: e.tensor_sub(stage[2][:], x_t[i][:], stage[0][:]))
        p.v(lambda e: e.tensor_mul(stage[2][:], stage[2][:], stage[1][:]))
        p.v(lambda e: e.tensor_scalar(stage[2][:], stage[2][:], cols[:, 1:2], MAGIC, ALU.mult, ALU.add))
        p.v(lambda e, i=i: e.tensor_single_scalar(Xp(i), stage[2][:], MAGIC, ALU.subtract))
    p.barrier()

    NBANK = 6
    for phase, (wT_src, spill) in enumerate([(L["cc_out_g"], L["gate_d"]), (L["cc_out_u"], L["up_d"])]):
        p.wait("s", "CC", L["cc_g"] if phase == 0 else L["cc_u"])
        for k in range(KC):
            p.dma("s", "di", Wb(k)[:, 0:I3], wT_src[128 * k:128 * (k + 1), :])
        p.wait("t", "di")
        mc = MaxChain(p, L)
        tA = [0] * NBANK
        dg_at = {}
        v_at = {}
        idx = 0
        for ot in range(IT):
            stg = stage[ot % 3]
            for bl in range(NB):
                b = idx % NBANK
                idx += 1
                if tA[b]:
                    p.wait("t", "A", tA[b])
                for k in range(KC):
                    p.t(lambda e, k=k, bl=bl, ot=ot, b=b: e.matmul(
                        psf[b][:, 0:BL], Wb(k)[:, 128 * ot:128 * (ot + 1)],
                        Xp(k)[:, BL * bl:BL * (bl + 1)],
                        start=(k == 0), stop=(k == KC - 1)))
                nt = p.mark("t", "T")
                p.wait("a", "T", nt)
                if ot >= 3 and bl == 0:
                    p.wait("a", "dg", dg_at[ot - 3])
                    p.wait("a", "V", v_at[ot - 3])
                p.a(lambda e, b=b, bl=bl, stg=stg: e.activation(
                    stg[:, BL * bl:BL * (bl + 1)], psf[b][:, 0:BL], AF.Copy, bias=0.0, scale=cols[:, 0:1]))
                tA[b] = p.mark("a", "A")
            p.wait("v", "A")
            mc.add(stg[:])
            v_at[ot] = p.mark("v", "V")
            p.wait("g", "A")
            dg_at[ot] = p.dma("g", "dg", spill[ot], stg[:])
        mc.finish()
        export_colmax(p, L, row(1, 0)[:, 3 + phase:4 + phase])
        p.wait("g", "G")
        if DEBUG_DUMPS:
            p.g(lambda e, ph=phase: e.tensor_copy(L["dbgt"][:, ph:ph + 1], L["acc"][:, 0:1]))
            p.g(lambda e, ph=phase: e.tensor_copy(L["dbgt"][0:1, 2 + ph:3 + ph], row(1, 0)[:, 3 + ph:4 + ph]))
        p.barrier()

    allreduce_row(p, L, row(1, 0)[:, 3:5], 2, ALU.max)
    sg = row(1, 0)[:, 5:6]
    su = row(1, 0)[:, 6:7]
    wideA = row(1, 0)[:, 64:128]
    wideB = row(1, 0)[:, 128:192]
    p.wait("g", "dg")
    p.g(lambda e: e.tensor_scalar_mul(sg, row(1, 0)[:, 3:4], 1.0 / QMAX))
    p.g(lambda e: e.tensor_scalar_max(sg, sg, 1e-12))
    p.g(lambda e: e.tensor_scalar_mul(su, row(1, 0)[:, 4:5], 1.0 / QMAX))
    p.g(lambda e: e.tensor_scalar_max(su, su, 1e-12))
    ng = p.mark("g", "G")
    p.wait("v", "G", ng)
    p.v(lambda e: e.reciprocal(wideA, sg.broadcast_to([1, 64])))
    p.v(lambda e: e.reciprocal(wideB, su.broadcast_to([1, 64])))
    nv = p.mark("v", "V")
    p.wait("g", "V", nv)
    p.g(lambda e: e.tensor_copy(row(1, 0)[:, 7:8], wideA[:, 0:1]))
    p.g(lambda e: e.tensor_copy(row(1, 0)[:, 8:9], wideB[:, 0:1]))
    scalars_to_cols(p, L, row(1, 0)[:, 5:9], 4, 2)
    p.barrier()

    HB = T // 2
    mc = MaxChain(p, L)
    halves = [(ot, hf) for ot in range(IT) for hf in range(2)]
    di_at = {}
    dg_at = {}
    na_at = {}
    vG = {}
    for j in range(2):
        ot, hf = halves[j]
        di_at[j] = p.dma("s", "di", stage[j % 2][:, 0:HB], L["gate_d"][ot, :, HB * hf:HB * (hf + 1)])

    def emit_G(j):
        bufi = stage[j % 2][:, 0:HB]
        tmp = stage[2][:, HB * (j % 2):HB * (j % 2) + HB]
        if j >= 2:
            p.wait("v", "A", na_at[j - 2])
        p.wait("v", "di", di_at[j])
        p.v(lambda e, bufi=bufi, tmp=tmp: e.tensor_scalar(tmp, bufi, cols[:, 4:5], MAGIC, ALU.mult, ALU.add))
        p.v(lambda e, tmp=tmp: e.tensor_single_scalar(tmp, tmp, MAGIC, ALU.subtract))
        vG[j] = p.mark("v", "V")

    emit_G(0)
    for j, (ot, hf) in enumerate(halves):
        bufi = stage[j % 2][:, 0:HB]
        tmp = stage[2][:, HB * (j % 2):HB * (j % 2) + HB]
        p.wait("a", "di", di_at[j])
        p.wait("a", "V", vG[j])
        p.a(lambda e, bufi=bufi, tmp=tmp: e.activation(bufi, tmp, AF.Sigmoid, bias=0.0, scale=cols[:, 2:3]))
        p.a(lambda e, bufi=bufi: e.activation(bufi, bufi, AF.Copy, bias=MAGIC, scale=65536.0))
        na_at[j] = p.mark("a", "A")
        if j + 1 < len(halves):
            emit_G(j + 1)
        p.wait("v", "A", na_at[j])
        p.v(lambda e, bufi=bufi: e.tensor_single_scalar(bufi, bufi, MAGIC, ALU.subtract))
        p.v(lambda e, bufi=bufi, tmp=tmp: e.scalar_tensor_tensor(bufi, bufi, 65535.0, tmp, ALU.min, ALU.mult))
        mc.add(bufi)
        nv = p.mark("v", "V")
        p.wait("g", "V", nv)
        dg_at[j] = p.dma("g", "dg", L["p_d"][ot, :, HB * hf:HB * (hf + 1)], bufi)
        if j + 2 < len(halves):
            ot2, hf2 = halves[j + 2]
            p.wait("s", "dg", dg_at[j])
            di_at[j + 2] = p.dma("s", "di", stage[(j + 2) % 2][:, 0:HB],
                                 L["gate_d"][ot2, :, HB * hf2:HB * (hf2 + 1)])
    mc.finish()
    export_colmax(p, L, row(1, 0)[:, 9:10])
    p.wait("g", "G")
    allreduce_row(p, L, row(1, 0)[:, 9:10], 1, ALU.max)
    sg2 = row(1, 0)[:, 10:11]
    c1 = row(1, 0)[:, 11:12]
    wideA = row(1, 0)[:, 64:128]
    p.wait("g", "dg")
    p.g(lambda e: e.tensor_mul(sg2, row(1, 0)[:, 9:10], sg))
    p.g(lambda e: e.tensor_scalar_mul(sg2, sg2, 1.0 / (65536.0 * QMAX)))
    p.g(lambda e: e.tensor_scalar_max(sg2, sg2, 1e-12))
    ng = p.mark("g", "G")
    p.wait("v", "G", ng)
    p.v(lambda e: e.reciprocal(wideA, sg2.broadcast_to([1, 64])))
    nv = p.mark("v", "V")
    p.wait("g", "V", nv)
    p.g(lambda e: e.tensor_mul(c1, wideA[:, 0:1], sg))
    p.g(lambda e: e.tensor_scalar_mul(c1, c1, 1.0 / 65536.0))
    scalars_to_cols(p, L, row(1, 0)[:, 10:12], 2, 6)
    p.barrier()

    mc = MaxChain(p, L)
    di_at = {}
    dg_at = {}
    QH = T // 2
    for j, (ot, hf) in enumerate(halves):
        if j < 1:
            p.dma("s", "di", stage[0][:, QH * (j % 2):QH * (j % 2) + QH], L["up_d"][ot, :, QH * hf:QH * (hf + 1)])
            di_at[j] = p.dma("s", "di", stage[1][:, QH * (j % 2):QH * (j % 2) + QH], L["p_d"][ot, :, QH * hf:QH * (hf + 1)])
    for j, (ot, hf) in enumerate(halves):
        par = j % 2
        bup = stage[0][:, QH * par:QH * par + QH]
        bp = stage[1][:, QH * par:QH * par + QH]
        tmp = stage[2][:, QH * par:QH * par + QH]
        p.wait("a", "di", di_at[j])
        p.wait("v", "di", di_at[j])
        p.a(lambda e, bup=bup, tmp=tmp: e.activation(tmp, bup, AF.Copy, bias=MAGIC, scale=cols[:, 5:6]))
        na = p.mark("a", "A")
        p.wait("v", "A", na)
        p.v(lambda e, tmp=tmp: e.tensor_single_scalar(tmp, tmp, MAGIC, ALU.subtract))
        p.v(lambda e, bp=bp: e.tensor_scalar(bp, bp, cols[:, 7:8], MAGIC, ALU.mult, ALU.add))
        p.v(lambda e, bp=bp: e.tensor_single_scalar(bp, bp, MAGIC, ALU.subtract))
        p.v(lambda e, bup=bup, bp=bp, tmp=tmp: e.tensor_mul(bup, tmp, bp))
        mc.add(bup)
        nv = p.mark("v", "V")
        p.wait("g", "V", nv)
        dg_at[j] = p.dma("g", "dg", L["q_d"][ot, :, QH * hf:QH * (hf + 1)], bup)
        if j + 1 < len(halves):
            ot2, hf2 = halves[j + 1]
            par2 = (j + 1) % 2
            p.wait("s", "dg", dg_at[j]) if False else None
            if j >= 1:
                p.wait("s", "dg", dg_at[j - 1])
            p.dma("s", "di", stage[0][:, QH * par2:QH * par2 + QH], L["up_d"][ot2, :, QH * hf2:QH * (hf2 + 1)])
            di_at[j + 1] = p.dma("s", "di", stage[1][:, QH * par2:QH * par2 + QH], L["p_d"][ot2, :, QH * hf2:QH * (hf2 + 1)])
    mc.finish()
    export_colmax(p, L, row(1, 0)[:, 12:13])
    p.wait("g", "G")
    allreduce_row(p, L, row(1, 0)[:, 12:13], 1, ALU.max)
    c2 = row(1, 0)[:, 13:14]
    so = row(1, 0)[:, 14:15]
    c3 = row(1, 0)[:, 15:16]
    halfso = row(1, 0)[:, 16:17]
    wideA = row(1, 0)[:, 64:128]
    p.wait("g", "dg")
    p.g(lambda e: e.tensor_mul(c2, su, sg2))
    p.g(lambda e: e.tensor_mul(so, row(1, 0)[:, 12:13], c2))
    p.g(lambda e: e.tensor_scalar_mul(so, so, 1.0 / QMAX))
    p.g(lambda e: e.tensor_scalar_max(so, so, 1e-12))
    p.g(lambda e: e.tensor_scalar_mul(halfso, so, 0.5))
    ng = p.mark("g", "G")
    p.wait("v", "G", ng)
    p.v(lambda e: e.reciprocal(wideA, so.broadcast_to([1, 64])))
    nv = p.mark("v", "V")
    p.wait("g", "V", nv)
    p.g(lambda e: e.tensor_mul(c3, wideA[:, 0:1], c2))
    scalars_to_cols(p, L, row(1, 0)[:, 15:17], 2, 8)
    p.barrier()

    di_at = {}
    for j, (ot, hf) in enumerate(halves):
        par = j % 2
        if j < 2:
            di_at[j] = p.dma("s", "di", stage[par][:, 0:QH], L["q_d"][ot, :, QH * hf:QH * (hf + 1)])
    for j, (ot, hf) in enumerate(halves):
        par = j % 2
        bq = stage[par][:, 0:QH]
        tmp = stage[2][:, QH * par:QH * par + QH]
        p.wait("a", "di", di_at[j])
        p.a(lambda e, bq=bq, tmp=tmp: e.activation(tmp, bq, AF.Copy, bias=MAGIC, scale=cols[:, 8:9]))
        na = p.mark("a", "A")
        if j + 2 < len(halves):
            ot2, hf2 = halves[j + 2]
            p.wait("s", "A", na)
            di_at[j + 2] = p.dma("s", "di", stage[(j + 2) % 2][:, 0:QH], L["q_d"][ot2, :, QH * hf2:QH * (hf2 + 1)])
        p.wait("v", "A", na)
        p.v(lambda e, ot=ot, hf=hf, tmp=tmp: e.tensor_single_scalar(
            Ob(ot)[:, QH * hf:QH * (hf + 1)], tmp, MAGIC, ALU.subtract))
    nv = p.mark("v", "V")
    p.wait("t", "V", nv)
    p.wait("s", "V", nv)

    tT = {}
    tA4 = [0, 0, 0, 0]
    v_at = {}
    groups = [(h, i) for h in range(KC) for i in range(IT)]
    p.wait("s", "CC", L["cc_d"])
    wd_src = lambda i, h: L["cc_out_d"][3072 * (i // 3) + 1024 * (i % 3) + 128 * h:
                                        3072 * (i // 3) + 1024 * (i % 3) + 128 * h + 128, :]
    if not F4_SERIAL_W:
        for g in range(2):
            h2, i2 = groups[g]
            p.dma("s", "di", wdtile[g % 3][:], wd_src(i2, h2))
    for g, (h, i) in enumerate(groups):
        stg = stage[h % 3]
        if True:
            wt = wdtile[g % 3]
            if F4_SERIAL_W:
                if g >= 1:
                    p.wait("s", "T", tT[groups[g - 1]])
                p.dma("s", "di", wt[:], wd_src(i, h))
            p.wait("t", "di")
            for bl in range(NB):
                if i == 0 and tA4[bl]:
                    p.wait("t", "A", tA4[bl])
                p.t(lambda e, i=i, bl=bl, wt=wt: e.matmul(
                    psf[bl][:, 0:BL], wt[:], Ob(i)[:, BL * bl:BL * (bl + 1)],
                    start=(i == 0), stop=(i == IT - 1)))
            nt = p.mark("t", "T")
            tT[(h, i)] = nt
            if g + 2 < len(groups) and not F4_SERIAL_W:
                h2, i2 = groups[g + 2]
                if g >= 1:
                    p.wait("s", "T", tT[groups[g - 1]])
                p.dma("s", "di", wdtile[(g + 2) % 3][:], wd_src(i2, h2))
        if i == IT - 1:
            p.wait("a", "T", tT[(h, IT - 1)])
            if h >= 3:
                p.wait("a", "V", v_at[h - 3])
            for bl in range(NB):
                p.a(lambda e, bl=bl, stg=stg: e.activation(stg[:, BL * bl:BL * (bl + 1)],
                                                           psf[bl][:, 0:BL], AF.Copy, bias=0.0, scale=cols[:, 9:10]))
                tA4[bl] = p.mark("a", "A")
            p.wait("v", "A", tA4[3])
            p.v(lambda e, h=h, stg=stg: e.tensor_add(x_t[h][:], x_t[h][:], stg[:]))
            v_at[h] = p.mark("v", "V")
            if F4_BARRIER:
                p.barrier()
    p.wait("t", "V")
    p.wait("a", "V")
    p.barrier()


def A_views(L):
    a = L["arena"]
    V = {}
    V["wqa"] = lambda k: a[:, 16384 + 1024 * k: 16384 + 1024 * (k + 1)]
    V["wka"] = lambda k: a[:, 24576 + 1024 * k: 24576 + 1024 * (k + 1)]
    V["wva"] = lambda k: a[:, 32768 + 1024 * k: 32768 + 1024 * (k + 1)]
    V["woa"] = lambda k: a[:, 16384 + 1024 * k: 16384 + 1024 * (k + 1)]
    V["qp"] = a[:, 40960:43008]
    V["kp"] = a[:, 43008:45056]
    V["vp"] = a[:, 45056:47136]
    V["ekb"] = a[:, 47136:47392]
    V["eqs"] = [a[:, 47392 + 512 * i: 47392 + 512 * (i + 1)] for i in range(2)]
    return V


def _attn_pair(p, L, V, pp):
    stage, psf, psb = L["stage"], L["psf"], L["psb"]
    junk, redt = L["junk"], L["redt"]
    Xp = L["Xp"]
    pjT, onesr3, onescf, onescb = L["pjT"], L["onesr3"], L["onescf"], L["onescb"]
    negselt, identb = L["negselt"], L["identb"]
    stab_tm, stabb, stabT, kacc = L["stab_tm"], L["stabb"], L["stabT"], L["kacc"]
    ctxe, svc = L["ctxe"], L["svc"]
    qp, kp, vp, ekb, eqs = V["qp"], V["kp"], V["vp"], V["ekb"], V["eqs"]
    NTT_ = NTT
    for (wsel, dst) in [("wqa", qp), ("wka", kp)]:
        pa = [psf[b] for b in range(4)]
        for k in range(KC):
            for bl in range(NB):
                p.t(lambda e, k=k, bl=bl, wsel=wsel, pp=pp, pa=pa: e.matmul(
                    pa[bl][:, 0:BL], V[wsel](k)[:, 128 * pp:128 * (pp + 1)],
                    Xp(k)[:, BL * bl:BL * (bl + 1)],
                    start=(k == 0), stop=(k == KC - 1)))
        nt = p.mark("t", "T")
        p.wait("a", "T", nt)
        for bl in range(NB):
            p.a(lambda e, bl=bl, dst=dst, pa=pa: e.activation(
                dst[:, BL * bl:BL * (bl + 1)], pa[bl][:, 0:BL], AF.Copy))
        na = p.mark("a", "A")
        p.wait("t", "A", na)
    for tt in range(NTT_):
        for k in range(KC):
            p.t(lambda e, tt=tt, k=k, pp=pp: e.matmul(
                psf[4][:, 0:128], Xp(k)[:, 128 * tt:128 * (tt + 1)],
                V["wva"](k)[:, 128 * pp:128 * (pp + 1)],
                start=(k == 0), stop=(k == KC - 1)))
        nt = p.mark("t", "T")
        p.wait("a", "T", nt)
        p.a(lambda e, tt=tt: e.activation(vp[:, 130 * tt:130 * tt + 64], psf[4][:, 0:64], AF.Copy))
        p.a(lambda e, tt=tt: e.activation(vp[:, 130 * tt + 65:130 * tt + 129], psf[4][:, 64:128], AF.Copy))
        na = p.mark("a", "A")
        p.wait("t", "A", na)
    p.wait("g", "A")
    ng = p.inc("g", lambda e: e.memset(vp[:].rearrange("p (t c) -> p t c", c=130)[:, :, 64:65], 1.0), "G")
    ng = p.inc("g", lambda e: e.memset(vp[:].rearrange("p (t c) -> p t c", c=130)[:, :, 129:130], 1.0), "G")
    p.wait("t", "G", ng)
    p.wait("v", "G", ng)
    for hh in range(2):
        _attn_head(p, L, V, pp, hh)


def _attn_head(p, L, V, pp, hh):
    stage, psf, psb = L["stage"], L["psf"], L["psb"]
    junk, redt = L["junk"], L["redt"]
    pjT, onesr3, onescf, onescb = L["pjT"], L["onesr3"], L["onescf"], L["onescb"]
    negselt, identb = L["negselt"], L["identb"]
    stab_tm, stabb, stabT, kacc = L["stab_tm"], L["stabb"], L["stabT"], L["kacc"]
    ctxe, svc = L["ctxe"], L["svc"]
    qp, kp, vp, ekb, eqs = V["qp"], V["kp"], V["vp"], V["ekb"], V["eqs"]
    h = 2 * pp + hh
    hb = 64 * hh
    qh = qp[hb:hb + 64, :]
    kh = kp[hb:hb + 64, :]
    pj = pjT[hb:hb + 64, :]
    for (src_, prow) in [(qh, 0), (kh, 32)]:
        for bl in range(NB):
            na = p.inc("a", lambda e, src_=src_, bl=bl: e.activation(
                stage[2][0:64, 0:BL], src_[:, BL * bl:BL * (bl + 1)], AF.Square), "A")
            p.wait("t", "A", na)
            p.t(lambda e: e.matmul(psf[6][0:1, :], onescf[0:64, :], stage[2][0:64, 0:BL],
                                   start=True, stop=True))
            nt = p.mark("t", "T")
            p.wait("a", "T", nt)
            na = p.inc("a", lambda e, prow=prow, bl=bl: e.activation(
                L["rowh0"][prow:prow + 1, BL * bl:BL * (bl + 1)], psf[6][0:1, :],
                AF.Copy, bias=0.0, scale=-0.5), "A")
            p.wait("t", "A", na)
    p.wait("v", "A")

    if h == 0 and L.get("dump", False):
        p.wait("g", "A")
        p.dma("g", "dg", L["dbg_nd"][0:1, 0:T], L["rowh0"][0:1, :])
        p.dma("g", "dg", L["dbg_nd"][0:1, T:2 * T], L["rowh0"][32:33, :])
        p.dma("g", "dg", L["dbg_qp"], qp[:])
        p.wait("v", "dg")
    for tt in range(NTT):
        p.t(lambda e, tt=tt: e.matmul(psf[4][:, 0:M], qh[:, 128 * tt:128 * (tt + 1)], pj,
                                      start=True, stop=True))
        nt = p.mark("t", "T")
        p.wait("v", "T", nt)
        p.v(lambda e, tt=tt: e.tensor_reduce(stab_tm[:, tt:tt + 1], psf[4][:, 0:M], AX.X, ALU.max))
        nv = p.mark("v", "V")
        p.wait("t", "V", nv)
    p.v(lambda e: e.tensor_scalar_add(junk[:, 0:1], junk[:, 0:1], 1.0))
    p.v(lambda e: e.tensor_copy(stabb[:], stab_tm[:]))
    nv = p.mark("v", "V")
    p.wait("t", "V", nv)
    p.t(lambda e: e.transpose(psb[0:16, 0:128], stabb[:], identb[:]))
    nt = p.mark("t", "T")
    p.wait("a", "T", nt)
    na = p.inc("a", lambda e: e.activation(stabT[:], psb[0:16, 0:128], AF.Copy), "A")
    na = p.inc("a", lambda e: e.activation(stage[2][0:16, 0:128], psb[0:16, 0:128],
                                           AF.Exp, bias=math.log(KEPS), scale=1.0), "A")
    p.wait("g", "A", na)
    p.dma("g", "dg", L["eps_d"][h:h + 1, :].rearrange("x (a b) -> (x a) b", a=16),
          stage[2][0:16, 0:128])
    p.wait("t", "A", na)
    p.wait("a", "dg")

    if h == 0 and L.get("dump", False):
        p.dma("g", "dg", L["dbg_stab"], stabT[:])
        p.wait("v", "dg")
    for mc in range(2):
        for bl in range(NB):
            ps = psf[mc % 2]
            p.t(lambda e, mc=mc, bl=bl, ps=ps: e.matmul(
                ps[:, 0:BL], pj[:, 128 * mc:128 * (mc + 1)],
                qh[:, BL * bl:BL * (bl + 1)], start=True, stop=False))
            p.t(lambda e, bl=bl, ps=ps: e.matmul(
                ps[:, 0:BL], onesr3[0:1, 0:128],
                L["rowh0"][0:1, BL * bl:BL * (bl + 1)], start=False, stop=False))
            for sb_ in range(4):
                tt = 4 * bl + sb_
                p.t(lambda e, tt=tt, sb_=sb_, ps=ps: e.matmul(
                    ps[:, 128 * sb_:128 * (sb_ + 1)],
                    negselt[:, 128 * tt:128 * (tt + 1)], stabT[:],
                    start=False, stop=(True)))
            nt = p.mark("t", "T")
            p.wait("a", "T", nt)
            na = p.inc("a", lambda e, ps=ps, bl=bl: e.activation(
                eqs[bl % 2][:], ps[:, 0:BL], AF.Exp), "A")
            p.wait("g", "A", na)
            p.dma("g", "dg", L["eq_d"][h, mc, :, BL * bl:BL * (bl + 1)], eqs[bl % 2][:])
            p.wait("t", "A", na)
            if bl >= 1:
                p.wait("a", "dg", p.cnt["dg"] - 16)
    p.wait("a", "dg")

    for tt in range(NTT):
        p.t(lambda e, tt=tt: e.matmul(psf[5][:, 0:M], kh[:, 128 * tt:128 * (tt + 1)], pj,
                                      start=True, stop=False))
        nt = p.mark("t", "T")
        p.wait("v", "T", nt)
        p.v(lambda e, tt=tt: e.tensor_reduce(redt[tt % 2][:], psf[5][:, 0:M], AX.X, ALU.max))
        if tt > 0:
            p.v(lambda e, tt=tt: e.tensor_max(kacc[:], kacc[:], redt[(tt - 1) % 2][:]))
        nv = p.mark("v", "V")
        p.wait("t", "V", nv)
        p.t(lambda e, tt=tt: e.matmul(psf[5][:, 0:M],
                                      L["rowh0"][32:33, 128 * tt:128 * (tt + 1)],
                                      onesr3[32:33, 0:M], start=False, stop=True))
        nt = p.mark("t", "T")
        p.wait("a", "T", nt)
        na = p.inc("a", lambda e: e.activation(ekb[:, 0:M], psf[5][:, 0:M], AF.Exp), "A")
        p.wait("t", "A", na)
        if h == 0 and tt == 0 and L.get("dump", False):
            p.wait("g", "A", na)
            p.dma("g", "dg", L["dbg_eku"], ekb[:, 0:M])
            p.wait("t", "dg")
        p.t(lambda e, tt=tt, hh=hh: e.matmul(
            psf[6][0:65, 0:M], vp[:, 130 * tt + 65 * hh:130 * tt + 65 * hh + 65],
            ekb[:, 0:M], start=(tt == 0), stop=(tt == NTT - 1)))
        nt = p.mark("t", "T")
        p.wait("a", "T", nt)
    p.v(lambda e: e.tensor_scalar_add(junk[:, 0:1], junk[:, 0:1], 1.0))
    p.v(lambda e: e.tensor_max(kacc[:], kacc[:], redt[(NTT - 1) % 2][:]))
    for tt in range(NTT):
        p.t(lambda e, tt=tt, hh=hh: e.matmul(
            psf[2][0:65, 0:1], vp[:, 130 * tt + 65 * hh:130 * tt + 65 * hh + 65],
            onescb[:], start=(tt == 0), stop=(tt == NTT - 1)))
    nt = p.mark("t", "T")
    p.wait("a", "T", nt)
    na = p.inc("a", lambda e, h=h: e.activation(svc[:, h:h + 1], psf[2][0:65, 0:1], AF.Copy), "A")
    na = p.inc("a", lambda e: e.activation(ctxe[:], psf[6][0:65, 0:M], AF.Copy), "A")
    p.wait("g", "A", na)
    p.dma("g", "dg", L["cc_in_ctx"][65 * h:65 * (h + 1), :], ctxe[:])
    p.wait("t", "A", na)
    p.wait("a", "dg")


def emit_attn(p, L):
    x_t, stage, psf, psb = L["x_t"], L["stage"], L["psf"], L["psb"]
    row, cols, junk = L["row"], L["cols"], L["junk"]
    redt = L["redt"]
    Xp = L["Xp"]
    pjT, onesr3, onescf, onescb = L["pjT"], L["onesr3"], L["onescf"], L["onescb"]
    negselt, identb = L["negselt"], L["identb"]
    stab_tm, stabb, stabT, kacc = L["stab_tm"], L["stabb"], L["stabT"], L["kacc"]
    ctxe, ctxbf, ctxb, svc = L["ctxe"], L["ctxbf"], L["ctxb"], L["svc"]
    V = A_views(L)
    qp, kp, vp, ekb, eqs = V["qp"], V["kp"], V["vp"], V["ekb"], V["eqs"]

    ln_stats(p, L)
    p.barrier()
    for i in range(KC):
        p.v(lambda e, i=i: e.tensor_sub(stage[2][:], x_t[i][:], stage[0][:]))
        p.v(lambda e, i=i: e.tensor_mul(Xp(i), stage[2][:], stage[1][:]))
    p.barrier()
    if L.get("dump", False):
        p.wait("g", "V")
        p.dma("g", "dg", L["dbg_lnb"], Xp(0))
        p.wait("s", "dg")
    for k in range(KC):
        att = L["cc_out_att"]
        if k == 0:
            p.wait("s", "CC", L["cc_att"])
        p.dma("s", "di", V["wqa"](k), att[512 * k:512 * k + 128, :])
        p.dma("s", "di", V["wka"](k), att[512 * k + 128:512 * k + 256, :])
        p.dma("s", "di", V["wva"](k), att[512 * k + 256:512 * k + 384, :])
    p.wait("t", "di")
    p.a(lambda e: e.activation(kacc[:], L["onescf"][:, 0:1], AF.Copy, bias=-1e30, scale=1.0))
    p.barrier()

    for pp in range(KC):
        _attn_pair(p, L, V, pp)
    p.barrier()

    p.v(lambda e: e.tensor_copy(L["colw"][:], kacc[:, 0:1].broadcast_to([128, 64])))
    nv = p.mark("v", "V")
    p.wait("g", "V", nv)
    p.inc("g", lambda e: e.tensor_reduce(row(1, 0)[:, 20:21], L["colw"][:, 0:1], AX.C, ALU.max), "G")
    p.dma("g", "dg", L["cc_in_s"][0:1, 0:1], row(1, 0)[:, 20:21])
    p.wait("g", "dg")
    n = p.inc("g", lambda e: e.collective_compute(
        "AllReduce", ALU.max, [[0, 1, 2, 3, 4, 5, 6, 7]],
        [L["cc_in_s"][0:1, 0:1].opt()], [L["cc_out_s"][0:1, 0:1].opt()]), "CC")
    p.dma("g", "dg", L["cc_in_sv"], svc[:, 0:HEADS])
    p.wait("g", "dg")
    n = p.inc("g", lambda e: e.collective_compute(
        "AllReduce", ALU.add, [[0, 1], [2, 3], [4, 5], [6, 7]],
        [L["cc_in_sv"].opt()], [L["cc_out_sv"].opt()]), "CC")
    n = p.inc("g", lambda e: e.collective_compute(
        "AllReduce", ALU.add, [[0, 1], [2, 3], [4, 5], [6, 7]],
        [L["cc_in_ctx"].opt()], [L["cc_out_ctx"].opt()]), "CC")
    p.wait("g", "CC", n)
    p.dma("g", "dg", row(1, 0)[:, 21:22], L["cc_out_s"][0:1, 0:1])
    p.dma("g", "dg", svc[:, 0:HEADS], L["cc_out_sv"])
    p.wait("a", "dg")
    p.wait("g", "dg")
    na = p.inc("a", lambda e: e.activation(row(1, 0)[:, 22:23], row(1, 0)[:, 21:22],
                                           AF.Exp, bias=math.log(KEPS), scale=1.0), "A")
    p.wait("t", "A", na)
    p.t(lambda e: e.matmul(psf[3][0:65, 0:1], onesr3[0:1, 0:65], row(1, 0)[:, 22:23],
                           start=True, stop=True))
    nt = p.mark("t", "T")
    p.wait("a", "T", nt)
    na = p.inc("a", lambda e: e.activation(cols[:, 16:17][0:65, :], psf[3][0:65, 0:1], AF.Copy), "A")
    p.wait("v", "A", na)
    p.v(lambda e: e.tensor_scalar(svc[:, 0:HEADS], svc[:, 0:HEADS], cols[0:65, 16:17], None, ALU.mult))
    nv = p.mark("v", "V")
    p.barrier()

    if L.get("dump", False):
        p.dma("g", "dg", L["dbg_ctx"], L["cc_out_ctx"][0:65, :])
        p.dma("g", "dg", L["dbg_sv"], L["svc"][:, 0:HEADS])
        p.wait("g", "dg")
    for k in range(KC):
        p.dma("s", "di", V["woa"](k), L["cc_out_att"][512 * k + 384:512 * k + 512, :])

    for h in range(HEADS):
        p.dma("s", "di", ctxe[:], L["cc_out_ctx"][65 * h:65 * (h + 1), :])
        p.wait("v", "di")
        p.v(lambda e, h=h: e.scalar_tensor_tensor(
            ctxbf[:], svc[:, h:h + 1].broadcast_to([65, M]), 1.0, ctxe[:], ALU.mult, ALU.add))
        nv = p.mark("v", "V")
        p.wait("t", "V", nv)
        for mc in range(2):
            p.t(lambda e, mc=mc: e.transpose(psb[:, 0:65], ctxbf[0:65, 128 * mc:128 * (mc + 1)], identb[0:65, 0:65]))
            nt = p.mark("t", "T")
            p.wait("a", "T", nt)
            na = p.inc("a", lambda e, mc=mc: e.activation(ctxb[:, 65 * mc:65 * (mc + 1)], psb[:, 0:65], AF.Copy), "A")
            p.wait("t", "A", na)
        for mc in range(2):
            p.t(lambda e, mc=mc: e.matmul(psf[2][0:1, 0:65], onescb[:], ctxb[:, 65 * mc:65 * (mc + 1)],
                                          start=(mc == 0), stop=(mc == 1)))
        nt = p.mark("t", "T")
        p.wait("a", "T", nt)
        na = p.inc("a", lambda e: e.activation(row(1, 0)[:, 192:257], psf[2][0:1, 0:65], AF.Copy), "A")
        p.dma("s", "di", L["rowh0"][0:1, :], L["eps_d"][h:h + 1, :])
        p.dma("s", "di", qp[:], L["eq_d"][h, 0])
        p.dma("s", "di", kp[:], L["eq_d"][h, 1])
        p.wait("t", "di")
        p.wait("t", "A")
        for bl in range(NB):
            eq_ = [qp, kp]
            p.t(lambda e, bl=bl: e.matmul(psf[0][0:65, 0:BL], ctxb[:, 0:65],
                                          qp[:, BL * bl:BL * (bl + 1)], start=True, stop=False))
            p.t(lambda e, bl=bl: e.matmul(psf[0][0:65, 0:BL], ctxb[:, 65:130],
                                          kp[:, BL * bl:BL * (bl + 1)], start=False, stop=False))
            p.t(lambda e, bl=bl: e.matmul(psf[0][0:65, 0:BL], row(1, 0)[:, 192:257],
                                          L["rowh0"][0:1, BL * bl:BL * (bl + 1)],
                                          start=False, stop=True))
            nt = p.mark("t", "T")
            p.wait("v", "T", nt)
            p.v(lambda e, bl=bl: e.reciprocal(L["rowh0"][32:33, BL * bl:BL * (bl + 1)], psf[0][64:65, 0:BL]))
            nv = p.mark("v", "V")
            p.wait("t", "V", nv)
            p.t(lambda e, bl=bl: e.matmul(psf[1][0:64, 0:BL], onesr3[32:33, 0:64],
                                          L["rowh0"][32:33, BL * bl:BL * (bl + 1)],
                                          start=True, stop=True))
            nt = p.mark("t", "T")
            p.wait("a", "T", nt)
            na = p.inc("a", lambda e: e.activation(stage[2][0:64, 0:BL], psf[0][0:64, 0:BL], AF.Copy), "A")
            p.wait("v", "A", na)
            dstc = Xp(h // 2)[64 * (h % 2):64 * (h % 2) + 64, :]
            p.v(lambda e, bl=bl, dstc=dstc: e.tensor_mul(
                dstc[:, BL * bl:BL * (bl + 1)], stage[2][0:64, 0:BL], psf[1][0:64, 0:BL]))
            nv = p.mark("v", "V")
            p.wait("t", "V", nv)
    if L.get("dump", False):
        p.barrier()
        p.dma("s", "di", L["dbg_att"], L["Xp"](0)[0:64, :])
        p.dma("s", "di", L["dbg_inv"], L["rowh0"][32:33, :])
        p.wait("s", "di")
    p.barrier()

    for z in range(KC):
        p.v(lambda e, z=z: e.tensor_scalar(x_t[z][:], x_t[z][:], L["bocol"][:, z:z + 1], None, ALU.add))
    nv = p.mark("v", "V")
    p.wait("t", "V", nv)
    p.wait("s", "di")
    for z in range(KC):
        pa = [psf[b] for b in range(4)]
        for k in range(KC):
            for bl in range(NB):
                p.t(lambda e, z=z, k=k, bl=bl: e.matmul(
                    pa[bl][:, 0:BL], V["woa"](k)[:, 128 * z:128 * (z + 1)],
                    Xp(k)[:, BL * bl:BL * (bl + 1)], start=(k == 0), stop=(k == KC - 1)))
        nt = p.mark("t", "T")
        p.wait("a", "T", nt)
        for bl in range(NB):
            p.a(lambda e, bl=bl: e.activation(stage[2][:, BL * bl:BL * (bl + 1)], pa[bl][:, 0:BL], AF.Copy))
        na = p.mark("a", "A")
        p.wait("v", "A", na)
        p.v(lambda e, z=z: e.tensor_add(x_t[z][:], x_t[z][:], stage[2][:]))
        nv = p.mark("v", "V")
        p.wait("t", "V", nv)
        p.wait("a", "V", nv)
    p.barrier()


_nc_cache = {}


def kernel(L0_attn_x, L0_attn_wq, L0_attn_wk, L0_attn_wv, L0_attn_wo, L0_attn_bo,
           L0_attn_proj, L0_ffn_w_gate, L0_ffn_w_up, L0_ffn_w_down):
    x = np.ascontiguousarray(np.asarray(L0_attn_x, np.float32))
    shards = x.reshape(NCORES, T, D)
    wq = np.ascontiguousarray(L0_attn_wq, np.float32)
    wk = np.ascontiguousarray(L0_attn_wk, np.float32)
    wv = np.ascontiguousarray(L0_attn_wv, np.float32)
    wo = np.ascontiguousarray(L0_attn_wo, np.float32)
    bo = np.ascontiguousarray(np.asarray(L0_attn_bo, np.float32).reshape(D, 1))
    projT = np.ascontiguousarray(np.asarray(L0_attn_proj, np.float32).T)
    wg = np.ascontiguousarray(L0_ffn_w_gate, np.float32)
    wu = np.ascontiguousarray(L0_ffn_w_up, np.float32)
    wd = np.ascontiguousarray(L0_ffn_w_down, np.float32)
    in_maps = []
    for c in range(NCORES):
        in_maps.append({
            "x_in": np.ascontiguousarray(shards[c].T),
            "bo_in": bo, "pjT_in": projT,
            "att4_in": np.ascontiguousarray(np.concatenate(
                [w[128 * c:128 * (c + 1), :] for w in (wq, wk, wv, wo)], axis=0)),
            "wgc_in": np.ascontiguousarray(wg[:, 128 * c:128 * (c + 1)]),
            "wuc_in": np.ascontiguousarray(wu[:, 128 * c:128 * (c + 1)]),
            "wdc_in": np.ascontiguousarray(wd[:, 384 * c:384 * (c + 1)]),
        })
    if "nc" not in _nc_cache:
        _nc_cache["nc"] = build_nc()
    nc = _nc_cache["nc"]
    res = run_bass_kernel_spmd(nc, in_maps, core_ids=list(range(NCORES)))
    _nc_cache["res0"] = res.results[0]
    _nc_cache["res_all"] = res.results
    outs = [np.asarray(res.results[c]["out"]).T for c in range(NCORES)]
    return np.concatenate(outs, axis=0).reshape(4, 4096, D).astype(np.float32)


# version 14
# speedup vs baseline: 1.0395x; 1.0395x over previous
import math
import numpy as np
import ml_dtypes
from contextlib import ExitStack

import concourse.bass as bass
import concourse.mybir as mybir
from concourse.bass_utils import run_bass_kernel_spmd

F32 = mybir.dt.float32
BF16 = mybir.dt.bfloat16
ALU = mybir.AluOpType
AF = mybir.ActivationFunctionType
AX = mybir.AxisListType

NCORES = 8
D = 1024
T = 2048
KC = 8
I3 = 3072
IT = 24
NB = 4
BL = 512
HEADS = 16
DH = 64
M = 256
NTT = 16
MAGIC = 12582912.0
QMAX = 32767.0
EPS_LN = 1e-5
KEPS = 1e-4
NORM = float(DH) ** -0.25

DEBUG_STAGE = 0
DEBUG_DUMPS = False
F4_BARRIER = False
F4_SERIAL_W = False


class P:
    ENGS = ("t", "v", "a", "g", "s")

    def __init__(self, nc):
        self.nc = nc
        self.steps = {e: [] for e in self.ENGS}
        self.has_inc = {e: -1 for e in self.ENGS}
        self.sems = {}
        self.cnt = {}
        self.bar_n = 0

    def sem(self, name):
        h = self.nc.alloc_semaphore(name)
        self.sems[name] = h
        self.cnt[name] = 0
        return h

    def _emit(self, e, fn):
        self.steps[e].append(fn)

    def t(self, fn): self._emit("t", fn)
    def v(self, fn): self._emit("v", fn)
    def a(self, fn): self._emit("a", fn)
    def g(self, fn): self._emit("g", fn)
    def s(self, fn): self._emit("s", fn)

    def inc(self, e, fn, sem, n=1):
        h = self.sems[sem]
        self._emit(e, lambda eng: fn(eng).then_inc(h, n))
        self.has_inc[e] = len(self.steps[e])
        self.cnt[sem] += n
        return self.cnt[sem]

    def mark(self, e, sem):
        if e == "v":
            jk = self.junk_ap
            self._emit("v", lambda eng: eng.tensor_scalar_add(jk, jk, 1.0))
        h = self.sems[sem]
        assert self.steps[e], "mark with no prior instruction"
        if self.has_inc[e] == len(self.steps[e]):
            self._emit(e, lambda eng: eng.nop())
        last = self.steps[e][-1]
        self.steps[e][-1] = (lambda eng, f=last, h=h: f(eng).then_inc(h, 1))
        self.has_inc[e] = len(self.steps[e])
        self.cnt[sem] += 1
        return self.cnt[sem]

    def wait(self, e, sem, val=None):
        h = self.sems[sem]
        v = self.cnt[sem] if val is None else val
        self._emit(e, lambda eng: eng.wait_ge(h, v))

    def dma(self, e, sem, dst, src):
        h = self.sems[sem]
        self._emit(e, lambda eng: eng.dma_start(dst, src).then_inc(h, 16))
        self.cnt[sem] += 16
        return self.cnt[sem]

    def barrier(self, engines="tvags"):
        self.bar_n += 1
        n = self.bar_n
        for e in engines:
            name = "bar_" + e
            if name not in self.sems:
                self.sem(name)
        for e in engines:
            h = self.sems["bar_" + e]
            self._emit(e, lambda eng, h=h: eng.drain().then_inc(h, 1))
            self.cnt["bar_" + e] += 1
        for e in engines:
            for o in engines:
                if o != e:
                    self._emit(e, lambda eng, h=self.sems["bar_" + o], n=n: eng.wait_ge(h, n))


def build_nc():
    nc = bass.Bass("TRN2", target_bir_lowering=False, debug=False)
    L = {}

    def D_(name, shape, dt=F32, kind="Internal", addr_space="Local"):
        L[name] = nc.dram_tensor(name, shape, dt, kind=kind, addr_space=addr_space).ap()
        return L[name]

    D_("x_in", [D, T], kind="ExternalInput")
    D_("att4_in", [4 * 128, D], kind="ExternalInput")
    D_("bo_in", [D, 1], kind="ExternalInput")
    D_("pjT_in", [DH, M], kind="ExternalInput")
    D_("wgc_in", [I3, 128], kind="ExternalInput")
    D_("wuc_in", [I3, 128], kind="ExternalInput")
    D_("wdc_in", [D, 384], kind="ExternalInput")
    D_("out", [D, T], kind="ExternalOutput")
    if DEBUG_DUMPS:
        D_("dbg_m", [128, T], kind="ExternalOutput")
        D_("dbg_r", [128, T], kind="ExternalOutput")
        D_("dbg_xp", [128, T], BF16, kind="ExternalOutput")
        D_("dbg_gate", [128, T], kind="ExternalOutput")
        D_("dbg_p", [128, T], kind="ExternalOutput")
        D_("dbg_q", [128, T], kind="ExternalOutput")
        D_("dbg_cols", [128, 64], kind="ExternalOutput")
        D_("dbg_wg", [128, I3], BF16, kind="ExternalOutput")
        D_("dbg_row", [1, 512], kind="ExternalOutput")
        D_("dbg_nd", [1, 2 * T], kind="ExternalOutput")
        D_("dbg_stab", [16, 128], BF16, kind="ExternalOutput")
        D_("dbg_eku", [128, M], BF16, kind="ExternalOutput")
        D_("dbg_ctx", [65, M], kind="ExternalOutput")
        D_("dbg_sv", [65, HEADS], kind="ExternalOutput")
        D_("dbg_eq", [128, T], BF16, kind="ExternalOutput")
        D_("dbg_inv", [1, T], kind="ExternalOutput")
        D_("dbg_att", [64, T], BF16, kind="ExternalOutput")
        D_("dbg_qp", [128, T], BF16, kind="ExternalOutput")
        D_("dbg_lnb", [128, T], BF16, kind="ExternalOutput")
        D_("dbg_wq", [128, D], BF16, kind="ExternalOutput")
        D_("dbg_gall", [IT * 128, T], kind="ExternalOutput")
        D_("dbg_uall", [IT * 128, T], kind="ExternalOutput")
        D_("dbg_acc", [128, 8], kind="ExternalOutput")
        D_("dbg_s2", [128, T], kind="ExternalOutput")
        D_("dbg_wgall", [NCORES * 128, I3], BF16, kind="ExternalOutput")

    D_("cc_in_att", [512, D], BF16)
    D_("cc_out_att", [NCORES * 512, D], BF16, addr_space="Shared")
    D_("cc_in_g", [128, I3], BF16)
    D_("cc_out_g", [NCORES * 128, I3], BF16, addr_space="Shared")
    D_("cc_in_u", [128, I3], BF16)
    D_("cc_out_u", [NCORES * 128, I3], BF16, addr_space="Shared")
    D_("cc_in_d", [3072, 128], BF16)
    D_("cc_out_d", [NCORES * 3072, 128], BF16, addr_space="Shared")
    D_("gate_d", [IT, 128, T])
    D_("up_d", [IT, 128, T])
    D_("p_d", [IT, 128, T])
    D_("q_d", [IT, 128, T])
    D_("eq_d", [HEADS, 2, 128, T], BF16)
    D_("eps_d", [HEADS, 2048])
    D_("cc_in_s", [1, 64])
    D_("cc_out_s", [1, 64])
    D_("cc_in_ctx", [HEADS * 65, M])
    D_("cc_out_ctx", [HEADS * 65, M])
    D_("cc_in_sv", [65, HEADS])
    D_("cc_out_sv", [65, HEADS])

    L["ident_f"] = nc.inline_tensor(np.eye(128, dtype=np.float32), name="ident_f").ap()
    L["ones_col_f"] = nc.inline_tensor(np.ones((128, 1), np.float32), name="ones_col_f").ap()
    L["ones_row_f"] = nc.inline_tensor(np.ones((1, 128), np.float32), name="ones_row_f").ap()
    negsel_np = np.zeros((16, 16 * 128), np.float32)
    for t_ in range(16):
        negsel_np[t_, t_ * 128:(t_ + 1) * 128] = -1.0
    L["negsel"] = nc.inline_tensor(negsel_np.astype(ml_dtypes.bfloat16), name="negsel").ap()
    CONSTS = [0.0, 1.0, 0.5, MAGIC, 65536.0, EPS_LN, -0.5, 1.0 / 1024.0, math.log(KEPS), -1e30]
    L["consts_in"] = nc.inline_tensor(np.tile(np.array([CONSTS], np.float32), (128, 1)), name="consts").ap()
    L["CONSTS"] = CONSTS

    st = ExitStack()
    with st:
        def sbt(name, shape, dt=F32):
            h = st.enter_context(nc.sbuf_tensor(name, shape, dt))
            L[name] = h
            return h

        wdt1 = sbt("wdt1", [128, 128], BF16)
        wdt2 = sbt("wdt2", [128, 128], BF16)
        x_t = [sbt(f"x{i}", [128, T]) for i in range(KC)]
        L["x_t"] = x_t
        stage = [sbt(f"stg{i}", [128, T]) for i in range(3)]
        L["stage"] = stage
        arena = sbt("arena", [128, 49152], BF16)
        rowh = [sbt("rowh0", [128, T]), sbt("rowh1", [128, 512])]
        L["rowh"] = rowh
        sbt("cols", [128, 64])
        sbt("colw", [128, 64])
        sbt("junk", [128, 8])
        L["redt"] = [sbt(f"red{i}", [128, 1]) for i in range(2)]
        sbt("acc", [128, 1])
        sbt("dbgt", [128, 8])
        sbt("constt", [128, len(CONSTS)])
        sbt("identt", [128, 128])
        sbt("identb", [128, 128], BF16)
        sbt("onescf", [128, 1])
        sbt("onescb", [128, 1], BF16)
        sbt("onesrf", [1, 128])
        sbt("onesr3", [128, 256])
        sbt("negselt", [16, 16 * 128], BF16)
        sbt("pjT", [128, M], BF16)
        sbt("stabT", [16, 128], BF16)
        sbt("stab_tm", [128, 16])
        sbt("stabb", [128, 16], BF16)
        sbt("kacc", [128, 1])
        sbt("ctxbf", [65, M], BF16)
        sbt("ctxe", [65, M])
        sbt("ctxb", [128, 2 * 65], BF16)
        sbt("svc", [65, HEADS + 1])
        L["wdtile"] = [sbt("wdt0", [128, 128], BF16), wdt1, wdt2]
        L["sblk"] = [sbt(f"sblk{i}", [128, 96]) for i in range(3)]
        sbt("bocol", [128, KC])

        for j, val in enumerate(CONSTS):
            nc.const_aps.register(F32, float(val), L["constt"][:, j:j + 1])

        L["Xp"] = lambda i: arena[:, 2048 * i: 2048 * (i + 1)]
        L["Wb"] = lambda k: arena[:, 16384 + 3072 * k: 16384 + 3072 * (k + 1)]
        L["Ob"] = lambda i: arena[:, 2048 * i: 2048 * (i + 1)]
        L["Ab"] = lambda i: arena[:, 16384 + 2048 * i: 16384 + 2048 * (i + 1)]
        L["row"] = lambda i, slot: rowh[i][slot:slot + 1, :]

        L["psf"] = [st.enter_context(nc.psum_tensor(f"ps{i}", [128, 512], F32)) for i in range(7)]
        L["psb"] = st.enter_context(nc.psum_tensor("psb", [128, 1024], BF16))

        L["dump"] = DEBUG_DUMPS and DEBUG_STAGE == 2
        p = P(nc)
        p.junk_ap = L["junk"][:, 7:8]
        for name in ("di", "dg", "T", "V", "A", "G", "CC"):
            p.sem(name)

        nc.all_engine_barrier()
        for h in list(p.sems.values()):
            nc.gpsimd.sem_clear(h)
        nc.all_engine_barrier()

        author(p, L)

        with nc.Block() as block:
            @block.tensor
            def _(e):
                for f in p.steps["t"]:
                    f(e)

            @block.vector
            def _(e):
                for f in p.steps["v"]:
                    f(e)

            @block.scalar
            def _(e):
                for f in p.steps["a"]:
                    f(e)

            @block.gpsimd
            def _(e):
                for f in p.steps["g"]:
                    f(e)

            @block.sync
            def _(e):
                for f in p.steps["s"]:
                    f(e)

        nc.all_engine_barrier()
        for h in list(p.sems.values()):
            nc.gpsimd.sem_clear(h)
        nc.all_engine_barrier()
    return nc


def author(p, L):
    x_t, stage = L["x_t"], L["stage"]
    row = L["row"]

    p.dma("s", "di", L["identt"][:], L["ident_f"])
    p.dma("s", "di", L["constt"][:], L["consts_in"])
    p.dma("s", "di", L["onesrf"][:], L["ones_row_f"])
    for b in (0, 32, 64):
        p.dma("s", "di", L["onesr3"][b:b + 1, 0:128], L["ones_row_f"])
        p.dma("s", "di", L["onesr3"][b:b + 1, 128:256], L["ones_row_f"])
    p.dma("s", "di", L["onescf"][:], L["ones_col_f"])
    p.dma("s", "di", L["negselt"][:], L["negsel"])
    for i in range(KC):
        p.dma("s", "di", x_t[i][:], L["x_in"][128 * i:128 * (i + 1), :])
        p.dma("s", "di", L["bocol"][:, i:i + 1], L["bo_in"][128 * i:128 * (i + 1), :])
    for e in "tvag":
        p.wait(e, "di")
    p.v(lambda e: e.tensor_copy(L["onescb"][:], L["onescf"][:]))
    p.v(lambda e: e.tensor_copy(L["identb"][:], L["identt"][:]))
    p.barrier()

    emit_weight_prep(p, L)

    emit_ffn(p, L)
    if DEBUG_STAGE != 1:
        emit_attn(p, L)
        if DEBUG_STAGE != 2:
            emit_ffn(p, L)

    p.barrier()
    if DEBUG_DUMPS:
        p.dma("s", "di", L["dbg_m"], stage[0][:])
        p.dma("s", "di", L["dbg_r"], stage[1][:])
        p.dma("s", "di", L["dbg_xp"], L["Xp"](0))
        p.dma("s", "di", L["dbg_gate"], L["gate_d"][0])
        p.dma("s", "di", L["dbg_p"], L["p_d"][0])
        p.dma("s", "di", L["dbg_q"], L["q_d"][0])
        p.dma("s", "di", L["dbg_cols"], L["cols"][:])
        p.dma("s", "di", L["dbg_wg"], L["cc_out_g"][0:128, :])
        p.dma("s", "di", L["dbg_row"], L["rowh1"][0:1, :])
        for ot_ in range(IT):
            p.dma("s", "di", L["dbg_gall"][128 * ot_:128 * (ot_ + 1), :], L["gate_d"][ot_])
            p.dma("s", "di", L["dbg_uall"][128 * ot_:128 * (ot_ + 1), :], L["up_d"][ot_])
        p.dma("s", "di", L["dbg_acc"], L["dbgt"][:])
        p.dma("s", "di", L["dbg_s2"], stage[2][:])
        for k_ in range(NCORES):
            p.dma("s", "di", L["dbg_wgall"][128 * k_:128 * (k_ + 1), :], L["cc_out_g"][128 * k_:128 * (k_ + 1), :])
        if DEBUG_STAGE == 2:
            p.dma("s", "di", L["dbg_eq"], L["eq_d"][0, 0])
            p.dma("s", "di", L["dbg_wq"], L["cc_out_att"][0:128, :])
    for i in range(KC):
        p.dma("s", "di", L["out"][128 * i:128 * (i + 1), :], x_t[i][:])
    p.wait("s", "di")


def emit_weight_prep(p, L):
    stage, sblk = L["stage"], L["sblk"]
    identb, psb = L["identb"], L["psb"]
    Ab, Wb = L["Ab"], L["Wb"]
    junk = L["junk"]

    for _ in range(200):
        p.t(lambda e: e.matmul(L["psf"][0][:, 0:BL], L["identt"][:],
                               L["x_t"][0][:, 0:BL], start=True, stop=True))

    units = [("att", r) for r in range(4)]
    units += [("g", ot) for ot in range(IT)]
    units += [("u", ot) for ot in range(IT)]
    units += [("d", j, h) for j in range(3) for h in range(KC)]

    def unit_src(u):
        if u[0] == "att":
            return L["att4_in"][128 * u[1]:128 * (u[1] + 1), :], D
        if u[0] == "g":
            return L["wgc_in"][128 * u[1]:128 * (u[1] + 1), :], 128
        if u[0] == "u":
            return L["wuc_in"][128 * u[1]:128 * (u[1] + 1), :], 128
        _, j, h = u
        return L["wdc_in"][128 * h:128 * (h + 1), 128 * j:128 * (j + 1)], 128

    def unit_dst(u):
        if u[0] == "att":
            return L["cc_in_att"][128 * u[1]:128 * (u[1] + 1), :]
        if u[0] == "g":
            return L["cc_in_g"][:, 128 * u[1]:128 * (u[1] + 1)]
        if u[0] == "u":
            return L["cc_in_u"][:, 128 * u[1]:128 * (u[1] + 1)]
        _, j, h = u
        return L["cc_in_d"][1024 * j + 128 * h:1024 * j + 128 * (h + 1), :]

    di_at = {}
    gdone = {}
    arena = L["arena"]
    dqs = []

    def slot_ap(d):
        return arena[:, 16384 + 128 * d:16384 + 128 * (d + 1)]

    batches = [("att", r) for r in range(4)]
    batches += [("wg", b) for b in range(6)] + [("wu", b) for b in range(6)]
    batches += [("wd", j, hb) for j in range(3) for hb in range(2)]

    wg3 = L["wgc_in"].rearrange("(t p) c -> p t c", p=128)
    wu3 = L["wuc_in"].rearrange("(t p) c -> p t c", p=128)
    wd3 = L["wdc_in"].rearrange("(t p) c -> p t c", p=128)

    def batch_src(u):
        if u[0] == "att":
            return L["att4_in"][128 * u[1]:128 * (u[1] + 1), :], D
        if u[0] == "wg":
            return wg3[:, 4 * u[1]:4 * (u[1] + 1), :], 512
        if u[0] == "wu":
            return wu3[:, 4 * u[1]:4 * (u[1] + 1), :], 512
        _, j, hb = u
        return wd3[:, 4 * hb:4 * (hb + 1), 128 * j:128 * (j + 1)], 512

    def batch_tiles(u):
        if u[0] == "wg":
            return [("g", 4 * u[1] + t) for t in range(4)]
        if u[0] == "wu":
            return [("u", 4 * u[1] + t) for t in range(4)]
        _, j, hb = u
        return [("d", j, 4 * hb + t) for t in range(4)]

    def in_slot(n, w):
        return stage[(n % 4) // 2][:, 1024 * (n % 2):1024 * (n % 2) + w]

    def do_load(n):
        u = batches[n]
        src_, w = batch_src(u)
        dst = in_slot(n, w)
        if u[0] != "att":
            dst = dst.rearrange("p (t c) -> p t c", c=128)
        di_at[n] = p.dma("s", "di", dst, src_)

    for n in range(4):
        do_load(n)
    cast_hist = []
    for n, u in enumerate(batches):
        _, w = batch_src(u)
        wbuf = in_slot(n, w)
        p.wait("v", "di", di_at[n])
        if u[0] == "att":
            btile = arena[:, 25600 + 1024 * (n % 2):25600 + 1024 * (n % 2) + 1024]
            if n >= 2 and batches[n - 2][0] == "att":
                p.wait("v", "dg", gdone[batches[n - 2]])
            scale = NORM if u[1] < 2 else 1.0
            if scale != 1.0:
                p.v(lambda e, b=btile, wb=wbuf, s_=scale: e.tensor_scalar_mul(b, wb, s_))
            else:
                p.v(lambda e, b=btile, wb=wbuf: e.tensor_copy(b, wb))
            nv = p.mark("v", "V")
            p.wait("g", "V", nv)
            gdone[u] = p.dma("g", "dg", unit_dst(u), btile)
        else:
            d0 = len(dqs)
            btile = arena[:, 16384 + 128 * d0:16384 + 128 * d0 + 512]
            wrow = wbuf
            wv3 = wrow.rearrange("p (b c) -> p b c", c=32)
            nblk = w // 32
            s1 = sblk[0][:, 0:nblk]
            s2 = sblk[1][:, 0:nblk]
            s3 = sblk[2][:, 0:nblk]
            p.v(lambda e, wv3=wv3, s1=s1: e.tensor_reduce(s1, wv3, AX.X, ALU.max, apply_absolute_value=True))
            p.v(lambda e: e.tensor_scalar_add(junk[:, 0:1], junk[:, 0:1], 1.0))
            p.v(lambda e, s1=s1, s2=s2: e.tensor_scalar(s2, s1, 1.0 / 7.0, 1e-12, ALU.mult, ALU.max))
            p.v(lambda e: e.tensor_scalar_add(junk[:, 0:1], junk[:, 0:1], 1.0))
            p.v(lambda e, s2=s2, s3=s3: e.reciprocal(s3, s2))
            p.v(lambda e: e.tensor_scalar_add(junk[:, 0:1], junk[:, 0:1], 1.0))
            s3b = s3.unsqueeze(2).broadcast_to([128, nblk, 32])
            s2b = s2.unsqueeze(2).broadcast_to([128, nblk, 32])
            p.v(lambda e, wv3=wv3, s3b=s3b: e.tensor_tensor(wv3, wv3, s3b, ALU.mult))
            p.v(lambda e, w2=wrow: e.tensor_single_scalar(w2, w2, MAGIC, ALU.add))
            p.v(lambda e, w2=wrow: e.tensor_single_scalar(w2, w2, MAGIC, ALU.subtract))
            f32q = stage[2][:, 512 * (n % 4):512 * (n % 4) + 512]
            if len(cast_hist) >= 4:
                p.wait("v", "A", cast_hist[-4])
            p.v(lambda e, wv3=wv3, s2b=s2b, f3=f32q.rearrange("p (b c) -> p b c", c=32):
                e.tensor_tensor(f3, wv3, s2b, ALU.mult))
            nv = p.mark("v", "V")
            p.wait("a", "V", nv)
            p.inc("a", lambda e, b=btile, f=f32q: e.activation(b, f, AF.Copy), "A")
            cast_hist.append(p.cnt["A"])
            dqs.extend(batch_tiles(u))
        if n + 4 < len(batches):
            p.wait("s", "V")
            do_load(n + 4)
    p.barrier()

    evA = 0
    abs_ = [arena[:, 27648:27776], arena[:, 27776:27904]]
    gd2 = {}
    for d, m in enumerate(dqs):
        if evA:
            p.wait("t", "A", evA)
        p.t(lambda e, b=slot_ap(d): e.transpose(psb[:, 0:128], b, identb[:]))
        nt = p.mark("t", "T")
        p.wait("a", "T", nt)
        if d >= 2:
            p.wait("a", "dg", gd2[d - 2])
        p.inc("a", lambda e, ab=abs_[d % 2]: e.activation(ab, psb[:, 0:128], AF.Copy), "A")
        evA = p.cnt["A"]
        p.wait("g", "A", evA)
        gd2[d] = p.dma("g", "dg", unit_dst(m), abs_[d % 2])
        p.wait("g", "dg")

    p.barrier()
    p.dma("s", "di", stage[0][0:DH, 0:M], L["pjT_in"])
    p.wait("v", "di")
    p.v(lambda e: e.tensor_copy(L["pjT"][0:DH, :], stage[0][0:DH, 0:M]))
    p.v(lambda e: e.tensor_copy(L["pjT"][64:64 + DH, :], stage[0][0:DH, 0:M]))
    nv = p.mark("v", "V")
    p.wait("s", "V", nv)
    p.barrier()

    for (ci, co, key) in [("cc_in_g", "cc_out_g", "cc_g"), ("cc_in_u", "cc_out_u", "cc_u")]:
        p.inc("g", lambda e, ci=ci, co=co: e.collective_compute(
            "AllGather", ALU.bypass, [[0, 1, 2, 3, 4, 5, 6, 7]],
            [L[ci][:].opt()], [L[co][:].opt()]), "CC")
        L[key] = p.cnt["CC"]


def ln_stats(p, L):
    x_t, stage, psf = L["x_t"], L["stage"], L["psf"]
    row = L["row"]
    onescf = L["onescf"]
    for bl in range(NB):
        for k in range(KC):
            p.t(lambda e, k=k, bl=bl: e.matmul(psf[4][0:1, :], onescf[:],
                                               x_t[k][:, BL * bl:BL * (bl + 1)],
                                               start=(k == 0), stop=(k == KC - 1)))
        for k in range(KC):
            na = p.inc("a", lambda e, k=k, bl=bl: e.activation(
                stage[2][:, 0:BL], x_t[k][:, BL * bl:BL * (bl + 1)], AF.Square), "A")
            p.wait("t", "A", na)
            p.t(lambda e, k=k: e.matmul(psf[5][0:1, :], onescf[:], stage[2][:, 0:BL],
                                        start=(k == 0), stop=(k == KC - 1)))
            if k < KC - 1:
                nt = p.mark("t", "T")
                p.wait("a", "T", nt)
        nt = p.mark("t", "T")
        p.wait("v", "T", nt)
        p.wait("a", "T", nt)
        mrow = row(0, 0)[:, 0:BL]
        erow = row(0, 0)[:, BL:2 * BL]
        tmp = row(0, 0)[:, 2 * BL:3 * BL]
        p.v(lambda e, mr=mrow: e.tensor_scalar_mul(mr, psf[4][0:1, :], 1.0 / 1024.0))
        p.v(lambda e, rr=erow: e.tensor_scalar_mul(rr, psf[5][0:1, :], 1.0 / 1024.0))
        p.v(lambda e, mr=mrow, tm=tmp: e.tensor_mul(tm, mr, mr))
        p.v(lambda e, rr=erow, tm=tmp: e.tensor_sub(rr, rr, tm))
        nv = p.mark("v", "V")
        p.wait("a", "V", nv)
        p.a(lambda e, rr=erow: e.activation(rr, rr, AF.Sqrt, bias=EPS_LN, scale=1.0))
        na = p.mark("a", "A")
        p.wait("v", "A", na)
        p.v(lambda e, rr=erow: e.reciprocal(rr, rr))
        nv = p.mark("v", "V")
        p.wait("t", "V", nv)
        for (r, d, ps) in [(mrow, stage[0], psf[4]), (erow, stage[1], psf[5])]:
            p.t(lambda e, r=r, ps=ps: e.matmul(ps[:, 0:BL], L["onesr3"][0:1, 0:128], r, start=True, stop=True))
            nt = p.mark("t", "T")
            p.wait("a", "T", nt)
            na = p.inc("a", lambda e, ps=ps, d=d, bl=bl: e.activation(
                d[:, BL * bl:BL * (bl + 1)], ps[:, 0:BL], AF.Copy), "A")
            p.wait("t", "A", na)
            p.wait("v", "A", na)
    p.mark("a", "A")


def scalars_to_cols(p, L, src_row, ncol, dst_col0):
    psf, onesrf, cols = L["psf"], L["onesrf"], L["cols"]
    nv = p.mark("v", "V")
    ng = p.mark("g", "G")
    p.wait("t", "V", nv)
    p.wait("t", "G", ng)
    b0 = src_row.base_partition()
    p.t(lambda e, b0=b0: e.matmul(psf[6][:, 0:ncol], L["onesr3"][b0:b0 + 1, 0:128], src_row, start=True, stop=True))
    nt = p.mark("t", "T")
    p.wait("a", "T", nt)
    na = p.inc("a", lambda e: e.activation(cols[:, dst_col0:dst_col0 + ncol], psf[6][:, 0:ncol], AF.Copy), "A")
    p.wait("v", "A", na)
    p.wait("g", "A", na)
    p.wait("t", "A", na)


class MaxChain:

    def __init__(self, p, L):
        self.p, self.L = p, L
        self.n = 0

    def add(self, tile_ap):
        p, L, n = self.p, self.L, self.n
        redt = L["redt"]
        p.v(lambda e, t=tile_ap, r=redt[n % 2]: e.tensor_reduce(
            r[:], t, AX.X, ALU.max, apply_absolute_value=True))
        if n == 1:
            p.v(lambda e: e.tensor_copy(L["acc"][:], redt[0][:]))
        elif n > 1:
            p.v(lambda e, r=redt[(n - 1) % 2]: e.tensor_max(L["acc"][:], L["acc"][:], r[:]))
        self.n += 1

    def finish(self):
        p, L, n = self.p, self.L, self.n
        redt, junk = L["redt"], L["junk"]
        p.v(lambda e: e.tensor_scalar_add(junk[:, 0:1], junk[:, 0:1], 1.0))
        if n == 1:
            p.v(lambda e: e.tensor_copy(L["acc"][:], redt[0][:]))
        else:
            p.v(lambda e, r=redt[(n - 1) % 2]: e.tensor_max(L["acc"][:], L["acc"][:], r[:]))
        p.v(lambda e: e.tensor_scalar_add(junk[:, 1:2], junk[:, 1:2], 1.0))


def export_colmax(p, L, dst_rowcol):
    colw, acc = L["colw"], L["acc"]
    p.v(lambda e: e.tensor_copy(colw[:], acc[:, 0:1].broadcast_to([128, 64])))
    n = p.mark("v", "V")
    p.wait("g", "V", n)
    return p.inc("g", lambda e: e.tensor_reduce(dst_rowcol, colw[:, 0:1], AX.C, ALU.max), "G")


def allreduce_row(p, L, row_ap, ncols, op=ALU.max):
    cc_in, cc_out = L["cc_in_s"], L["cc_out_s"]
    p.dma("g", "dg", cc_in[0:1, 0:ncols], row_ap)
    p.wait("g", "dg")
    n = p.inc("g", lambda e: e.collective_compute(
        "AllReduce", op, [[0, 1, 2, 3, 4, 5, 6, 7]],
        [cc_in[0:1, 0:ncols].opt()], [cc_out[0:1, 0:ncols].opt()]), "CC")
    p.wait("g", "CC", n)
    p.dma("g", "dg", row_ap, cc_out[0:1, 0:ncols])
    p.wait("v", "dg")


def emit_ffn(p, L):
    x_t, stage, psf = L["x_t"], L["stage"], L["psf"]
    row, cols, junk = L["row"], L["cols"], L["junk"]
    redt = L["redt"]
    Xp, Wb, Ob = L["Xp"], L["Wb"], L["Ob"]
    wdtile = L["wdtile"]

    ln_stats(p, L)
    p.barrier()
    mc = MaxChain(p, L)
    for i in range(KC):
        p.v(lambda e, i=i: e.tensor_sub(stage[2][:], x_t[i][:], stage[0][:]))
        p.v(lambda e: e.tensor_mul(stage[2][:], stage[2][:], stage[1][:]))
        mc.add(stage[2][:])
    mc.finish()
    export_colmax(p, L, row(1, 0)[:, 0:1])
    p.wait("g", "G")
    allreduce_row(p, L, row(1, 0)[:, 0:1], 1, ALU.max)
    if "cc_d" not in L:
        for (ci, co, key) in [("cc_in_d", "cc_out_d", "cc_d"), ("cc_in_att", "cc_out_att", "cc_att")]:
            p.inc("g", lambda e, ci=ci, co=co: e.collective_compute(
                "AllGather", ALU.bypass, [[0, 1, 2, 3, 4, 5, 6, 7]],
                [L[ci][:].opt()], [L[co][:].opt()]), "CC")
            L[key] = p.cnt["CC"]
    sx = row(1, 0)[:, 1:2]
    wideA = row(1, 0)[:, 64:128]
    p.wait("g", "dg")
    p.g(lambda e: e.tensor_scalar_mul(sx, row(1, 0)[:, 0:1], 1.0 / QMAX))
    p.g(lambda e: e.tensor_scalar_max(sx, sx, 1e-12))
    ng = p.mark("g", "G")
    p.wait("v", "G", ng)
    p.v(lambda e: e.reciprocal(wideA, sx.broadcast_to([1, 64])))
    nv = p.mark("v", "V")
    p.wait("g", "V", nv)
    p.g(lambda e: e.tensor_copy(row(1, 0)[:, 2:3], wideA[:, 0:1]))
    scalars_to_cols(p, L, row(1, 0)[:, 1:3], 2, 0)
    for i in range(KC):
        p.v(lambda e, i=i: e.tensor_sub(stage[2][:], x_t[i][:], stage[0][:]))
        p.v(lambda e: e.tensor_mul(stage[2][:], stage[2][:], stage[1][:]))
        p.v(lambda e: e.tensor_scalar(stage[2][:], stage[2][:], cols[:, 1:2], MAGIC, ALU.mult, ALU.add))
        p.v(lambda e, i=i: e.tensor_single_scalar(Xp(i), stage[2][:], MAGIC, ALU.subtract))
    p.barrier()

    NBANK = 6
    for phase, (wT_src, spill) in enumerate([(L["cc_out_g"], L["gate_d"]), (L["cc_out_u"], L["up_d"])]):
        p.wait("s", "CC", L["cc_g"] if phase == 0 else L["cc_u"])
        for k in range(KC):
            p.dma("s", "di", Wb(k)[:, 0:I3], wT_src[128 * k:128 * (k + 1), :])
        p.wait("t", "di")
        mc = MaxChain(p, L)
        tA = [0] * NBANK
        dg_at = {}
        v_at = {}
        idx = 0
        for ot in range(IT):
            stg = stage[ot % 3]
            for bl in range(NB):
                b = idx % NBANK
                idx += 1
                if tA[b]:
                    p.wait("t", "A", tA[b])
                for k in range(KC):
                    p.t(lambda e, k=k, bl=bl, ot=ot, b=b: e.matmul(
                        psf[b][:, 0:BL], Wb(k)[:, 128 * ot:128 * (ot + 1)],
                        Xp(k)[:, BL * bl:BL * (bl + 1)],
                        start=(k == 0), stop=(k == KC - 1)))
                nt = p.mark("t", "T")
                p.wait("a", "T", nt)
                if ot >= 3 and bl == 0:
                    p.wait("a", "dg", dg_at[ot - 3])
                    p.wait("a", "V", v_at[ot - 3])
                p.a(lambda e, b=b, bl=bl, stg=stg: e.activation(
                    stg[:, BL * bl:BL * (bl + 1)], psf[b][:, 0:BL], AF.Copy, bias=0.0, scale=cols[:, 0:1]))
                tA[b] = p.mark("a", "A")
            p.wait("v", "A")
            mc.add(stg[:])
            v_at[ot] = p.mark("v", "V")
            p.wait("g", "A")
            dg_at[ot] = p.dma("g", "dg", spill[ot], stg[:])
        mc.finish()
        export_colmax(p, L, row(1, 0)[:, 3 + phase:4 + phase])
        p.wait("g", "G")
        if DEBUG_DUMPS:
            p.g(lambda e, ph=phase: e.tensor_copy(L["dbgt"][:, ph:ph + 1], L["acc"][:, 0:1]))
            p.g(lambda e, ph=phase: e.tensor_copy(L["dbgt"][0:1, 2 + ph:3 + ph], row(1, 0)[:, 3 + ph:4 + ph]))
        p.barrier()

    allreduce_row(p, L, row(1, 0)[:, 3:5], 2, ALU.max)
    sg = row(1, 0)[:, 5:6]
    su = row(1, 0)[:, 6:7]
    wideA = row(1, 0)[:, 64:128]
    wideB = row(1, 0)[:, 128:192]
    p.wait("g", "dg")
    p.g(lambda e: e.tensor_scalar_mul(sg, row(1, 0)[:, 3:4], 1.0 / QMAX))
    p.g(lambda e: e.tensor_scalar_max(sg, sg, 1e-12))
    p.g(lambda e: e.tensor_scalar_mul(su, row(1, 0)[:, 4:5], 1.0 / QMAX))
    p.g(lambda e: e.tensor_scalar_max(su, su, 1e-12))
    ng = p.mark("g", "G")
    p.wait("v", "G", ng)
    p.v(lambda e: e.reciprocal(wideA, sg.broadcast_to([1, 64])))
    p.v(lambda e: e.reciprocal(wideB, su.broadcast_to([1, 64])))
    nv = p.mark("v", "V")
    p.wait("g", "V", nv)
    p.g(lambda e: e.tensor_copy(row(1, 0)[:, 7:8], wideA[:, 0:1]))
    p.g(lambda e: e.tensor_copy(row(1, 0)[:, 8:9], wideB[:, 0:1]))
    scalars_to_cols(p, L, row(1, 0)[:, 5:9], 4, 2)
    p.barrier()

    HB = T // 2
    mc = MaxChain(p, L)
    halves = [(ot, hf) for ot in range(IT) for hf in range(2)]
    di_at = {}
    dg_at = {}
    for j, (ot, hf) in enumerate(halves):
        if j < 2:
            di_at[j] = p.dma("s", "di", stage[j % 2][:, HB * (j % 2):HB * (j % 2) + HB]
                             if False else stage[j % 2][:, 0:HB], L["gate_d"][ot, :, HB * hf:HB * (hf + 1)])
    for j, (ot, hf) in enumerate(halves):
        bufi = stage[j % 2][:, 0:HB]
        tmp = stage[2][:, HB * (j % 2):HB * (j % 2) + HB]
        p.wait("v", "di", di_at[j])
        p.wait("a", "di", di_at[j])
        p.v(lambda e, bufi=bufi, tmp=tmp: e.tensor_scalar(tmp, bufi, cols[:, 4:5], MAGIC, ALU.mult, ALU.add))
        p.v(lambda e, tmp=tmp: e.tensor_single_scalar(tmp, tmp, MAGIC, ALU.subtract))
        nv = p.mark("v", "V")
        p.wait("a", "V", nv)
        p.a(lambda e, bufi=bufi, tmp=tmp: e.activation(bufi, tmp, AF.Sigmoid, bias=0.0, scale=cols[:, 2:3]))
        p.a(lambda e, bufi=bufi: e.activation(bufi, bufi, AF.Copy, bias=MAGIC, scale=65536.0))
        na = p.mark("a", "A")
        p.wait("v", "A", na)
        p.v(lambda e, bufi=bufi: e.tensor_single_scalar(bufi, bufi, MAGIC, ALU.subtract))
        p.v(lambda e, bufi=bufi, tmp=tmp: e.scalar_tensor_tensor(bufi, bufi, 65535.0, tmp, ALU.min, ALU.mult))
        mc.add(bufi)
        nv = p.mark("v", "V")
        p.wait("g", "V", nv)
        dg_at[j] = p.dma("g", "dg", L["p_d"][ot, :, HB * hf:HB * (hf + 1)], bufi)
        if j + 2 < len(halves):
            ot2, hf2 = halves[j + 2]
            p.wait("s", "dg", dg_at[j])
            di_at[j + 2] = p.dma("s", "di", stage[(j + 2) % 2][:, 0:HB] if True else None,
                                 L["gate_d"][ot2, :, HB * hf2:HB * (hf2 + 1)])
    mc.finish()
    export_colmax(p, L, row(1, 0)[:, 9:10])
    p.wait("g", "G")
    allreduce_row(p, L, row(1, 0)[:, 9:10], 1, ALU.max)
    sg2 = row(1, 0)[:, 10:11]
    c1 = row(1, 0)[:, 11:12]
    wideA = row(1, 0)[:, 64:128]
    p.wait("g", "dg")
    p.g(lambda e: e.tensor_mul(sg2, row(1, 0)[:, 9:10], sg))
    p.g(lambda e: e.tensor_scalar_mul(sg2, sg2, 1.0 / (65536.0 * QMAX)))
    p.g(lambda e: e.tensor_scalar_max(sg2, sg2, 1e-12))
    ng = p.mark("g", "G")
    p.wait("v", "G", ng)
    p.v(lambda e: e.reciprocal(wideA, sg2.broadcast_to([1, 64])))
    nv = p.mark("v", "V")
    p.wait("g", "V", nv)
    p.g(lambda e: e.tensor_mul(c1, wideA[:, 0:1], sg))
    p.g(lambda e: e.tensor_scalar_mul(c1, c1, 1.0 / 65536.0))
    scalars_to_cols(p, L, row(1, 0)[:, 10:12], 2, 6)
    p.barrier()

    mc = MaxChain(p, L)
    di_at = {}
    dg_at = {}
    QH = T // 2
    for j, (ot, hf) in enumerate(halves):
        if j < 1:
            p.dma("s", "di", stage[0][:, QH * (j % 2):QH * (j % 2) + QH], L["up_d"][ot, :, QH * hf:QH * (hf + 1)])
            di_at[j] = p.dma("s", "di", stage[1][:, QH * (j % 2):QH * (j % 2) + QH], L["p_d"][ot, :, QH * hf:QH * (hf + 1)])
    for j, (ot, hf) in enumerate(halves):
        par = j % 2
        bup = stage[0][:, QH * par:QH * par + QH]
        bp = stage[1][:, QH * par:QH * par + QH]
        tmp = stage[2][:, QH * par:QH * par + QH]
        p.wait("a", "di", di_at[j])
        p.wait("v", "di", di_at[j])
        p.a(lambda e, bup=bup, tmp=tmp: e.activation(tmp, bup, AF.Copy, bias=MAGIC, scale=cols[:, 5:6]))
        na = p.mark("a", "A")
        p.wait("v", "A", na)
        p.v(lambda e, tmp=tmp: e.tensor_single_scalar(tmp, tmp, MAGIC, ALU.subtract))
        p.v(lambda e, bp=bp: e.tensor_scalar(bp, bp, cols[:, 7:8], MAGIC, ALU.mult, ALU.add))
        p.v(lambda e, bp=bp: e.tensor_single_scalar(bp, bp, MAGIC, ALU.subtract))
        p.v(lambda e, bup=bup, bp=bp, tmp=tmp: e.tensor_mul(bup, tmp, bp))
        mc.add(bup)
        nv = p.mark("v", "V")
        p.wait("g", "V", nv)
        dg_at[j] = p.dma("g", "dg", L["q_d"][ot, :, QH * hf:QH * (hf + 1)], bup)
        if j + 1 < len(halves):
            ot2, hf2 = halves[j + 1]
            par2 = (j + 1) % 2
            p.wait("s", "dg", dg_at[j]) if False else None
            if j >= 1:
                p.wait("s", "dg", dg_at[j - 1])
            p.dma("s", "di", stage[0][:, QH * par2:QH * par2 + QH], L["up_d"][ot2, :, QH * hf2:QH * (hf2 + 1)])
            di_at[j + 1] = p.dma("s", "di", stage[1][:, QH * par2:QH * par2 + QH], L["p_d"][ot2, :, QH * hf2:QH * (hf2 + 1)])
    mc.finish()
    export_colmax(p, L, row(1, 0)[:, 12:13])
    p.wait("g", "G")
    allreduce_row(p, L, row(1, 0)[:, 12:13], 1, ALU.max)
    c2 = row(1, 0)[:, 13:14]
    so = row(1, 0)[:, 14:15]
    c3 = row(1, 0)[:, 15:16]
    halfso = row(1, 0)[:, 16:17]
    wideA = row(1, 0)[:, 64:128]
    p.wait("g", "dg")
    p.g(lambda e: e.tensor_mul(c2, su, sg2))
    p.g(lambda e: e.tensor_mul(so, row(1, 0)[:, 12:13], c2))
    p.g(lambda e: e.tensor_scalar_mul(so, so, 1.0 / QMAX))
    p.g(lambda e: e.tensor_scalar_max(so, so, 1e-12))
    p.g(lambda e: e.tensor_scalar_mul(halfso, so, 0.5))
    ng = p.mark("g", "G")
    p.wait("v", "G", ng)
    p.v(lambda e: e.reciprocal(wideA, so.broadcast_to([1, 64])))
    nv = p.mark("v", "V")
    p.wait("g", "V", nv)
    p.g(lambda e: e.tensor_mul(c3, wideA[:, 0:1], c2))
    scalars_to_cols(p, L, row(1, 0)[:, 15:17], 2, 8)
    p.barrier()

    di_at = {}
    for j, (ot, hf) in enumerate(halves):
        par = j % 2
        if j < 2:
            di_at[j] = p.dma("s", "di", stage[par][:, 0:QH], L["q_d"][ot, :, QH * hf:QH * (hf + 1)])
    for j, (ot, hf) in enumerate(halves):
        par = j % 2
        bq = stage[par][:, 0:QH]
        tmp = stage[2][:, QH * par:QH * par + QH]
        p.wait("a", "di", di_at[j])
        p.a(lambda e, bq=bq, tmp=tmp: e.activation(tmp, bq, AF.Copy, bias=MAGIC, scale=cols[:, 8:9]))
        na = p.mark("a", "A")
        if j + 2 < len(halves):
            ot2, hf2 = halves[j + 2]
            p.wait("s", "A", na)
            di_at[j + 2] = p.dma("s", "di", stage[(j + 2) % 2][:, 0:QH], L["q_d"][ot2, :, QH * hf2:QH * (hf2 + 1)])
        p.wait("v", "A", na)
        p.v(lambda e, ot=ot, hf=hf, tmp=tmp: e.tensor_single_scalar(
            Ob(ot)[:, QH * hf:QH * (hf + 1)], tmp, MAGIC, ALU.subtract))
    nv = p.mark("v", "V")
    p.wait("t", "V", nv)
    p.wait("s", "V", nv)

    tT = {}
    tA4 = [0, 0, 0, 0]
    v_at = {}
    groups = [(h, i) for h in range(KC) for i in range(IT)]
    p.wait("s", "CC", L["cc_d"])
    wd_src = lambda i, h: L["cc_out_d"][3072 * (i // 3) + 1024 * (i % 3) + 128 * h:
                                        3072 * (i // 3) + 1024 * (i % 3) + 128 * h + 128, :]
    if not F4_SERIAL_W:
        for g in range(2):
            h2, i2 = groups[g]
            p.dma("s", "di", wdtile[g % 3][:], wd_src(i2, h2))
    for g, (h, i) in enumerate(groups):
        stg = stage[h % 3]
        if True:
            wt = wdtile[g % 3]
            if F4_SERIAL_W:
                if g >= 1:
                    p.wait("s", "T", tT[groups[g - 1]])
                p.dma("s", "di", wt[:], wd_src(i, h))
            p.wait("t", "di")
            for bl in range(NB):
                if i == 0 and tA4[bl]:
                    p.wait("t", "A", tA4[bl])
                p.t(lambda e, i=i, bl=bl, wt=wt: e.matmul(
                    psf[bl][:, 0:BL], wt[:], Ob(i)[:, BL * bl:BL * (bl + 1)],
                    start=(i == 0), stop=(i == IT - 1)))
            nt = p.mark("t", "T")
            tT[(h, i)] = nt
            if g + 2 < len(groups) and not F4_SERIAL_W:
                h2, i2 = groups[g + 2]
                if g >= 1:
                    p.wait("s", "T", tT[groups[g - 1]])
                p.dma("s", "di", wdtile[(g + 2) % 3][:], wd_src(i2, h2))
        if i == IT - 1:
            p.wait("a", "T", tT[(h, IT - 1)])
            if h >= 3:
                p.wait("a", "V", v_at[h - 3])
            for bl in range(NB):
                p.a(lambda e, bl=bl, stg=stg: e.activation(stg[:, BL * bl:BL * (bl + 1)],
                                                           psf[bl][:, 0:BL], AF.Copy, bias=0.0, scale=cols[:, 9:10]))
                tA4[bl] = p.mark("a", "A")
            p.wait("v", "A", tA4[3])
            p.v(lambda e, h=h, stg=stg: e.tensor_add(x_t[h][:], x_t[h][:], stg[:]))
            v_at[h] = p.mark("v", "V")
            if F4_BARRIER:
                p.barrier()
    p.wait("t", "V")
    p.wait("a", "V")
    p.barrier()


def A_views(L):
    a = L["arena"]
    V = {}
    V["wqa"] = lambda k: a[:, 16384 + 1024 * k: 16384 + 1024 * (k + 1)]
    V["wka"] = lambda k: a[:, 24576 + 1024 * k: 24576 + 1024 * (k + 1)]
    V["wva"] = lambda k: a[:, 32768 + 1024 * k: 32768 + 1024 * (k + 1)]
    V["woa"] = lambda k: a[:, 16384 + 1024 * k: 16384 + 1024 * (k + 1)]
    V["qp"] = a[:, 40960:43008]
    V["kp"] = a[:, 43008:45056]
    V["vp"] = a[:, 45056:47136]
    V["ekb"] = a[:, 47136:47392]
    V["eqs"] = [a[:, 47392 + 512 * i: 47392 + 512 * (i + 1)] for i in range(2)]
    return V


def _attn_pair(p, L, V, pp):
    stage, psf, psb = L["stage"], L["psf"], L["psb"]
    junk, redt = L["junk"], L["redt"]
    Xp = L["Xp"]
    pjT, onesr3, onescf, onescb = L["pjT"], L["onesr3"], L["onescf"], L["onescb"]
    negselt, identb = L["negselt"], L["identb"]
    stab_tm, stabb, stabT, kacc = L["stab_tm"], L["stabb"], L["stabT"], L["kacc"]
    ctxe, svc = L["ctxe"], L["svc"]
    qp, kp, vp, ekb, eqs = V["qp"], V["kp"], V["vp"], V["ekb"], V["eqs"]
    NTT_ = NTT
    for (wsel, dst) in [("wqa", qp), ("wka", kp)]:
        pa = [psf[b] for b in range(4)]
        for k in range(KC):
            for bl in range(NB):
                p.t(lambda e, k=k, bl=bl, wsel=wsel, pp=pp, pa=pa: e.matmul(
                    pa[bl][:, 0:BL], V[wsel](k)[:, 128 * pp:128 * (pp + 1)],
                    Xp(k)[:, BL * bl:BL * (bl + 1)],
                    start=(k == 0), stop=(k == KC - 1)))
        nt = p.mark("t", "T")
        p.wait("a", "T", nt)
        for bl in range(NB):
            p.a(lambda e, bl=bl, dst=dst, pa=pa: e.activation(
                dst[:, BL * bl:BL * (bl + 1)], pa[bl][:, 0:BL], AF.Copy))
        na = p.mark("a", "A")
        p.wait("t", "A", na)
    for tt in range(NTT_):
        for k in range(KC):
            p.t(lambda e, tt=tt, k=k, pp=pp: e.matmul(
                psf[4][:, 0:128], Xp(k)[:, 128 * tt:128 * (tt + 1)],
                V["wva"](k)[:, 128 * pp:128 * (pp + 1)],
                start=(k == 0), stop=(k == KC - 1)))
        nt = p.mark("t", "T")
        p.wait("a", "T", nt)
        p.a(lambda e, tt=tt: e.activation(vp[:, 130 * tt:130 * tt + 64], psf[4][:, 0:64], AF.Copy))
        p.a(lambda e, tt=tt: e.activation(vp[:, 130 * tt + 65:130 * tt + 129], psf[4][:, 64:128], AF.Copy))
        na = p.mark("a", "A")
        p.wait("t", "A", na)
    p.wait("g", "A")
    ng = p.inc("g", lambda e: e.memset(vp[:].rearrange("p (t c) -> p t c", c=130)[:, :, 64:65], 1.0), "G")
    ng = p.inc("g", lambda e: e.memset(vp[:].rearrange("p (t c) -> p t c", c=130)[:, :, 129:130], 1.0), "G")
    p.wait("t", "G", ng)
    p.wait("v", "G", ng)
    for hh in range(2):
        _attn_head(p, L, V, pp, hh)


def _attn_head(p, L, V, pp, hh):
    stage, psf, psb = L["stage"], L["psf"], L["psb"]
    junk, redt = L["junk"], L["redt"]
    pjT, onesr3, onescf, onescb = L["pjT"], L["onesr3"], L["onescf"], L["onescb"]
    negselt, identb = L["negselt"], L["identb"]
    stab_tm, stabb, stabT, kacc = L["stab_tm"], L["stabb"], L["stabT"], L["kacc"]
    ctxe, svc = L["ctxe"], L["svc"]
    qp, kp, vp, ekb, eqs = V["qp"], V["kp"], V["vp"], V["ekb"], V["eqs"]
    h = 2 * pp + hh
    hb = 64 * hh
    qh = qp[hb:hb + 64, :]
    kh = kp[hb:hb + 64, :]
    pj = pjT[hb:hb + 64, :]
    for (src_, prow) in [(qh, 0), (kh, 32)]:
        for bl in range(NB):
            na = p.inc("a", lambda e, src_=src_, bl=bl: e.activation(
                stage[2][0:64, 0:BL], src_[:, BL * bl:BL * (bl + 1)], AF.Square), "A")
            p.wait("t", "A", na)
            p.t(lambda e: e.matmul(psf[6][0:1, :], onescf[0:64, :], stage[2][0:64, 0:BL],
                                   start=True, stop=True))
            nt = p.mark("t", "T")
            p.wait("a", "T", nt)
            na = p.inc("a", lambda e, prow=prow, bl=bl: e.activation(
                L["rowh0"][prow:prow + 1, BL * bl:BL * (bl + 1)], psf[6][0:1, :],
                AF.Copy, bias=0.0, scale=-0.5), "A")
            p.wait("t", "A", na)
    p.wait("v", "A")

    if h == 0 and L.get("dump", False):
        p.wait("g", "A")
        p.dma("g", "dg", L["dbg_nd"][0:1, 0:T], L["rowh0"][0:1, :])
        p.dma("g", "dg", L["dbg_nd"][0:1, T:2 * T], L["rowh0"][32:33, :])
        p.dma("g", "dg", L["dbg_qp"], qp[:])
        p.wait("v", "dg")
    for tt in range(NTT):
        p.t(lambda e, tt=tt: e.matmul(psf[4][:, 0:M], qh[:, 128 * tt:128 * (tt + 1)], pj,
                                      start=True, stop=True))
        nt = p.mark("t", "T")
        p.wait("v", "T", nt)
        p.v(lambda e, tt=tt: e.tensor_reduce(stab_tm[:, tt:tt + 1], psf[4][:, 0:M], AX.X, ALU.max))
        nv = p.mark("v", "V")
        p.wait("t", "V", nv)
    p.v(lambda e: e.tensor_scalar_add(junk[:, 0:1], junk[:, 0:1], 1.0))
    p.v(lambda e: e.tensor_copy(stabb[:], stab_tm[:]))
    nv = p.mark("v", "V")
    p.wait("t", "V", nv)
    p.t(lambda e: e.transpose(psb[0:16, 0:128], stabb[:], identb[:]))
    nt = p.mark("t", "T")
    p.wait("a", "T", nt)
    na = p.inc("a", lambda e: e.activation(stabT[:], psb[0:16, 0:128], AF.Copy), "A")
    na = p.inc("a", lambda e: e.activation(stage[2][0:16, 0:128], psb[0:16, 0:128],
                                           AF.Exp, bias=math.log(KEPS), scale=1.0), "A")
    p.wait("g", "A", na)
    p.dma("g", "dg", L["eps_d"][h:h + 1, :].rearrange("x (a b) -> (x a) b", a=16),
          stage[2][0:16, 0:128])
    p.wait("t", "A", na)
    p.wait("a", "dg")

    if h == 0 and L.get("dump", False):
        p.dma("g", "dg", L["dbg_stab"], stabT[:])
        p.wait("v", "dg")
    for mc in range(2):
        for bl in range(NB):
            ps = psf[mc % 2]
            p.t(lambda e, mc=mc, bl=bl, ps=ps: e.matmul(
                ps[:, 0:BL], pj[:, 128 * mc:128 * (mc + 1)],
                qh[:, BL * bl:BL * (bl + 1)], start=True, stop=False))
            p.t(lambda e, bl=bl, ps=ps: e.matmul(
                ps[:, 0:BL], onesr3[0:1, 0:128],
                L["rowh0"][0:1, BL * bl:BL * (bl + 1)], start=False, stop=False))
            for sb_ in range(4):
                tt = 4 * bl + sb_
                p.t(lambda e, tt=tt, sb_=sb_, ps=ps: e.matmul(
                    ps[:, 128 * sb_:128 * (sb_ + 1)],
                    negselt[:, 128 * tt:128 * (tt + 1)], stabT[:],
                    start=False, stop=(True)))
            nt = p.mark("t", "T")
            p.wait("a", "T", nt)
            na = p.inc("a", lambda e, ps=ps, bl=bl: e.activation(
                eqs[bl % 2][:], ps[:, 0:BL], AF.Exp), "A")
            p.wait("g", "A", na)
            p.dma("g", "dg", L["eq_d"][h, mc, :, BL * bl:BL * (bl + 1)], eqs[bl % 2][:])
            p.wait("t", "A", na)
            if bl >= 1:
                p.wait("a", "dg", p.cnt["dg"] - 16)
    p.wait("a", "dg")

    for tt in range(NTT):
        p.t(lambda e, tt=tt: e.matmul(psf[5][:, 0:M], kh[:, 128 * tt:128 * (tt + 1)], pj,
                                      start=True, stop=False))
        nt = p.mark("t", "T")
        p.wait("v", "T", nt)
        p.v(lambda e, tt=tt: e.tensor_reduce(redt[tt % 2][:], psf[5][:, 0:M], AX.X, ALU.max))
        if tt > 0:
            p.v(lambda e, tt=tt: e.tensor_max(kacc[:], kacc[:], redt[(tt - 1) % 2][:]))
        nv = p.mark("v", "V")
        p.wait("t", "V", nv)
        p.t(lambda e, tt=tt: e.matmul(psf[5][:, 0:M],
                                      L["rowh0"][32:33, 128 * tt:128 * (tt + 1)],
                                      onesr3[32:33, 0:M], start=False, stop=True))
        nt = p.mark("t", "T")
        p.wait("a", "T", nt)
        na = p.inc("a", lambda e: e.activation(ekb[:, 0:M], psf[5][:, 0:M], AF.Exp), "A")
        p.wait("t", "A", na)
        if h == 0 and tt == 0 and L.get("dump", False):
            p.wait("g", "A", na)
            p.dma("g", "dg", L["dbg_eku"], ekb[:, 0:M])
            p.wait("t", "dg")
        p.t(lambda e, tt=tt, hh=hh: e.matmul(
            psf[6][0:65, 0:M], vp[:, 130 * tt + 65 * hh:130 * tt + 65 * hh + 65],
            ekb[:, 0:M], start=(tt == 0), stop=(tt == NTT - 1)))
        nt = p.mark("t", "T")
        p.wait("a", "T", nt)
    p.v(lambda e: e.tensor_scalar_add(junk[:, 0:1], junk[:, 0:1], 1.0))
    p.v(lambda e: e.tensor_max(kacc[:], kacc[:], redt[(NTT - 1) % 2][:]))
    for tt in range(NTT):
        p.t(lambda e, tt=tt, hh=hh: e.matmul(
            psf[2][0:65, 0:1], vp[:, 130 * tt + 65 * hh:130 * tt + 65 * hh + 65],
            onescb[:], start=(tt == 0), stop=(tt == NTT - 1)))
    nt = p.mark("t", "T")
    p.wait("a", "T", nt)
    na = p.inc("a", lambda e, h=h: e.activation(svc[:, h:h + 1], psf[2][0:65, 0:1], AF.Copy), "A")
    na = p.inc("a", lambda e: e.activation(ctxe[:], psf[6][0:65, 0:M], AF.Copy), "A")
    p.wait("g", "A", na)
    p.dma("g", "dg", L["cc_in_ctx"][65 * h:65 * (h + 1), :], ctxe[:])
    p.wait("t", "A", na)
    p.wait("a", "dg")


def emit_attn(p, L):
    x_t, stage, psf, psb = L["x_t"], L["stage"], L["psf"], L["psb"]
    row, cols, junk = L["row"], L["cols"], L["junk"]
    redt = L["redt"]
    Xp = L["Xp"]
    pjT, onesr3, onescf, onescb = L["pjT"], L["onesr3"], L["onescf"], L["onescb"]
    negselt, identb = L["negselt"], L["identb"]
    stab_tm, stabb, stabT, kacc = L["stab_tm"], L["stabb"], L["stabT"], L["kacc"]
    ctxe, ctxbf, ctxb, svc = L["ctxe"], L["ctxbf"], L["ctxb"], L["svc"]
    V = A_views(L)
    qp, kp, vp, ekb, eqs = V["qp"], V["kp"], V["vp"], V["ekb"], V["eqs"]

    ln_stats(p, L)
    p.barrier()
    for i in range(KC):
        p.v(lambda e, i=i: e.tensor_sub(stage[2][:], x_t[i][:], stage[0][:]))
        p.v(lambda e, i=i: e.tensor_mul(Xp(i), stage[2][:], stage[1][:]))
    p.barrier()
    if L.get("dump", False):
        p.wait("g", "V")
        p.dma("g", "dg", L["dbg_lnb"], Xp(0))
        p.wait("s", "dg")
    for k in range(KC):
        att = L["cc_out_att"]
        if k == 0:
            p.wait("s", "CC", L["cc_att"])
        p.dma("s", "di", V["wqa"](k), att[512 * k:512 * k + 128, :])
        p.dma("s", "di", V["wka"](k), att[512 * k + 128:512 * k + 256, :])
        p.dma("s", "di", V["wva"](k), att[512 * k + 256:512 * k + 384, :])
    p.wait("t", "di")
    p.a(lambda e: e.activation(kacc[:], L["onescf"][:, 0:1], AF.Copy, bias=-1e30, scale=1.0))
    p.barrier()

    for pp in range(KC):
        _attn_pair(p, L, V, pp)
    p.barrier()

    p.v(lambda e: e.tensor_copy(L["colw"][:], kacc[:, 0:1].broadcast_to([128, 64])))
    nv = p.mark("v", "V")
    p.wait("g", "V", nv)
    p.inc("g", lambda e: e.tensor_reduce(row(1, 0)[:, 20:21], L["colw"][:, 0:1], AX.C, ALU.max), "G")
    p.dma("g", "dg", L["cc_in_s"][0:1, 0:1], row(1, 0)[:, 20:21])
    p.wait("g", "dg")
    n = p.inc("g", lambda e: e.collective_compute(
        "AllReduce", ALU.max, [[0, 1, 2, 3, 4, 5, 6, 7]],
        [L["cc_in_s"][0:1, 0:1].opt()], [L["cc_out_s"][0:1, 0:1].opt()]), "CC")
    p.dma("g", "dg", L["cc_in_sv"], svc[:, 0:HEADS])
    p.wait("g", "dg")
    n = p.inc("g", lambda e: e.collective_compute(
        "AllReduce", ALU.add, [[0, 1], [2, 3], [4, 5], [6, 7]],
        [L["cc_in_sv"].opt()], [L["cc_out_sv"].opt()]), "CC")
    n = p.inc("g", lambda e: e.collective_compute(
        "AllReduce", ALU.add, [[0, 1], [2, 3], [4, 5], [6, 7]],
        [L["cc_in_ctx"].opt()], [L["cc_out_ctx"].opt()]), "CC")
    p.wait("g", "CC", n)
    p.dma("g", "dg", row(1, 0)[:, 21:22], L["cc_out_s"][0:1, 0:1])
    p.dma("g", "dg", svc[:, 0:HEADS], L["cc_out_sv"])
    p.wait("a", "dg")
    p.wait("g", "dg")
    na = p.inc("a", lambda e: e.activation(row(1, 0)[:, 22:23], row(1, 0)[:, 21:22],
                                           AF.Exp, bias=math.log(KEPS), scale=1.0), "A")
    p.wait("t", "A", na)
    p.t(lambda e: e.matmul(psf[3][0:65, 0:1], onesr3[0:1, 0:65], row(1, 0)[:, 22:23],
                           start=True, stop=True))
    nt = p.mark("t", "T")
    p.wait("a", "T", nt)
    na = p.inc("a", lambda e: e.activation(cols[:, 16:17][0:65, :], psf[3][0:65, 0:1], AF.Copy), "A")
    p.wait("v", "A", na)
    p.v(lambda e: e.tensor_scalar(svc[:, 0:HEADS], svc[:, 0:HEADS], cols[0:65, 16:17], None, ALU.mult))
    nv = p.mark("v", "V")
    p.barrier()

    if L.get("dump", False):
        p.dma("g", "dg", L["dbg_ctx"], L["cc_out_ctx"][0:65, :])
        p.dma("g", "dg", L["dbg_sv"], L["svc"][:, 0:HEADS])
        p.wait("g", "dg")
    for k in range(KC):
        p.dma("s", "di", V["woa"](k), L["cc_out_att"][512 * k + 384:512 * k + 512, :])

    for h in range(HEADS):
        p.dma("s", "di", ctxe[:], L["cc_out_ctx"][65 * h:65 * (h + 1), :])
        p.wait("v", "di")
        p.v(lambda e, h=h: e.scalar_tensor_tensor(
            ctxbf[:], svc[:, h:h + 1].broadcast_to([65, M]), 1.0, ctxe[:], ALU.mult, ALU.add))
        nv = p.mark("v", "V")
        p.wait("t", "V", nv)
        for mc in range(2):
            p.t(lambda e, mc=mc: e.transpose(psb[:, 0:65], ctxbf[0:65, 128 * mc:128 * (mc + 1)], identb[0:65, 0:65]))
            nt = p.mark("t", "T")
            p.wait("a", "T", nt)
            na = p.inc("a", lambda e, mc=mc: e.activation(ctxb[:, 65 * mc:65 * (mc + 1)], psb[:, 0:65], AF.Copy), "A")
            p.wait("t", "A", na)
        for mc in range(2):
            p.t(lambda e, mc=mc: e.matmul(psf[2][0:1, 0:65], onescb[:], ctxb[:, 65 * mc:65 * (mc + 1)],
                                          start=(mc == 0), stop=(mc == 1)))
        nt = p.mark("t", "T")
        p.wait("a", "T", nt)
        na = p.inc("a", lambda e: e.activation(row(1, 0)[:, 192:257], psf[2][0:1, 0:65], AF.Copy), "A")
        p.dma("s", "di", L["rowh0"][0:1, :], L["eps_d"][h:h + 1, :])
        p.dma("s", "di", qp[:], L["eq_d"][h, 0])
        p.dma("s", "di", kp[:], L["eq_d"][h, 1])
        p.wait("t", "di")
        p.wait("t", "A")
        for bl in range(NB):
            eq_ = [qp, kp]
            p.t(lambda e, bl=bl: e.matmul(psf[0][0:65, 0:BL], ctxb[:, 0:65],
                                          qp[:, BL * bl:BL * (bl + 1)], start=True, stop=False))
            p.t(lambda e, bl=bl: e.matmul(psf[0][0:65, 0:BL], ctxb[:, 65:130],
                                          kp[:, BL * bl:BL * (bl + 1)], start=False, stop=False))
            p.t(lambda e, bl=bl: e.matmul(psf[0][0:65, 0:BL], row(1, 0)[:, 192:257],
                                          L["rowh0"][0:1, BL * bl:BL * (bl + 1)],
                                          start=False, stop=True))
            nt = p.mark("t", "T")
            p.wait("v", "T", nt)
            p.v(lambda e, bl=bl: e.reciprocal(L["rowh0"][32:33, BL * bl:BL * (bl + 1)], psf[0][64:65, 0:BL]))
            nv = p.mark("v", "V")
            p.wait("t", "V", nv)
            p.t(lambda e, bl=bl: e.matmul(psf[1][0:64, 0:BL], onesr3[32:33, 0:64],
                                          L["rowh0"][32:33, BL * bl:BL * (bl + 1)],
                                          start=True, stop=True))
            nt = p.mark("t", "T")
            p.wait("a", "T", nt)
            na = p.inc("a", lambda e: e.activation(stage[2][0:64, 0:BL], psf[0][0:64, 0:BL], AF.Copy), "A")
            p.wait("v", "A", na)
            dstc = Xp(h // 2)[64 * (h % 2):64 * (h % 2) + 64, :]
            p.v(lambda e, bl=bl, dstc=dstc: e.tensor_mul(
                dstc[:, BL * bl:BL * (bl + 1)], stage[2][0:64, 0:BL], psf[1][0:64, 0:BL]))
            nv = p.mark("v", "V")
            p.wait("t", "V", nv)
    if L.get("dump", False):
        p.barrier()
        p.dma("s", "di", L["dbg_att"], L["Xp"](0)[0:64, :])
        p.dma("s", "di", L["dbg_inv"], L["rowh0"][32:33, :])
        p.wait("s", "di")
    p.barrier()

    for z in range(KC):
        p.v(lambda e, z=z: e.tensor_scalar(x_t[z][:], x_t[z][:], L["bocol"][:, z:z + 1], None, ALU.add))
    nv = p.mark("v", "V")
    p.wait("t", "V", nv)
    p.wait("s", "di")
    for z in range(KC):
        pa = [psf[b] for b in range(4)]
        for k in range(KC):
            for bl in range(NB):
                p.t(lambda e, z=z, k=k, bl=bl: e.matmul(
                    pa[bl][:, 0:BL], V["woa"](k)[:, 128 * z:128 * (z + 1)],
                    Xp(k)[:, BL * bl:BL * (bl + 1)], start=(k == 0), stop=(k == KC - 1)))
        nt = p.mark("t", "T")
        p.wait("a", "T", nt)
        for bl in range(NB):
            p.a(lambda e, bl=bl: e.activation(stage[2][:, BL * bl:BL * (bl + 1)], pa[bl][:, 0:BL], AF.Copy))
        na = p.mark("a", "A")
        p.wait("v", "A", na)
        p.v(lambda e, z=z: e.tensor_add(x_t[z][:], x_t[z][:], stage[2][:]))
        nv = p.mark("v", "V")
        p.wait("t", "V", nv)
        p.wait("a", "V", nv)
    p.barrier()


_nc_cache = {}


def kernel(L0_attn_x, L0_attn_wq, L0_attn_wk, L0_attn_wv, L0_attn_wo, L0_attn_bo,
           L0_attn_proj, L0_ffn_w_gate, L0_ffn_w_up, L0_ffn_w_down):
    x = np.ascontiguousarray(np.asarray(L0_attn_x, np.float32))
    shards = x.reshape(NCORES, T, D)
    wq = np.ascontiguousarray(L0_attn_wq, np.float32)
    wk = np.ascontiguousarray(L0_attn_wk, np.float32)
    wv = np.ascontiguousarray(L0_attn_wv, np.float32)
    wo = np.ascontiguousarray(L0_attn_wo, np.float32)
    bo = np.ascontiguousarray(np.asarray(L0_attn_bo, np.float32).reshape(D, 1))
    projT = np.ascontiguousarray(np.asarray(L0_attn_proj, np.float32).T)
    wg = np.ascontiguousarray(L0_ffn_w_gate, np.float32)
    wu = np.ascontiguousarray(L0_ffn_w_up, np.float32)
    wd = np.ascontiguousarray(L0_ffn_w_down, np.float32)
    in_maps = []
    for c in range(NCORES):
        in_maps.append({
            "x_in": np.ascontiguousarray(shards[c].T),
            "bo_in": bo, "pjT_in": projT,
            "att4_in": np.ascontiguousarray(np.concatenate(
                [w[128 * c:128 * (c + 1), :] for w in (wq, wk, wv, wo)], axis=0)),
            "wgc_in": np.ascontiguousarray(wg[:, 128 * c:128 * (c + 1)]),
            "wuc_in": np.ascontiguousarray(wu[:, 128 * c:128 * (c + 1)]),
            "wdc_in": np.ascontiguousarray(wd[:, 384 * c:384 * (c + 1)]),
        })
    if "nc" not in _nc_cache:
        _nc_cache["nc"] = build_nc()
    nc = _nc_cache["nc"]
    res = run_bass_kernel_spmd(nc, in_maps, core_ids=list(range(NCORES)))
    _nc_cache["res0"] = res.results[0]
    _nc_cache["res_all"] = res.results
    outs = [np.asarray(res.results[c]["out"]).T for c in range(NCORES)]
    return np.concatenate(outs, axis=0).reshape(4, 4096, D).astype(np.float32)


# version 15 (final)
# speedup vs baseline: 1.0808x; 1.0397x over previous
import math
import numpy as np
import ml_dtypes
from contextlib import ExitStack

import concourse.bass as bass
import concourse.mybir as mybir
from concourse.bass_utils import run_bass_kernel_spmd

F32 = mybir.dt.float32
BF16 = mybir.dt.bfloat16
ALU = mybir.AluOpType
AF = mybir.ActivationFunctionType
AX = mybir.AxisListType

NCORES = 8
D = 1024
T = 2048
KC = 8
I3 = 3072
IT = 24
NB = 4
BL = 512
HEADS = 16
DH = 64
M = 256
NTT = 16
MAGIC = 12582912.0
QMAX = 32767.0
EPS_LN = 1e-5
KEPS = 1e-4
NORM = float(DH) ** -0.25

DEBUG_STAGE = 0
DEBUG_DUMPS = False
F4_BARRIER = False
F4_SERIAL_W = False


class P:
    ENGS = ("t", "v", "a", "g", "s")

    def __init__(self, nc):
        self.nc = nc
        self.steps = {e: [] for e in self.ENGS}
        self.has_inc = {e: -1 for e in self.ENGS}
        self.sems = {}
        self.cnt = {}
        self.bar_n = 0

    def sem(self, name):
        h = self.nc.alloc_semaphore(name)
        self.sems[name] = h
        self.cnt[name] = 0
        return h

    def _emit(self, e, fn):
        self.steps[e].append(fn)

    def t(self, fn): self._emit("t", fn)
    def v(self, fn): self._emit("v", fn)
    def a(self, fn): self._emit("a", fn)
    def g(self, fn): self._emit("g", fn)
    def s(self, fn): self._emit("s", fn)

    def inc(self, e, fn, sem, n=1):
        h = self.sems[sem]
        self._emit(e, lambda eng: fn(eng).then_inc(h, n))
        self.has_inc[e] = len(self.steps[e])
        self.cnt[sem] += n
        return self.cnt[sem]

    def mark(self, e, sem):
        if e == "v":
            jk = self.junk_ap
            self._emit("v", lambda eng: eng.tensor_scalar_add(jk, jk, 1.0))
        h = self.sems[sem]
        assert self.steps[e], "mark with no prior instruction"
        if self.has_inc[e] == len(self.steps[e]):
            self._emit(e, lambda eng: eng.nop())
        last = self.steps[e][-1]
        self.steps[e][-1] = (lambda eng, f=last, h=h: f(eng).then_inc(h, 1))
        self.has_inc[e] = len(self.steps[e])
        self.cnt[sem] += 1
        return self.cnt[sem]

    def wait(self, e, sem, val=None):
        h = self.sems[sem]
        v = self.cnt[sem] if val is None else val
        self._emit(e, lambda eng: eng.wait_ge(h, v))

    def dma(self, e, sem, dst, src):
        h = self.sems[sem]
        self._emit(e, lambda eng: eng.dma_start(dst, src).then_inc(h, 16))
        self.cnt[sem] += 16
        return self.cnt[sem]

    def barrier(self, engines="tvags"):
        self.bar_n += 1
        n = self.bar_n
        for e in engines:
            name = "bar_" + e
            if name not in self.sems:
                self.sem(name)
        for e in engines:
            h = self.sems["bar_" + e]
            self._emit(e, lambda eng, h=h: eng.drain().then_inc(h, 1))
            self.cnt["bar_" + e] += 1
        for e in engines:
            for o in engines:
                if o != e:
                    self._emit(e, lambda eng, h=self.sems["bar_" + o], n=n: eng.wait_ge(h, n))


def build_nc():
    nc = bass.Bass("TRN2", target_bir_lowering=False, debug=False)
    L = {}

    def D_(name, shape, dt=F32, kind="Internal", addr_space="Local"):
        L[name] = nc.dram_tensor(name, shape, dt, kind=kind, addr_space=addr_space).ap()
        return L[name]

    D_("x_in", [D, T], kind="ExternalInput")
    D_("att4_in", [4 * 128, D], kind="ExternalInput")
    D_("bo_in", [D, 1], kind="ExternalInput")
    D_("pjT_in", [DH, M], kind="ExternalInput")
    D_("wgc_in", [I3, 128], kind="ExternalInput")
    D_("wuc_in", [I3, 128], kind="ExternalInput")
    D_("wdc_in", [D, 384], kind="ExternalInput")
    D_("out", [D, T], kind="ExternalOutput")
    if DEBUG_DUMPS:
        D_("dbg_m", [128, T], kind="ExternalOutput")
        D_("dbg_r", [128, T], kind="ExternalOutput")
        D_("dbg_xp", [128, T], BF16, kind="ExternalOutput")
        D_("dbg_gate", [128, T], kind="ExternalOutput")
        D_("dbg_p", [128, T], kind="ExternalOutput")
        D_("dbg_q", [128, T], kind="ExternalOutput")
        D_("dbg_cols", [128, 64], kind="ExternalOutput")
        D_("dbg_wg", [128, I3], BF16, kind="ExternalOutput")
        D_("dbg_row", [1, 512], kind="ExternalOutput")
        D_("dbg_nd", [1, 2 * T], kind="ExternalOutput")
        D_("dbg_stab", [16, 128], BF16, kind="ExternalOutput")
        D_("dbg_eku", [128, M], BF16, kind="ExternalOutput")
        D_("dbg_ctx", [65, M], kind="ExternalOutput")
        D_("dbg_sv", [65, HEADS], kind="ExternalOutput")
        D_("dbg_eq", [128, T], BF16, kind="ExternalOutput")
        D_("dbg_inv", [1, T], kind="ExternalOutput")
        D_("dbg_att", [64, T], BF16, kind="ExternalOutput")
        D_("dbg_qp", [128, T], BF16, kind="ExternalOutput")
        D_("dbg_lnb", [128, T], BF16, kind="ExternalOutput")
        D_("dbg_wq", [128, D], BF16, kind="ExternalOutput")
        D_("dbg_gall", [IT * 128, T], kind="ExternalOutput")
        D_("dbg_uall", [IT * 128, T], kind="ExternalOutput")
        D_("dbg_acc", [128, 8], kind="ExternalOutput")
        D_("dbg_s2", [128, T], kind="ExternalOutput")
        D_("dbg_wgall", [NCORES * 128, I3], BF16, kind="ExternalOutput")

    D_("cc_in_att", [512, D], BF16)
    D_("cc_out_att", [NCORES * 512, D], BF16, addr_space="Shared")
    D_("cc_in_g", [128, I3], BF16)
    D_("cc_out_g", [NCORES * 128, I3], BF16, addr_space="Shared")
    D_("cc_in_u", [128, I3], BF16)
    D_("cc_out_u", [NCORES * 128, I3], BF16, addr_space="Shared")
    D_("cc_in_d", [3072, 128], BF16)
    D_("cc_out_d", [NCORES * 3072, 128], BF16, addr_space="Shared")
    D_("gate_d", [IT, 128, T])
    D_("up_d", [IT, 128, T])
    D_("p_d", [IT, 128, T])
    D_("q_d", [IT, 128, T])
    D_("eq_d", [HEADS, 2, 128, T], BF16)
    D_("eps_d", [HEADS, 2048])
    D_("cc_in_s", [1, 64])
    D_("cc_out_s", [1, 64])
    D_("cc_in_ctx", [HEADS * 65, M])
    D_("cc_out_ctx", [HEADS * 65, M])
    D_("cc_in_sv", [65, HEADS])
    D_("cc_out_sv", [65, HEADS])

    L["ident_f"] = nc.inline_tensor(np.eye(128, dtype=np.float32), name="ident_f").ap()
    L["ones_col_f"] = nc.inline_tensor(np.ones((128, 1), np.float32), name="ones_col_f").ap()
    L["ones_row_f"] = nc.inline_tensor(np.ones((1, 128), np.float32), name="ones_row_f").ap()
    negsel_np = np.zeros((16, 16 * 128), np.float32)
    for t_ in range(16):
        negsel_np[t_, t_ * 128:(t_ + 1) * 128] = -1.0
    L["negsel"] = nc.inline_tensor(negsel_np.astype(ml_dtypes.bfloat16), name="negsel").ap()
    CONSTS = [0.0, 1.0, 0.5, MAGIC, 65536.0, EPS_LN, -0.5, 1.0 / 1024.0, math.log(KEPS), -1e30]
    L["consts_in"] = nc.inline_tensor(np.tile(np.array([CONSTS], np.float32), (128, 1)), name="consts").ap()
    L["CONSTS"] = CONSTS

    st = ExitStack()
    with st:
        def sbt(name, shape, dt=F32):
            h = st.enter_context(nc.sbuf_tensor(name, shape, dt))
            L[name] = h
            return h

        wdt1 = sbt("wdt1", [128, 128], BF16)
        wdt2 = sbt("wdt2", [128, 128], BF16)
        x_t = [sbt(f"x{i}", [128, T]) for i in range(KC)]
        L["x_t"] = x_t
        stage = [sbt(f"stg{i}", [128, T]) for i in range(3)]
        L["stage"] = stage
        arena = sbt("arena", [128, 49152], BF16)
        rowh = [sbt("rowh0", [128, T]), sbt("rowh1", [128, 512])]
        L["rowh"] = rowh
        sbt("cols", [128, 64])
        sbt("colw", [128, 64])
        sbt("junk", [128, 8])
        L["redt"] = [sbt(f"red{i}", [128, 1]) for i in range(2)]
        sbt("acc", [128, 1])
        sbt("dbgt", [128, 8])
        sbt("constt", [128, len(CONSTS)])
        sbt("identt", [128, 128])
        sbt("identb", [128, 128], BF16)
        sbt("onescf", [128, 1])
        sbt("onescb", [128, 1], BF16)
        sbt("onesrf", [1, 128])
        sbt("onesr3", [128, 256])
        sbt("negselt", [16, 16 * 128], BF16)
        sbt("pjT", [128, M], BF16)
        sbt("stabT", [16, 128], BF16)
        sbt("stab_tm", [128, 16])
        sbt("stabb", [128, 16], BF16)
        sbt("kacc", [128, 1])
        sbt("ctxbf", [65, M], BF16)
        sbt("ctxe", [65, M])
        sbt("ctxb", [128, 2 * 65], BF16)
        sbt("svc", [65, HEADS + 1])
        L["wdtile"] = [sbt("wdt0", [128, 128], BF16), wdt1, wdt2]
        L["sblk"] = [sbt(f"sblk{i}", [128, 96]) for i in range(3)]
        sbt("bocol", [128, KC])

        for j, val in enumerate(CONSTS):
            nc.const_aps.register(F32, float(val), L["constt"][:, j:j + 1])

        L["Xp"] = lambda i: arena[:, 2048 * i: 2048 * (i + 1)]
        L["Wb"] = lambda k: arena[:, 16384 + 3072 * k: 16384 + 3072 * (k + 1)]
        L["Ob"] = lambda i: arena[:, 2048 * i: 2048 * (i + 1)]
        L["Ab"] = lambda i: arena[:, 16384 + 2048 * i: 16384 + 2048 * (i + 1)]
        L["row"] = lambda i, slot: rowh[i][slot:slot + 1, :]

        L["psf"] = [st.enter_context(nc.psum_tensor(f"ps{i}", [128, 512], F32)) for i in range(7)]
        L["psb"] = st.enter_context(nc.psum_tensor("psb", [128, 1024], BF16))

        L["dump"] = DEBUG_DUMPS and DEBUG_STAGE == 2
        p = P(nc)
        p.junk_ap = L["junk"][:, 7:8]
        for name in ("di", "dg", "T", "V", "A", "G", "CC"):
            p.sem(name)

        nc.all_engine_barrier()
        for h in list(p.sems.values()):
            nc.gpsimd.sem_clear(h)
        nc.all_engine_barrier()

        author(p, L)

        with nc.Block() as block:
            @block.tensor
            def _(e):
                for f in p.steps["t"]:
                    f(e)

            @block.vector
            def _(e):
                for f in p.steps["v"]:
                    f(e)

            @block.scalar
            def _(e):
                for f in p.steps["a"]:
                    f(e)

            @block.gpsimd
            def _(e):
                for f in p.steps["g"]:
                    f(e)

            @block.sync
            def _(e):
                for f in p.steps["s"]:
                    f(e)

        nc.all_engine_barrier()
        for h in list(p.sems.values()):
            nc.gpsimd.sem_clear(h)
        nc.all_engine_barrier()
    return nc


def author(p, L):
    x_t, stage = L["x_t"], L["stage"]
    row = L["row"]

    p.dma("s", "di", L["identt"][:], L["ident_f"])
    p.dma("s", "di", L["constt"][:], L["consts_in"])
    p.dma("s", "di", L["onesrf"][:], L["ones_row_f"])
    for b in (0, 32, 64):
        p.dma("s", "di", L["onesr3"][b:b + 1, 0:128], L["ones_row_f"])
        p.dma("s", "di", L["onesr3"][b:b + 1, 128:256], L["ones_row_f"])
    p.dma("s", "di", L["onescf"][:], L["ones_col_f"])
    p.dma("s", "di", L["negselt"][:], L["negsel"])
    for i in range(KC):
        p.dma("s", "di", x_t[i][:], L["x_in"][128 * i:128 * (i + 1), :])
        p.dma("s", "di", L["bocol"][:, i:i + 1], L["bo_in"][128 * i:128 * (i + 1), :])
    for e in "tvag":
        p.wait(e, "di")
    p.v(lambda e: e.tensor_copy(L["onescb"][:], L["onescf"][:]))
    p.v(lambda e: e.tensor_copy(L["identb"][:], L["identt"][:]))
    p.barrier()

    emit_weight_prep(p, L)

    emit_ffn(p, L)
    if DEBUG_STAGE != 1:
        emit_attn(p, L)
        if DEBUG_STAGE != 2:
            emit_ffn(p, L)

    p.barrier()
    if DEBUG_DUMPS:
        p.dma("s", "di", L["dbg_m"], stage[0][:])
        p.dma("s", "di", L["dbg_r"], stage[1][:])
        p.dma("s", "di", L["dbg_xp"], L["Xp"](0))
        p.dma("s", "di", L["dbg_gate"], L["gate_d"][0])
        p.dma("s", "di", L["dbg_p"], L["p_d"][0])
        p.dma("s", "di", L["dbg_q"], L["q_d"][0])
        p.dma("s", "di", L["dbg_cols"], L["cols"][:])
        p.dma("s", "di", L["dbg_wg"], L["cc_out_g"][0:128, :])
        p.dma("s", "di", L["dbg_row"], L["rowh1"][0:1, :])
        for ot_ in range(IT):
            p.dma("s", "di", L["dbg_gall"][128 * ot_:128 * (ot_ + 1), :], L["gate_d"][ot_])
            p.dma("s", "di", L["dbg_uall"][128 * ot_:128 * (ot_ + 1), :], L["up_d"][ot_])
        p.dma("s", "di", L["dbg_acc"], L["dbgt"][:])
        p.dma("s", "di", L["dbg_s2"], stage[2][:])
        for k_ in range(NCORES):
            p.dma("s", "di", L["dbg_wgall"][128 * k_:128 * (k_ + 1), :], L["cc_out_g"][128 * k_:128 * (k_ + 1), :])
        if DEBUG_STAGE == 2:
            p.dma("s", "di", L["dbg_eq"], L["eq_d"][0, 0])
            p.dma("s", "di", L["dbg_wq"], L["cc_out_att"][0:128, :])
    for i in range(KC):
        p.dma("s", "di", L["out"][128 * i:128 * (i + 1), :], x_t[i][:])
    p.wait("s", "di")


def emit_weight_prep(p, L):
    stage, sblk = L["stage"], L["sblk"]
    identb, psb = L["identb"], L["psb"]
    Ab, Wb = L["Ab"], L["Wb"]
    junk = L["junk"]

    units = [("att", r) for r in range(4)]
    units += [("g", ot) for ot in range(IT)]
    units += [("u", ot) for ot in range(IT)]
    units += [("d", j, h) for j in range(3) for h in range(KC)]

    def unit_src(u):
        if u[0] == "att":
            return L["att4_in"][128 * u[1]:128 * (u[1] + 1), :], D
        if u[0] == "g":
            return L["wgc_in"][128 * u[1]:128 * (u[1] + 1), :], 128
        if u[0] == "u":
            return L["wuc_in"][128 * u[1]:128 * (u[1] + 1), :], 128
        _, j, h = u
        return L["wdc_in"][128 * h:128 * (h + 1), 128 * j:128 * (j + 1)], 128

    def unit_dst(u):
        if u[0] == "att":
            return L["cc_in_att"][128 * u[1]:128 * (u[1] + 1), :]
        if u[0] == "g":
            return L["cc_in_g"][:, 128 * u[1]:128 * (u[1] + 1)]
        if u[0] == "u":
            return L["cc_in_u"][:, 128 * u[1]:128 * (u[1] + 1)]
        _, j, h = u
        return L["cc_in_d"][1024 * j + 128 * h:1024 * j + 128 * (h + 1), :]

    di_at = {}
    gdone = {}
    arena = L["arena"]
    dqs = []

    def slot_ap(d):
        return arena[:, 16384 + 128 * d:16384 + 128 * (d + 1)]

    batches = [("att", r) for r in range(4)]
    batches += [("wg", b) for b in range(6)] + [("wu", b) for b in range(6)]
    batches += [("wd", j, hb) for j in range(3) for hb in range(2)]

    wg3 = L["wgc_in"].rearrange("(t p) c -> p t c", p=128)
    wu3 = L["wuc_in"].rearrange("(t p) c -> p t c", p=128)
    wd3 = L["wdc_in"].rearrange("(t p) c -> p t c", p=128)

    def batch_src(u):
        if u[0] == "att":
            return L["att4_in"][128 * u[1]:128 * (u[1] + 1), :], D
        if u[0] == "wg":
            return wg3[:, 4 * u[1]:4 * (u[1] + 1), :], 512
        if u[0] == "wu":
            return wu3[:, 4 * u[1]:4 * (u[1] + 1), :], 512
        _, j, hb = u
        return wd3[:, 4 * hb:4 * (hb + 1), 128 * j:128 * (j + 1)], 512

    def batch_tiles(u):
        if u[0] == "wg":
            return [("g", 4 * u[1] + t) for t in range(4)]
        if u[0] == "wu":
            return [("u", 4 * u[1] + t) for t in range(4)]
        _, j, hb = u
        return [("d", j, 4 * hb + t) for t in range(4)]

    def in_slot(n, w):
        return stage[(n % 4) // 2][:, 1024 * (n % 2):1024 * (n % 2) + w]

    def do_load(n):
        u = batches[n]
        src_, w = batch_src(u)
        dst = in_slot(n, w)
        if u[0] != "att":
            dst = dst.rearrange("p (t c) -> p t c", c=128)
        di_at[n] = p.dma("s", "di", dst, src_)

    for n in range(4):
        do_load(n)
    cast_hist = []
    for n, u in enumerate(batches):
        _, w = batch_src(u)
        wbuf = in_slot(n, w)
        p.wait("v", "di", di_at[n])
        if u[0] == "att":
            btile = arena[:, 25600 + 1024 * (n % 2):25600 + 1024 * (n % 2) + 1024]
            if n >= 2 and batches[n - 2][0] == "att":
                p.wait("v", "dg", gdone[batches[n - 2]])
            scale = NORM if u[1] < 2 else 1.0
            if scale != 1.0:
                p.v(lambda e, b=btile, wb=wbuf, s_=scale: e.tensor_scalar_mul(b, wb, s_))
            else:
                p.v(lambda e, b=btile, wb=wbuf: e.tensor_copy(b, wb))
            nv = p.mark("v", "V")
            p.wait("g", "V", nv)
            gdone[u] = p.dma("g", "dg", unit_dst(u), btile)
        else:
            d0 = len(dqs)
            btile = arena[:, 16384 + 128 * d0:16384 + 128 * d0 + 512]
            wrow = wbuf
            wv3 = wrow.rearrange("p (b c) -> p b c", c=32)
            nblk = w // 32
            s1 = sblk[0][:, 0:nblk]
            s2 = sblk[1][:, 0:nblk]
            s3 = sblk[2][:, 0:nblk]
            p.v(lambda e, wv3=wv3, s1=s1: e.tensor_reduce(s1, wv3, AX.X, ALU.max, apply_absolute_value=True))
            p.v(lambda e: e.tensor_scalar_add(junk[:, 0:1], junk[:, 0:1], 1.0))
            p.v(lambda e, s1=s1, s2=s2: e.tensor_scalar(s2, s1, 1.0 / 7.0, 1e-12, ALU.mult, ALU.max))
            p.v(lambda e: e.tensor_scalar_add(junk[:, 0:1], junk[:, 0:1], 1.0))
            p.v(lambda e, s2=s2, s3=s3: e.reciprocal(s3, s2))
            p.v(lambda e: e.tensor_scalar_add(junk[:, 0:1], junk[:, 0:1], 1.0))
            s3b = s3.unsqueeze(2).broadcast_to([128, nblk, 32])
            s2b = s2.unsqueeze(2).broadcast_to([128, nblk, 32])
            p.v(lambda e, wv3=wv3, s3b=s3b: e.tensor_tensor(wv3, wv3, s3b, ALU.mult))
            p.v(lambda e, w2=wrow: e.tensor_single_scalar(w2, w2, MAGIC, ALU.add))
            p.v(lambda e, w2=wrow: e.tensor_single_scalar(w2, w2, MAGIC, ALU.subtract))
            f32q = stage[2][:, 512 * (n % 4):512 * (n % 4) + 512]
            if len(cast_hist) >= 4:
                p.wait("v", "A", cast_hist[-4])
            p.v(lambda e, wv3=wv3, s2b=s2b, f3=f32q.rearrange("p (b c) -> p b c", c=32):
                e.tensor_tensor(f3, wv3, s2b, ALU.mult))
            nv = p.mark("v", "V")
            p.wait("a", "V", nv)
            p.inc("a", lambda e, b=btile, f=f32q: e.activation(b, f, AF.Copy), "A")
            cast_hist.append(p.cnt["A"])
            dqs.extend(batch_tiles(u))
        if n + 4 < len(batches):
            p.wait("s", "V")
            do_load(n + 4)
    p.barrier()

    evA = 0
    abs_ = [arena[:, 27648:27776], arena[:, 27776:27904]]
    gd2 = {}
    for d, m in enumerate(dqs):
        if evA:
            p.wait("t", "A", evA)
        p.t(lambda e, b=slot_ap(d): e.transpose(psb[:, 0:128], b, identb[:]))
        nt = p.mark("t", "T")
        p.wait("a", "T", nt)
        if d >= 2:
            p.wait("a", "dg", gd2[d - 2])
        p.inc("a", lambda e, ab=abs_[d % 2]: e.activation(ab, psb[:, 0:128], AF.Copy), "A")
        evA = p.cnt["A"]
        p.wait("g", "A", evA)
        gd2[d] = p.dma("g", "dg", unit_dst(m), abs_[d % 2])
        p.wait("g", "dg")

    p.barrier()
    p.dma("s", "di", stage[0][0:DH, 0:M], L["pjT_in"])
    p.wait("v", "di")
    p.v(lambda e: e.tensor_copy(L["pjT"][0:DH, :], stage[0][0:DH, 0:M]))
    p.v(lambda e: e.tensor_copy(L["pjT"][64:64 + DH, :], stage[0][0:DH, 0:M]))
    nv = p.mark("v", "V")
    p.wait("s", "V", nv)
    p.barrier()

    for (ci, co, key) in [("cc_in_g", "cc_out_g", "cc_g"), ("cc_in_u", "cc_out_u", "cc_u")]:
        p.inc("g", lambda e, ci=ci, co=co: e.collective_compute(
            "AllGather", ALU.bypass, [[0, 1, 2, 3, 4, 5, 6, 7]],
            [L[ci][:].opt()], [L[co][:].opt()]), "CC")
        L[key] = p.cnt["CC"]


def ln_stats(p, L):
    x_t, stage, psf = L["x_t"], L["stage"], L["psf"]
    row = L["row"]
    onescf = L["onescf"]
    for bl in range(NB):
        for k in range(KC):
            p.t(lambda e, k=k, bl=bl: e.matmul(psf[4][0:1, :], onescf[:],
                                               x_t[k][:, BL * bl:BL * (bl + 1)],
                                               start=(k == 0), stop=(k == KC - 1)))
        for k in range(KC):
            na = p.inc("a", lambda e, k=k, bl=bl: e.activation(
                stage[2][:, 0:BL], x_t[k][:, BL * bl:BL * (bl + 1)], AF.Square), "A")
            p.wait("t", "A", na)
            p.t(lambda e, k=k: e.matmul(psf[5][0:1, :], onescf[:], stage[2][:, 0:BL],
                                        start=(k == 0), stop=(k == KC - 1)))
            if k < KC - 1:
                nt = p.mark("t", "T")
                p.wait("a", "T", nt)
        nt = p.mark("t", "T")
        p.wait("v", "T", nt)
        p.wait("a", "T", nt)
        mrow = row(0, 0)[:, 0:BL]
        erow = row(0, 0)[:, BL:2 * BL]
        tmp = row(0, 0)[:, 2 * BL:3 * BL]
        p.v(lambda e, mr=mrow: e.tensor_scalar_mul(mr, psf[4][0:1, :], 1.0 / 1024.0))
        p.v(lambda e, rr=erow: e.tensor_scalar_mul(rr, psf[5][0:1, :], 1.0 / 1024.0))
        p.v(lambda e, mr=mrow, tm=tmp: e.tensor_mul(tm, mr, mr))
        p.v(lambda e, rr=erow, tm=tmp: e.tensor_sub(rr, rr, tm))
        nv = p.mark("v", "V")
        p.wait("a", "V", nv)
        p.a(lambda e, rr=erow: e.activation(rr, rr, AF.Sqrt, bias=EPS_LN, scale=1.0))
        na = p.mark("a", "A")
        p.wait("v", "A", na)
        p.v(lambda e, rr=erow: e.reciprocal(rr, rr))
        nv = p.mark("v", "V")
        p.wait("t", "V", nv)
        for (r, d, ps) in [(mrow, stage[0], psf[4]), (erow, stage[1], psf[5])]:
            p.t(lambda e, r=r, ps=ps: e.matmul(ps[:, 0:BL], L["onesr3"][0:1, 0:128], r, start=True, stop=True))
            nt = p.mark("t", "T")
            p.wait("a", "T", nt)
            na = p.inc("a", lambda e, ps=ps, d=d, bl=bl: e.activation(
                d[:, BL * bl:BL * (bl + 1)], ps[:, 0:BL], AF.Copy), "A")
            p.wait("t", "A", na)
            p.wait("v", "A", na)
    p.mark("a", "A")


def scalars_to_cols(p, L, src_row, ncol, dst_col0):
    psf, onesrf, cols = L["psf"], L["onesrf"], L["cols"]
    nv = p.mark("v", "V")
    ng = p.mark("g", "G")
    p.wait("t", "V", nv)
    p.wait("t", "G", ng)
    b0 = src_row.base_partition()
    p.t(lambda e, b0=b0: e.matmul(psf[6][:, 0:ncol], L["onesr3"][b0:b0 + 1, 0:128], src_row, start=True, stop=True))
    nt = p.mark("t", "T")
    p.wait("a", "T", nt)
    na = p.inc("a", lambda e: e.activation(cols[:, dst_col0:dst_col0 + ncol], psf[6][:, 0:ncol], AF.Copy), "A")
    p.wait("v", "A", na)
    p.wait("g", "A", na)
    p.wait("t", "A", na)


class MaxChain:

    def __init__(self, p, L):
        self.p, self.L = p, L
        self.n = 0

    def add(self, tile_ap):
        p, L, n = self.p, self.L, self.n
        redt = L["redt"]
        p.v(lambda e, t=tile_ap, r=redt[n % 2]: e.tensor_reduce(
            r[:], t, AX.X, ALU.max, apply_absolute_value=True))
        if n == 1:
            p.v(lambda e: e.tensor_copy(L["acc"][:], redt[0][:]))
        elif n > 1:
            p.v(lambda e, r=redt[(n - 1) % 2]: e.tensor_max(L["acc"][:], L["acc"][:], r[:]))
        self.n += 1

    def finish(self):
        p, L, n = self.p, self.L, self.n
        redt, junk = L["redt"], L["junk"]
        p.v(lambda e: e.tensor_scalar_add(junk[:, 0:1], junk[:, 0:1], 1.0))
        if n == 1:
            p.v(lambda e: e.tensor_copy(L["acc"][:], redt[0][:]))
        else:
            p.v(lambda e, r=redt[(n - 1) % 2]: e.tensor_max(L["acc"][:], L["acc"][:], r[:]))
        p.v(lambda e: e.tensor_scalar_add(junk[:, 1:2], junk[:, 1:2], 1.0))


def export_colmax(p, L, dst_rowcol):
    colw, acc = L["colw"], L["acc"]
    p.v(lambda e: e.tensor_copy(colw[:], acc[:, 0:1].broadcast_to([128, 64])))
    n = p.mark("v", "V")
    p.wait("g", "V", n)
    return p.inc("g", lambda e: e.tensor_reduce(dst_rowcol, colw[:, 0:1], AX.C, ALU.max), "G")


def allreduce_row(p, L, row_ap, ncols, op=ALU.max):
    cc_in, cc_out = L["cc_in_s"], L["cc_out_s"]
    p.dma("g", "dg", cc_in[0:1, 0:ncols], row_ap)
    p.wait("g", "dg")
    n = p.inc("g", lambda e: e.collective_compute(
        "AllReduce", op, [[0, 1, 2, 3, 4, 5, 6, 7]],
        [cc_in[0:1, 0:ncols].opt()], [cc_out[0:1, 0:ncols].opt()]), "CC")
    p.wait("g", "CC", n)
    p.dma("g", "dg", row_ap, cc_out[0:1, 0:ncols])
    p.wait("v", "dg")


def emit_ffn(p, L):
    x_t, stage, psf = L["x_t"], L["stage"], L["psf"]
    row, cols, junk = L["row"], L["cols"], L["junk"]
    redt = L["redt"]
    Xp, Wb, Ob = L["Xp"], L["Wb"], L["Ob"]
    wdtile = L["wdtile"]

    ln_stats(p, L)
    p.barrier()
    mc = MaxChain(p, L)
    for i in range(KC):
        p.v(lambda e, i=i: e.tensor_sub(stage[2][:], x_t[i][:], stage[0][:]))
        p.v(lambda e: e.tensor_mul(stage[2][:], stage[2][:], stage[1][:]))
        mc.add(stage[2][:])
    mc.finish()
    export_colmax(p, L, row(1, 0)[:, 0:1])
    p.wait("g", "G")
    allreduce_row(p, L, row(1, 0)[:, 0:1], 1, ALU.max)
    if "cc_d" not in L:
        for (ci, co, key) in [("cc_in_d", "cc_out_d", "cc_d"), ("cc_in_att", "cc_out_att", "cc_att")]:
            p.inc("g", lambda e, ci=ci, co=co: e.collective_compute(
                "AllGather", ALU.bypass, [[0, 1, 2, 3, 4, 5, 6, 7]],
                [L[ci][:].opt()], [L[co][:].opt()]), "CC")
            L[key] = p.cnt["CC"]
    sx = row(1, 0)[:, 1:2]
    wideA = row(1, 0)[:, 64:128]
    p.wait("g", "dg")
    p.g(lambda e: e.tensor_scalar_mul(sx, row(1, 0)[:, 0:1], 1.0 / QMAX))
    p.g(lambda e: e.tensor_scalar_max(sx, sx, 1e-12))
    ng = p.mark("g", "G")
    p.wait("v", "G", ng)
    p.v(lambda e: e.reciprocal(wideA, sx.broadcast_to([1, 64])))
    nv = p.mark("v", "V")
    p.wait("g", "V", nv)
    p.g(lambda e: e.tensor_copy(row(1, 0)[:, 2:3], wideA[:, 0:1]))
    scalars_to_cols(p, L, row(1, 0)[:, 1:3], 2, 0)
    for i in range(KC):
        p.v(lambda e, i=i: e.tensor_sub(stage[2][:], x_t[i][:], stage[0][:]))
        p.v(lambda e: e.tensor_mul(stage[2][:], stage[2][:], stage[1][:]))
        p.v(lambda e: e.tensor_scalar(stage[2][:], stage[2][:], cols[:, 1:2], MAGIC, ALU.mult, ALU.add))
        p.v(lambda e, i=i: e.tensor_single_scalar(Xp(i), stage[2][:], MAGIC, ALU.subtract))
    p.barrier()

    NBANK = 6
    for phase, (wT_src, spill) in enumerate([(L["cc_out_g"], L["gate_d"]), (L["cc_out_u"], L["up_d"])]):
        p.wait("s", "CC", L["cc_g"] if phase == 0 else L["cc_u"])
        for k in range(KC):
            p.dma("s", "di", Wb(k)[:, 0:I3], wT_src[128 * k:128 * (k + 1), :])
        p.wait("t", "di")
        mc = MaxChain(p, L)
        tA = [0] * NBANK
        dg_at = {}
        v_at = {}
        idx = 0
        for ot in range(IT):
            stg = stage[ot % 3]
            for bl in range(NB):
                b = idx % NBANK
                idx += 1
                if tA[b]:
                    p.wait("t", "A", tA[b])
                for k in range(KC):
                    p.t(lambda e, k=k, bl=bl, ot=ot, b=b: e.matmul(
                        psf[b][:, 0:BL], Wb(k)[:, 128 * ot:128 * (ot + 1)],
                        Xp(k)[:, BL * bl:BL * (bl + 1)],
                        start=(k == 0), stop=(k == KC - 1)))
                nt = p.mark("t", "T")
                p.wait("a", "T", nt)
                if ot >= 3 and bl == 0:
                    p.wait("a", "dg", dg_at[ot - 3])
                    p.wait("a", "V", v_at[ot - 3])
                p.a(lambda e, b=b, bl=bl, stg=stg: e.activation(
                    stg[:, BL * bl:BL * (bl + 1)], psf[b][:, 0:BL], AF.Copy, bias=0.0, scale=cols[:, 0:1]))
                tA[b] = p.mark("a", "A")
            p.wait("v", "A")
            mc.add(stg[:])
            v_at[ot] = p.mark("v", "V")
            p.wait("g", "A")
            dg_at[ot] = p.dma("g", "dg", spill[ot], stg[:])
        mc.finish()
        export_colmax(p, L, row(1, 0)[:, 3 + phase:4 + phase])
        p.wait("g", "G")
        if DEBUG_DUMPS:
            p.g(lambda e, ph=phase: e.tensor_copy(L["dbgt"][:, ph:ph + 1], L["acc"][:, 0:1]))
            p.g(lambda e, ph=phase: e.tensor_copy(L["dbgt"][0:1, 2 + ph:3 + ph], row(1, 0)[:, 3 + ph:4 + ph]))
        p.barrier()

    allreduce_row(p, L, row(1, 0)[:, 3:5], 2, ALU.max)
    sg = row(1, 0)[:, 5:6]
    su = row(1, 0)[:, 6:7]
    wideA = row(1, 0)[:, 64:128]
    wideB = row(1, 0)[:, 128:192]
    p.wait("g", "dg")
    p.g(lambda e: e.tensor_scalar_mul(sg, row(1, 0)[:, 3:4], 1.0 / QMAX))
    p.g(lambda e: e.tensor_scalar_max(sg, sg, 1e-12))
    p.g(lambda e: e.tensor_scalar_mul(su, row(1, 0)[:, 4:5], 1.0 / QMAX))
    p.g(lambda e: e.tensor_scalar_max(su, su, 1e-12))
    ng = p.mark("g", "G")
    p.wait("v", "G", ng)
    p.v(lambda e: e.reciprocal(wideA, sg.broadcast_to([1, 64])))
    p.v(lambda e: e.reciprocal(wideB, su.broadcast_to([1, 64])))
    nv = p.mark("v", "V")
    p.wait("g", "V", nv)
    p.g(lambda e: e.tensor_copy(row(1, 0)[:, 7:8], wideA[:, 0:1]))
    p.g(lambda e: e.tensor_copy(row(1, 0)[:, 8:9], wideB[:, 0:1]))
    scalars_to_cols(p, L, row(1, 0)[:, 5:9], 4, 2)
    p.barrier()

    HB = T // 2
    mc = MaxChain(p, L)
    halves = [(ot, hf) for ot in range(IT) for hf in range(2)]
    di_at = {}
    dg_at = {}
    for ot in range(2):
        di_at[ot] = p.dma("s", "di", stage[ot % 2][:], L["gate_d"][ot])
    for ot in range(IT):
        bufi = stage[ot % 2][:]
        tmp = stage[2][:]
        p.wait("v", "di", di_at[ot])
        p.wait("a", "di", di_at[ot])
        p.v(lambda e, bufi=bufi, tmp=tmp: e.tensor_scalar(tmp, bufi, cols[:, 4:5], MAGIC, ALU.mult, ALU.add))
        p.v(lambda e, tmp=tmp: e.tensor_single_scalar(tmp, tmp, MAGIC, ALU.subtract))
        nv = p.mark("v", "V")
        p.wait("a", "V", nv)
        p.a(lambda e, bufi=bufi, tmp=tmp: e.activation(bufi, tmp, AF.Sigmoid, bias=0.0, scale=cols[:, 2:3]))
        p.a(lambda e, bufi=bufi: e.activation(bufi, bufi, AF.Copy, bias=MAGIC, scale=65536.0))
        na = p.mark("a", "A")
        p.wait("v", "A", na)
        p.v(lambda e, bufi=bufi: e.tensor_single_scalar(bufi, bufi, MAGIC, ALU.subtract))
        p.v(lambda e, bufi=bufi, tmp=tmp: e.scalar_tensor_tensor(bufi, bufi, 65535.0, tmp, ALU.min, ALU.mult))
        mc.add(bufi)
        nv = p.mark("v", "V")
        p.wait("g", "V", nv)
        dg_at[ot] = p.dma("g", "dg", L["p_d"][ot], bufi)
        if ot + 2 < IT:
            p.wait("s", "dg", dg_at[ot])
            di_at[ot + 2] = p.dma("s", "di", stage[ot % 2][:], L["gate_d"][ot + 2])
    mc.finish()
    export_colmax(p, L, row(1, 0)[:, 9:10])
    p.wait("g", "G")
    allreduce_row(p, L, row(1, 0)[:, 9:10], 1, ALU.max)
    sg2 = row(1, 0)[:, 10:11]
    c1 = row(1, 0)[:, 11:12]
    wideA = row(1, 0)[:, 64:128]
    p.wait("g", "dg")
    p.g(lambda e: e.tensor_mul(sg2, row(1, 0)[:, 9:10], sg))
    p.g(lambda e: e.tensor_scalar_mul(sg2, sg2, 1.0 / (65536.0 * QMAX)))
    p.g(lambda e: e.tensor_scalar_max(sg2, sg2, 1e-12))
    ng = p.mark("g", "G")
    p.wait("v", "G", ng)
    p.v(lambda e: e.reciprocal(wideA, sg2.broadcast_to([1, 64])))
    nv = p.mark("v", "V")
    p.wait("g", "V", nv)
    p.g(lambda e: e.tensor_mul(c1, wideA[:, 0:1], sg))
    p.g(lambda e: e.tensor_scalar_mul(c1, c1, 1.0 / 65536.0))
    scalars_to_cols(p, L, row(1, 0)[:, 10:12], 2, 6)
    p.barrier()

    mc = MaxChain(p, L)
    di_at = {}
    dg_at = {}
    QH = T // 2
    for j, (ot, hf) in enumerate(halves):
        if j < 1:
            p.dma("s", "di", stage[0][:, QH * (j % 2):QH * (j % 2) + QH], L["up_d"][ot, :, QH * hf:QH * (hf + 1)])
            di_at[j] = p.dma("s", "di", stage[1][:, QH * (j % 2):QH * (j % 2) + QH], L["p_d"][ot, :, QH * hf:QH * (hf + 1)])
    for j, (ot, hf) in enumerate(halves):
        par = j % 2
        bup = stage[0][:, QH * par:QH * par + QH]
        bp = stage[1][:, QH * par:QH * par + QH]
        tmp = stage[2][:, QH * par:QH * par + QH]
        p.wait("a", "di", di_at[j])
        p.wait("v", "di", di_at[j])
        p.a(lambda e, bup=bup, tmp=tmp: e.activation(tmp, bup, AF.Copy, bias=MAGIC, scale=cols[:, 5:6]))
        na = p.mark("a", "A")
        p.wait("v", "A", na)
        p.v(lambda e, tmp=tmp: e.tensor_single_scalar(tmp, tmp, MAGIC, ALU.subtract))
        p.v(lambda e, bp=bp: e.tensor_scalar(bp, bp, cols[:, 7:8], MAGIC, ALU.mult, ALU.add))
        p.v(lambda e, bp=bp: e.tensor_single_scalar(bp, bp, MAGIC, ALU.subtract))
        p.v(lambda e, bup=bup, bp=bp, tmp=tmp: e.tensor_mul(bup, tmp, bp))
        mc.add(bup)
        nv = p.mark("v", "V")
        p.wait("g", "V", nv)
        dg_at[j] = p.dma("g", "dg", L["q_d"][ot, :, QH * hf:QH * (hf + 1)], bup)
        if j + 1 < len(halves):
            ot2, hf2 = halves[j + 1]
            par2 = (j + 1) % 2
            p.wait("s", "dg", dg_at[j]) if False else None
            if j >= 1:
                p.wait("s", "dg", dg_at[j - 1])
            p.dma("s", "di", stage[0][:, QH * par2:QH * par2 + QH], L["up_d"][ot2, :, QH * hf2:QH * (hf2 + 1)])
            di_at[j + 1] = p.dma("s", "di", stage[1][:, QH * par2:QH * par2 + QH], L["p_d"][ot2, :, QH * hf2:QH * (hf2 + 1)])
    mc.finish()
    export_colmax(p, L, row(1, 0)[:, 12:13])
    p.wait("g", "G")
    allreduce_row(p, L, row(1, 0)[:, 12:13], 1, ALU.max)
    c2 = row(1, 0)[:, 13:14]
    so = row(1, 0)[:, 14:15]
    c3 = row(1, 0)[:, 15:16]
    halfso = row(1, 0)[:, 16:17]
    wideA = row(1, 0)[:, 64:128]
    p.wait("g", "dg")
    p.g(lambda e: e.tensor_mul(c2, su, sg2))
    p.g(lambda e: e.tensor_mul(so, row(1, 0)[:, 12:13], c2))
    p.g(lambda e: e.tensor_scalar_mul(so, so, 1.0 / QMAX))
    p.g(lambda e: e.tensor_scalar_max(so, so, 1e-12))
    p.g(lambda e: e.tensor_scalar_mul(halfso, so, 0.5))
    ng = p.mark("g", "G")
    p.wait("v", "G", ng)
    p.v(lambda e: e.reciprocal(wideA, so.broadcast_to([1, 64])))
    nv = p.mark("v", "V")
    p.wait("g", "V", nv)
    p.g(lambda e: e.tensor_mul(c3, wideA[:, 0:1], c2))
    scalars_to_cols(p, L, row(1, 0)[:, 15:17], 2, 8)
    p.barrier()

    di_at = {}
    for ot in range(2):
        di_at[ot] = p.dma("s", "di", stage[ot % 2][:], L["q_d"][ot])
    va = {}
    for ot in range(IT):
        bq = stage[ot % 2][:]
        tmp = stage[2][:]
        p.wait("a", "di", di_at[ot])
        if ot >= 1:
            p.wait("a", "V", va[ot - 1])
        p.a(lambda e, bq=bq, tmp=tmp: e.activation(tmp, bq, AF.Copy, bias=MAGIC, scale=cols[:, 8:9]))
        na = p.mark("a", "A")
        if ot + 2 < IT:
            p.wait("s", "A", na)
            di_at[ot + 2] = p.dma("s", "di", stage[ot % 2][:], L["q_d"][ot + 2])
        p.wait("v", "A", na)
        p.v(lambda e, ot=ot, tmp=tmp: e.tensor_single_scalar(Ob(ot)[:], tmp, MAGIC, ALU.subtract))
        va[ot] = p.mark("v", "V")
    nv = p.mark("v", "V")
    p.wait("t", "V", nv)
    p.wait("s", "V", nv)

    tT = {}
    tA4 = [0, 0, 0, 0]
    v_at = {}
    groups = [(h, i) for h in range(KC) for i in range(IT)]
    p.wait("s", "CC", L["cc_d"])
    wd_src = lambda i, h: L["cc_out_d"][3072 * (i // 3) + 1024 * (i % 3) + 128 * h:
                                        3072 * (i // 3) + 1024 * (i % 3) + 128 * h + 128, :]
    if not F4_SERIAL_W:
        for g in range(2):
            h2, i2 = groups[g]
            p.dma("s", "di", wdtile[g % 3][:], wd_src(i2, h2))
    for g, (h, i) in enumerate(groups):
        stg = stage[h % 3]
        if True:
            wt = wdtile[g % 3]
            if F4_SERIAL_W:
                if g >= 1:
                    p.wait("s", "T", tT[groups[g - 1]])
                p.dma("s", "di", wt[:], wd_src(i, h))
            p.wait("t", "di")
            for bl in range(NB):
                if i == 0 and tA4[bl]:
                    p.wait("t", "A", tA4[bl])
                p.t(lambda e, i=i, bl=bl, wt=wt: e.matmul(
                    psf[bl][:, 0:BL], wt[:], Ob(i)[:, BL * bl:BL * (bl + 1)],
                    start=(i == 0), stop=(i == IT - 1)))
            nt = p.mark("t", "T")
            tT[(h, i)] = nt
            if g + 2 < len(groups) and not F4_SERIAL_W:
                h2, i2 = groups[g + 2]
                if g >= 1:
                    p.wait("s", "T", tT[groups[g - 1]])
                p.dma("s", "di", wdtile[(g + 2) % 3][:], wd_src(i2, h2))
        if i == IT - 1:
            p.wait("a", "T", tT[(h, IT - 1)])
            if h >= 3:
                p.wait("a", "V", v_at[h - 3])
            for bl in range(NB):
                p.a(lambda e, bl=bl, stg=stg: e.activation(stg[:, BL * bl:BL * (bl + 1)],
                                                           psf[bl][:, 0:BL], AF.Copy, bias=0.0, scale=cols[:, 9:10]))
                tA4[bl] = p.mark("a", "A")
            p.wait("v", "A", tA4[3])
            p.v(lambda e, h=h, stg=stg: e.tensor_add(x_t[h][:], x_t[h][:], stg[:]))
            v_at[h] = p.mark("v", "V")
            if F4_BARRIER:
                p.barrier()
    p.wait("t", "V")
    p.wait("a", "V")
    p.barrier()


def A_views(L):
    a = L["arena"]
    V = {}
    V["wqa"] = lambda k: a[:, 16384 + 1024 * k: 16384 + 1024 * (k + 1)]
    V["wka"] = lambda k: a[:, 24576 + 1024 * k: 24576 + 1024 * (k + 1)]
    V["wva"] = lambda k: a[:, 32768 + 1024 * k: 32768 + 1024 * (k + 1)]
    V["woa"] = lambda k: a[:, 16384 + 1024 * k: 16384 + 1024 * (k + 1)]
    V["qp"] = a[:, 40960:43008]
    V["kp"] = a[:, 43008:45056]
    V["vp"] = a[:, 45056:47136]
    V["ekb"] = a[:, 47136:47392]
    V["eqs"] = [a[:, 47392 + 512 * i: 47392 + 512 * (i + 1)] for i in range(2)]
    return V


def _attn_pair(p, L, V, pp):
    stage, psf, psb = L["stage"], L["psf"], L["psb"]
    junk, redt = L["junk"], L["redt"]
    Xp = L["Xp"]
    pjT, onesr3, onescf, onescb = L["pjT"], L["onesr3"], L["onescf"], L["onescb"]
    negselt, identb = L["negselt"], L["identb"]
    stab_tm, stabb, stabT, kacc = L["stab_tm"], L["stabb"], L["stabT"], L["kacc"]
    ctxe, svc = L["ctxe"], L["svc"]
    qp, kp, vp, ekb, eqs = V["qp"], V["kp"], V["vp"], V["ekb"], V["eqs"]
    NTT_ = NTT
    for (wsel, dst) in [("wqa", qp), ("wka", kp)]:
        pa = [psf[b] for b in range(4)]
        for k in range(KC):
            for bl in range(NB):
                p.t(lambda e, k=k, bl=bl, wsel=wsel, pp=pp, pa=pa: e.matmul(
                    pa[bl][:, 0:BL], V[wsel](k)[:, 128 * pp:128 * (pp + 1)],
                    Xp(k)[:, BL * bl:BL * (bl + 1)],
                    start=(k == 0), stop=(k == KC - 1)))
        nt = p.mark("t", "T")
        p.wait("a", "T", nt)
        for bl in range(NB):
            p.a(lambda e, bl=bl, dst=dst, pa=pa: e.activation(
                dst[:, BL * bl:BL * (bl + 1)], pa[bl][:, 0:BL], AF.Copy))
        na = p.mark("a", "A")
        p.wait("t", "A", na)
    for tt in range(NTT_):
        for k in range(KC):
            p.t(lambda e, tt=tt, k=k, pp=pp: e.matmul(
                psf[4][:, 0:128], Xp(k)[:, 128 * tt:128 * (tt + 1)],
                V["wva"](k)[:, 128 * pp:128 * (pp + 1)],
                start=(k == 0), stop=(k == KC - 1)))
        nt = p.mark("t", "T")
        p.wait("a", "T", nt)
        p.a(lambda e, tt=tt: e.activation(vp[:, 130 * tt:130 * tt + 64], psf[4][:, 0:64], AF.Copy))
        p.a(lambda e, tt=tt: e.activation(vp[:, 130 * tt + 65:130 * tt + 129], psf[4][:, 64:128], AF.Copy))
        na = p.mark("a", "A")
        p.wait("t", "A", na)
    p.wait("g", "A")
    ng = p.inc("g", lambda e: e.memset(vp[:].rearrange("p (t c) -> p t c", c=130)[:, :, 64:65], 1.0), "G")
    ng = p.inc("g", lambda e: e.memset(vp[:].rearrange("p (t c) -> p t c", c=130)[:, :, 129:130], 1.0), "G")
    p.wait("t", "G", ng)
    p.wait("v", "G", ng)
    for hh in range(2):
        _attn_head(p, L, V, pp, hh)


def _attn_head(p, L, V, pp, hh):
    stage, psf, psb = L["stage"], L["psf"], L["psb"]
    junk, redt = L["junk"], L["redt"]
    pjT, onesr3, onescf, onescb = L["pjT"], L["onesr3"], L["onescf"], L["onescb"]
    negselt, identb = L["negselt"], L["identb"]
    stab_tm, stabb, stabT, kacc = L["stab_tm"], L["stabb"], L["stabT"], L["kacc"]
    ctxe, svc = L["ctxe"], L["svc"]
    qp, kp, vp, ekb, eqs = V["qp"], V["kp"], V["vp"], V["ekb"], V["eqs"]
    h = 2 * pp + hh
    hb = 64 * hh
    qh = qp[hb:hb + 64, :]
    kh = kp[hb:hb + 64, :]
    pj = pjT[hb:hb + 64, :]
    for (src_, prow) in [(qh, 0), (kh, 32)]:
        for bl in range(NB):
            na = p.inc("a", lambda e, src_=src_, bl=bl: e.activation(
                stage[2][0:64, 0:BL], src_[:, BL * bl:BL * (bl + 1)], AF.Square), "A")
            p.wait("t", "A", na)
            p.t(lambda e: e.matmul(psf[6][0:1, :], onescf[0:64, :], stage[2][0:64, 0:BL],
                                   start=True, stop=True))
            nt = p.mark("t", "T")
            p.wait("a", "T", nt)
            na = p.inc("a", lambda e, prow=prow, bl=bl: e.activation(
                L["rowh0"][prow:prow + 1, BL * bl:BL * (bl + 1)], psf[6][0:1, :],
                AF.Copy, bias=0.0, scale=-0.5), "A")
            p.wait("t", "A", na)
    p.wait("v", "A")

    if h == 0 and L.get("dump", False):
        p.wait("g", "A")
        p.dma("g", "dg", L["dbg_nd"][0:1, 0:T], L["rowh0"][0:1, :])
        p.dma("g", "dg", L["dbg_nd"][0:1, T:2 * T], L["rowh0"][32:33, :])
        p.dma("g", "dg", L["dbg_qp"], qp[:])
        p.wait("v", "dg")
    for tt in range(NTT):
        p.t(lambda e, tt=tt: e.matmul(psf[4][:, 0:M], qh[:, 128 * tt:128 * (tt + 1)], pj,
                                      start=True, stop=True))
        nt = p.mark("t", "T")
        p.wait("v", "T", nt)
        p.v(lambda e, tt=tt: e.tensor_reduce(stab_tm[:, tt:tt + 1], psf[4][:, 0:M], AX.X, ALU.max))
        nv = p.mark("v", "V")
        p.wait("t", "V", nv)
    p.v(lambda e: e.tensor_scalar_add(junk[:, 0:1], junk[:, 0:1], 1.0))
    p.v(lambda e: e.tensor_copy(stabb[:], stab_tm[:]))
    nv = p.mark("v", "V")
    p.wait("t", "V", nv)
    p.t(lambda e: e.transpose(psb[0:16, 0:128], stabb[:], identb[:]))
    nt = p.mark("t", "T")
    p.wait("a", "T", nt)
    na = p.inc("a", lambda e: e.activation(stabT[:], psb[0:16, 0:128], AF.Copy), "A")
    na = p.inc("a", lambda e: e.activation(stage[2][0:16, 0:128], psb[0:16, 0:128],
                                           AF.Exp, bias=math.log(KEPS), scale=1.0), "A")
    p.wait("g", "A", na)
    p.dma("g", "dg", L["eps_d"][h:h + 1, :].rearrange("x (a b) -> (x a) b", a=16),
          stage[2][0:16, 0:128])
    p.wait("t", "A", na)
    p.wait("a", "dg")

    if h == 0 and L.get("dump", False):
        p.dma("g", "dg", L["dbg_stab"], stabT[:])
        p.wait("v", "dg")
    for mc in range(2):
        for bl in range(NB):
            ps = psf[mc % 2]
            p.t(lambda e, mc=mc, bl=bl, ps=ps: e.matmul(
                ps[:, 0:BL], pj[:, 128 * mc:128 * (mc + 1)],
                qh[:, BL * bl:BL * (bl + 1)], start=True, stop=False))
            p.t(lambda e, bl=bl, ps=ps: e.matmul(
                ps[:, 0:BL], onesr3[0:1, 0:128],
                L["rowh0"][0:1, BL * bl:BL * (bl + 1)], start=False, stop=False))
            for sb_ in range(4):
                tt = 4 * bl + sb_
                p.t(lambda e, tt=tt, sb_=sb_, ps=ps: e.matmul(
                    ps[:, 128 * sb_:128 * (sb_ + 1)],
                    negselt[:, 128 * tt:128 * (tt + 1)], stabT[:],
                    start=False, stop=(True)))
            nt = p.mark("t", "T")
            p.wait("a", "T", nt)
            na = p.inc("a", lambda e, ps=ps, bl=bl: e.activation(
                eqs[bl % 2][:], ps[:, 0:BL], AF.Exp), "A")
            p.wait("g", "A", na)
            p.dma("g", "dg", L["eq_d"][h, mc, :, BL * bl:BL * (bl + 1)], eqs[bl % 2][:])
            p.wait("t", "A", na)
            if bl >= 1:
                p.wait("a", "dg", p.cnt["dg"] - 16)
    p.wait("a", "dg")

    for tt in range(NTT):
        p.t(lambda e, tt=tt: e.matmul(psf[5][:, 0:M], kh[:, 128 * tt:128 * (tt + 1)], pj,
                                      start=True, stop=False))
        nt = p.mark("t", "T")
        p.wait("v", "T", nt)
        p.v(lambda e, tt=tt: e.tensor_reduce(redt[tt % 2][:], psf[5][:, 0:M], AX.X, ALU.max))
        if tt > 0:
            p.v(lambda e, tt=tt: e.tensor_max(kacc[:], kacc[:], redt[(tt - 1) % 2][:]))
        nv = p.mark("v", "V")
        p.wait("t", "V", nv)
        p.t(lambda e, tt=tt: e.matmul(psf[5][:, 0:M],
                                      L["rowh0"][32:33, 128 * tt:128 * (tt + 1)],
                                      onesr3[32:33, 0:M], start=False, stop=True))
        nt = p.mark("t", "T")
        p.wait("a", "T", nt)
        na = p.inc("a", lambda e: e.activation(ekb[:, 0:M], psf[5][:, 0:M], AF.Exp), "A")
        p.wait("t", "A", na)
        if h == 0 and tt == 0 and L.get("dump", False):
            p.wait("g", "A", na)
            p.dma("g", "dg", L["dbg_eku"], ekb[:, 0:M])
            p.wait("t", "dg")
        p.t(lambda e, tt=tt, hh=hh: e.matmul(
            psf[6][0:65, 0:M], vp[:, 130 * tt + 65 * hh:130 * tt + 65 * hh + 65],
            ekb[:, 0:M], start=(tt == 0), stop=(tt == NTT - 1)))
        nt = p.mark("t", "T")
        p.wait("a", "T", nt)
    p.v(lambda e: e.tensor_scalar_add(junk[:, 0:1], junk[:, 0:1], 1.0))
    p.v(lambda e: e.tensor_max(kacc[:], kacc[:], redt[(NTT - 1) % 2][:]))
    for tt in range(NTT):
        p.t(lambda e, tt=tt, hh=hh: e.matmul(
            psf[2][0:65, 0:1], vp[:, 130 * tt + 65 * hh:130 * tt + 65 * hh + 65],
            onescb[:], start=(tt == 0), stop=(tt == NTT - 1)))
    nt = p.mark("t", "T")
    p.wait("a", "T", nt)
    na = p.inc("a", lambda e, h=h: e.activation(svc[:, h:h + 1], psf[2][0:65, 0:1], AF.Copy), "A")
    na = p.inc("a", lambda e: e.activation(ctxe[:], psf[6][0:65, 0:M], AF.Copy), "A")
    p.wait("g", "A", na)
    p.dma("g", "dg", L["cc_in_ctx"][65 * h:65 * (h + 1), :], ctxe[:])
    p.wait("t", "A", na)
    p.wait("a", "dg")


def emit_attn(p, L):
    x_t, stage, psf, psb = L["x_t"], L["stage"], L["psf"], L["psb"]
    row, cols, junk = L["row"], L["cols"], L["junk"]
    redt = L["redt"]
    Xp = L["Xp"]
    pjT, onesr3, onescf, onescb = L["pjT"], L["onesr3"], L["onescf"], L["onescb"]
    negselt, identb = L["negselt"], L["identb"]
    stab_tm, stabb, stabT, kacc = L["stab_tm"], L["stabb"], L["stabT"], L["kacc"]
    ctxe, ctxbf, ctxb, svc = L["ctxe"], L["ctxbf"], L["ctxb"], L["svc"]
    V = A_views(L)
    qp, kp, vp, ekb, eqs = V["qp"], V["kp"], V["vp"], V["ekb"], V["eqs"]

    ln_stats(p, L)
    p.barrier()
    for i in range(KC):
        p.v(lambda e, i=i: e.tensor_sub(stage[2][:], x_t[i][:], stage[0][:]))
        p.v(lambda e, i=i: e.tensor_mul(Xp(i), stage[2][:], stage[1][:]))
    p.barrier()
    if L.get("dump", False):
        p.wait("g", "V")
        p.dma("g", "dg", L["dbg_lnb"], Xp(0))
        p.wait("s", "dg")
    for k in range(KC):
        att = L["cc_out_att"]
        if k == 0:
            p.wait("s", "CC", L["cc_att"])
        p.dma("s", "di", V["wqa"](k), att[512 * k:512 * k + 128, :])
        p.dma("s", "di", V["wka"](k), att[512 * k + 128:512 * k + 256, :])
        p.dma("s", "di", V["wva"](k), att[512 * k + 256:512 * k + 384, :])
    p.wait("t", "di")
    p.a(lambda e: e.activation(kacc[:], L["onescf"][:, 0:1], AF.Copy, bias=-1e30, scale=1.0))
    p.barrier()

    for pp in range(KC):
        _attn_pair(p, L, V, pp)
    p.barrier()

    p.v(lambda e: e.tensor_copy(L["colw"][:], kacc[:, 0:1].broadcast_to([128, 64])))
    nv = p.mark("v", "V")
    p.wait("g", "V", nv)
    p.inc("g", lambda e: e.tensor_reduce(row(1, 0)[:, 20:21], L["colw"][:, 0:1], AX.C, ALU.max), "G")
    p.dma("g", "dg", L["cc_in_s"][0:1, 0:1], row(1, 0)[:, 20:21])
    p.wait("g", "dg")
    n = p.inc("g", lambda e: e.collective_compute(
        "AllReduce", ALU.max, [[0, 1, 2, 3, 4, 5, 6, 7]],
        [L["cc_in_s"][0:1, 0:1].opt()], [L["cc_out_s"][0:1, 0:1].opt()]), "CC")
    p.dma("g", "dg", L["cc_in_sv"], svc[:, 0:HEADS])
    p.wait("g", "dg")
    n = p.inc("g", lambda e: e.collective_compute(
        "AllReduce", ALU.add, [[0, 1], [2, 3], [4, 5], [6, 7]],
        [L["cc_in_sv"].opt()], [L["cc_out_sv"].opt()]), "CC")
    n = p.inc("g", lambda e: e.collective_compute(
        "AllReduce", ALU.add, [[0, 1], [2, 3], [4, 5], [6, 7]],
        [L["cc_in_ctx"].opt()], [L["cc_out_ctx"].opt()]), "CC")
    p.wait("g", "CC", n)
    p.dma("g", "dg", row(1, 0)[:, 21:22], L["cc_out_s"][0:1, 0:1])
    p.dma("g", "dg", svc[:, 0:HEADS], L["cc_out_sv"])
    p.wait("a", "dg")
    p.wait("g", "dg")
    na = p.inc("a", lambda e: e.activation(row(1, 0)[:, 22:23], row(1, 0)[:, 21:22],
                                           AF.Exp, bias=math.log(KEPS), scale=1.0), "A")
    p.wait("t", "A", na)
    p.t(lambda e: e.matmul(psf[3][0:65, 0:1], onesr3[0:1, 0:65], row(1, 0)[:, 22:23],
                           start=True, stop=True))
    nt = p.mark("t", "T")
    p.wait("a", "T", nt)
    na = p.inc("a", lambda e: e.activation(cols[:, 16:17][0:65, :], psf[3][0:65, 0:1], AF.Copy), "A")
    p.wait("v", "A", na)
    p.v(lambda e: e.tensor_scalar(svc[:, 0:HEADS], svc[:, 0:HEADS], cols[0:65, 16:17], None, ALU.mult))
    nv = p.mark("v", "V")
    p.barrier()

    if L.get("dump", False):
        p.dma("g", "dg", L["dbg_ctx"], L["cc_out_ctx"][0:65, :])
        p.dma("g", "dg", L["dbg_sv"], L["svc"][:, 0:HEADS])
        p.wait("g", "dg")
    for k in range(KC):
        p.dma("s", "di", V["woa"](k), L["cc_out_att"][512 * k + 384:512 * k + 512, :])

    for h in range(HEADS):
        p.dma("s", "di", ctxe[:], L["cc_out_ctx"][65 * h:65 * (h + 1), :])
        p.wait("v", "di")
        p.v(lambda e, h=h: e.scalar_tensor_tensor(
            ctxbf[:], svc[:, h:h + 1].broadcast_to([65, M]), 1.0, ctxe[:], ALU.mult, ALU.add))
        nv = p.mark("v", "V")
        p.wait("t", "V", nv)
        for mc in range(2):
            p.t(lambda e, mc=mc: e.transpose(psb[:, 0:65], ctxbf[0:65, 128 * mc:128 * (mc + 1)], identb[0:65, 0:65]))
            nt = p.mark("t", "T")
            p.wait("a", "T", nt)
            na = p.inc("a", lambda e, mc=mc: e.activation(ctxb[:, 65 * mc:65 * (mc + 1)], psb[:, 0:65], AF.Copy), "A")
            p.wait("t", "A", na)
        for mc in range(2):
            p.t(lambda e, mc=mc: e.matmul(psf[2][0:1, 0:65], onescb[:], ctxb[:, 65 * mc:65 * (mc + 1)],
                                          start=(mc == 0), stop=(mc == 1)))
        nt = p.mark("t", "T")
        p.wait("a", "T", nt)
        na = p.inc("a", lambda e: e.activation(row(1, 0)[:, 192:257], psf[2][0:1, 0:65], AF.Copy), "A")
        p.dma("s", "di", L["rowh0"][0:1, :], L["eps_d"][h:h + 1, :])
        p.dma("s", "di", qp[:], L["eq_d"][h, 0])
        p.dma("s", "di", kp[:], L["eq_d"][h, 1])
        p.wait("t", "di")
        p.wait("t", "A")
        for bl in range(NB):
            eq_ = [qp, kp]
            p.t(lambda e, bl=bl: e.matmul(psf[0][0:65, 0:BL], ctxb[:, 0:65],
                                          qp[:, BL * bl:BL * (bl + 1)], start=True, stop=False))
            p.t(lambda e, bl=bl: e.matmul(psf[0][0:65, 0:BL], ctxb[:, 65:130],
                                          kp[:, BL * bl:BL * (bl + 1)], start=False, stop=False))
            p.t(lambda e, bl=bl: e.matmul(psf[0][0:65, 0:BL], row(1, 0)[:, 192:257],
                                          L["rowh0"][0:1, BL * bl:BL * (bl + 1)],
                                          start=False, stop=True))
            nt = p.mark("t", "T")
            p.wait("v", "T", nt)
            p.v(lambda e, bl=bl: e.reciprocal(L["rowh0"][32:33, BL * bl:BL * (bl + 1)], psf[0][64:65, 0:BL]))
            nv = p.mark("v", "V")
            p.wait("t", "V", nv)
            p.t(lambda e, bl=bl: e.matmul(psf[1][0:64, 0:BL], onesr3[32:33, 0:64],
                                          L["rowh0"][32:33, BL * bl:BL * (bl + 1)],
                                          start=True, stop=True))
            nt = p.mark("t", "T")
            p.wait("a", "T", nt)
            na = p.inc("a", lambda e: e.activation(stage[2][0:64, 0:BL], psf[0][0:64, 0:BL], AF.Copy), "A")
            p.wait("v", "A", na)
            dstc = Xp(h // 2)[64 * (h % 2):64 * (h % 2) + 64, :]
            p.v(lambda e, bl=bl, dstc=dstc: e.tensor_mul(
                dstc[:, BL * bl:BL * (bl + 1)], stage[2][0:64, 0:BL], psf[1][0:64, 0:BL]))
            nv = p.mark("v", "V")
            p.wait("t", "V", nv)
    if L.get("dump", False):
        p.barrier()
        p.dma("s", "di", L["dbg_att"], L["Xp"](0)[0:64, :])
        p.dma("s", "di", L["dbg_inv"], L["rowh0"][32:33, :])
        p.wait("s", "di")
    p.barrier()

    for z in range(KC):
        p.v(lambda e, z=z: e.tensor_scalar(x_t[z][:], x_t[z][:], L["bocol"][:, z:z + 1], None, ALU.add))
    nv = p.mark("v", "V")
    p.wait("t", "V", nv)
    p.wait("s", "di")
    for z in range(KC):
        pa = [psf[b] for b in range(4)]
        for k in range(KC):
            for bl in range(NB):
                p.t(lambda e, z=z, k=k, bl=bl: e.matmul(
                    pa[bl][:, 0:BL], V["woa"](k)[:, 128 * z:128 * (z + 1)],
                    Xp(k)[:, BL * bl:BL * (bl + 1)], start=(k == 0), stop=(k == KC - 1)))
        nt = p.mark("t", "T")
        p.wait("a", "T", nt)
        for bl in range(NB):
            p.a(lambda e, bl=bl: e.activation(stage[2][:, BL * bl:BL * (bl + 1)], pa[bl][:, 0:BL], AF.Copy))
        na = p.mark("a", "A")
        p.wait("v", "A", na)
        p.v(lambda e, z=z: e.tensor_add(x_t[z][:], x_t[z][:], stage[2][:]))
        nv = p.mark("v", "V")
        p.wait("t", "V", nv)
        p.wait("a", "V", nv)
    p.barrier()


_nc_cache = {}


def kernel(L0_attn_x, L0_attn_wq, L0_attn_wk, L0_attn_wv, L0_attn_wo, L0_attn_bo,
           L0_attn_proj, L0_ffn_w_gate, L0_ffn_w_up, L0_ffn_w_down):
    x = np.ascontiguousarray(np.asarray(L0_attn_x, np.float32))
    shards = x.reshape(NCORES, T, D)
    wq = np.ascontiguousarray(L0_attn_wq, np.float32)
    wk = np.ascontiguousarray(L0_attn_wk, np.float32)
    wv = np.ascontiguousarray(L0_attn_wv, np.float32)
    wo = np.ascontiguousarray(L0_attn_wo, np.float32)
    bo = np.ascontiguousarray(np.asarray(L0_attn_bo, np.float32).reshape(D, 1))
    projT = np.ascontiguousarray(np.asarray(L0_attn_proj, np.float32).T)
    wg = np.ascontiguousarray(L0_ffn_w_gate, np.float32)
    wu = np.ascontiguousarray(L0_ffn_w_up, np.float32)
    wd = np.ascontiguousarray(L0_ffn_w_down, np.float32)
    in_maps = []
    for c in range(NCORES):
        in_maps.append({
            "x_in": np.ascontiguousarray(shards[c].T),
            "bo_in": bo, "pjT_in": projT,
            "att4_in": np.ascontiguousarray(np.concatenate(
                [w[128 * c:128 * (c + 1), :] for w in (wq, wk, wv, wo)], axis=0)),
            "wgc_in": np.ascontiguousarray(wg[:, 128 * c:128 * (c + 1)]),
            "wuc_in": np.ascontiguousarray(wu[:, 128 * c:128 * (c + 1)]),
            "wdc_in": np.ascontiguousarray(wd[:, 384 * c:384 * (c + 1)]),
        })
    if "nc" not in _nc_cache:
        _nc_cache["nc"] = build_nc()
    nc = _nc_cache["nc"]
    res = run_bass_kernel_spmd(nc, in_maps, core_ids=list(range(NCORES)))
    _nc_cache["res0"] = res.results[0]
    _nc_cache["res_all"] = res.results
    outs = [np.asarray(res.results[c]["out"]).T for c in range(NCORES)]
    return np.concatenate(outs, axis=0).reshape(4, 4096, D).astype(np.float32)
